# version 03
# baseline (speedup 1.0000x reference)
import numpy as np
import ml_dtypes
import concourse.bass as bass
import concourse.mybir as mybir
from concourse.bass_utils import run_bass_kernel_spmd
from concourse.alu_op_type import AluOpType as OP

F32 = mybir.dt.float32
BF16 = mybir.dt.bfloat16
AF = mybir.ActivationFunctionType
AX = mybir.AxisListType

NCORES = 8
D = 2048
S = 4096
BATCH = 2
NG, HPG, DH = 8, 4, 64
NE_REAL = 16
I_EXP, I_SH = 1408, 5632
NE = 20
OWN = 1024
NTB = 8
NEG = -1e30
BOUND_MARGIN = 1.05


def _mblocks_2816(W):
    A = W.reshape(22, 128, 16, 128).transpose(0, 3, 2, 1).reshape(22, 128, 2048)
    order = []
    for j in range(11):
        order += [j, 11 + j]
    return np.ascontiguousarray(A[order])

def _nblocks_1408(W2):
    return np.ascontiguousarray(W2.reshape(4, 512, 11, 128).transpose(0, 3, 2, 1).reshape(4, 128, 5632))

def _shared_layouts(inp):
    out = {}
    w_in = np.asarray(inp["L0_attn_w_in"], np.float32)
    permk = []
    for m in range(4):
        for half in (0, 1):
            for g in (2 * m, 2 * m + 1):
                base = 2048 + g * 64 + 32 * half
                permk += list(range(base, base + 32))
    WK = w_in[permk]
    WV = w_in[2560:3072]
    out["wkv"] = np.ascontiguousarray(np.concatenate(
        [WK.reshape(512, 16, 128).transpose(1, 2, 0),
         WV.reshape(512, 16, 128).transpose(1, 2, 0)], axis=2))
    permq = []
    for m in range(16):
        for half in (0, 1):
            for h in (2 * m, 2 * m + 1):
                base = h * 64 + 32 * half
                permq += list(range(base, base + 32))
    WQ = w_in[permq]
    out["wq"] = np.ascontiguousarray(WQ.reshape(16, 128, 16, 128).transpose(0, 3, 2, 1).reshape(16, 128, 2048))
    w_out = np.asarray(inp["L0_attn_w_out"], np.float32)
    out["wout"] = np.ascontiguousarray(w_out.reshape(4, 512, 16, 128).transpose(0, 2, 3, 1))
    out["rms16"] = np.ascontiguousarray(np.asarray(inp["L0_attn_rms_w"], np.float32).reshape(16, 128).T)
    gate = np.asarray(inp["L0_ffn_gate_w"], np.float32)
    g16 = gate.T.reshape(16, 128, 16)
    sg1 = np.asarray(inp["L0_ffn_sgate_w"], np.float32)[0].reshape(16, 128, 1)
    out["wgate"] = np.ascontiguousarray(np.concatenate([g16, sg1], axis=2).transpose(1, 0, 2).reshape(128, 272))
    sgu = np.asarray(inp["L0_ffn_shared_gate_up_w"], np.float32)
    sdn = np.asarray(inp["L0_ffn_shared_down_w"], np.float32)
    w13 = np.asarray(inp["L0_ffn_w13"], np.float32)
    w2 = np.asarray(inp["L0_ffn_w2"], np.float32)
    e13, e2 = [], []
    for j in range(4):
        W13p = np.concatenate([sgu[I_EXP * j:I_EXP * (j + 1)],
                               sgu[I_SH + I_EXP * j:I_SH + I_EXP * (j + 1)]], axis=0)
        e13.append(_mblocks_2816(W13p))
        e2.append(_nblocks_1408(sdn[:, I_EXP * j:I_EXP * (j + 1)]))
    for e in range(NE_REAL):
        e13.append(_mblocks_2816(w13[e]))
        e2.append(_nblocks_1408(w2[e]))
    out["w13"] = np.ascontiguousarray(np.stack(e13))
    out["w2"] = np.ascontiguousarray(np.stack(e2))
    return out

def _rope_tab(pos, scale):
    inv = 1.0 / (10000.0 ** (np.arange(0, DH, 2) / DH))
    ang = pos[None, :].astype(np.float64) * inv[:, None]
    c = (np.cos(ang) * scale).astype(np.float32)
    s = (np.sin(ang) * scale).astype(np.float32)
    return np.ascontiguousarray(np.concatenate([c, c, s, s], axis=0))

def _per_core(inp, core):
    b, i = core // 4, core % 4
    x = np.asarray(inp["L0_attn_x"], np.float32)[b]
    others = list(range(0, i)) + list(range(i + 1, 7 - i))
    xrows = np.zeros((S, D), np.float32)
    pos = np.zeros((S,), np.int64)
    for s_, ch in enumerate(others):
        xrows[s_ * 512:(s_ + 1) * 512] = x[ch * 512:(ch + 1) * 512]
        pos[s_ * 512:(s_ + 1) * 512] = np.arange(ch * 512, (ch + 1) * 512)
    xrows[3072:3584] = x[i * 512:(i + 1) * 512]
    pos[3072:3584] = np.arange(i * 512, (i + 1) * 512)
    xrows[3584:4096] = x[(7 - i) * 512:(8 - i) * 512]
    pos[3584:4096] = np.arange((7 - i) * 512, (8 - i) * 512)
    biascol = np.zeros((128, 49), np.float32)
    nlo, nhi = 4 * i, 24 - 4 * i
    for kt in range(24):
        biascol[:, kt] = 0.0 if kt < nlo else NEG
        biascol[:, 24 + kt] = 0.0 if kt < nhi else NEG
    return {"xrows": xrows, "ktab": _rope_tab(pos, 1.0),
            "qtab": _rope_tab(pos[3072:], 0.125), "biascol": biascol}


ENGS = ("tensor", "scalar", "vector", "gpsimd", "sync")
QSEMS = ("qi", "qkv", "qt0", "qt1", "qx0", "qx1", "qq", "qwo0", "qwo1",
         "qxq0", "qxq1", "q13a", "q13b", "q2a", "q2b", "qo")
ALLSEMS = ENGS + QSEMS

class Bld:
    def __init__(self, me, eng, sems):
        self.me, self.eng, self.sems = me, eng, sems
        self.ct = {k: 0 for k in ALLSEMS}
        self.last_wait = {}

    def _wait(self, sem_name, val):
        if val <= 0 or self.last_wait.get(sem_name, -1) >= val:
            return
        self.last_wait[sem_name] = val
        self.eng.wait_ge(self.sems[sem_name], val)

    def op(self, eng_name, fn, waits=(), inc=True):
        if self.me == eng_name:
            for (s, v) in waits:
                self._wait(s, v)
            instr = fn(self.eng)
            if inc:
                instr.then_inc(self.sems[eng_name], 1)
        if inc:
            self.ct[eng_name] += 1
        return self.ct[eng_name]

    def dma(self, q_name, sem, out, in_, waits=()):
        if self.me == q_name:
            for (s, v) in waits:
                self._wait(s, v)
            self.eng.dma_start(out=out, in_=in_).then_inc(self.sems[sem], 16)
        self.ct[sem] += 16
        return self.ct[sem]

    def wait(self, eng_name, sem_name, val):
        if self.me == eng_name:
            self._wait(sem_name, val)

    def fence(self):
        return dict(self.ct)


def build_walk(T, C, b: Bld, phases=6):
    def dump_and_stop():
        f = b.fence()
        last = 0
        for name, src_ap in C.get("dumps", []):
            last = b.dma("gpsimd", "qo", C["dbg_aps"][name], src_ap,
                         waits=[(e_, f[e_]) for e_ in ALLSEMS if e_ != "qo"])
        b.wait("gpsimd", "qo", last)

    pb = T["pb"]
    def bank(ix):
        return pb[:, 512 * ix:512 * (ix + 1)]

    b.dma("gpsimd", "qi", T["biascol"][:], C["in"]["biascol"])
    b.dma("gpsimd", "qi", T["rms16land"][:], C["in"]["rms16"])
    b.dma("gpsimd", "qi", T["qtab"][:], C["in"]["qtab"])
    b.dma("gpsimd", "qi", T["wgateland"][:], C["in"]["wgate"])
    qi = b.dma("gpsimd", "qi", T["ident"][:], C["ident_dram"].ap())
    g_bias = g_rms = g_qtab = g_gate = g_sg = g_id = ("qi", qi)
    v_ones = b.op("vector", lambda e: e.memset(T["vst_ones"], 1.0))
    v_cst = b.op("vector", lambda e: e.memset(T["cst"][:, 0:1], 1e-11))
    v_cst = b.op("vector", lambda e: e.memset(T["cst"][:, 1:2], 1e-5))
    v_o164 = b.op("vector", lambda e: e.memset(T["ones164"][:], 1.0))
    v_gate = b.op("vector", lambda e: e.tensor_copy(out=T["wgateb"][:], in_=T["wgateland"][:]),
                  waits=[g_gate])
    rms_v = b.op("vector", lambda e: e.tensor_copy(out=T["rms16"][:], in_=T["rms16land"][:]),
                 waits=[g_rms])
    v_wkv = 0
    for k in range(16):
        q_kv = b.dma("sync", "qkv", T["kvland"][:], C["in"]["wkv"][k],
                     waits=([("vector", v_wkv)] if k else []))
        v_wkv = b.op("vector", lambda e, k=k: e.tensor_copy(
            out=T["wkvb"][:, 1024 * k:1024 * (k + 1)], in_=T["kvland"][:]),
            waits=[("qkv", q_kv)])

    if phases <= 0:
        dump_and_stop()
        return
    if phases in (0.2,):
        b.op("vector", lambda e: e.memset(T["kst"][:, :], 0.0))
        b.op("vector", lambda e: e.memset(T["vst"][:, :], 0.0))
    x_rd_v = [0, 0]
    x_rd_s = [0, 0]
    tab_rd = [0, 0]
    zbuf_rd_pe = 0
    prev_tev = 0
    last_rope = 0
    sc_v = 0

    def ln_tile(src_ap, src_waits, dst_fn, fold_rms, stats_ix, zb="zbuf"):
        nonlocal zbuf_rd_pe, prev_tev
        vb = b.op("vector", lambda e: e.bn_stats(out=T["bn24"][:, 0:6], in_=src_ap[:, 0:512]),
                  waits=src_waits)
        for j in range(1, 4):
            vb = b.op("vector", lambda e, j=j: e.bn_stats(
                out=T["bn24"][:, 6 * j:6 * (j + 1)], in_=src_ap[:, 512 * j:512 * (j + 1)]))
        b.wait("vector", "vector", vb)
        stat = T["stats"][:, 2 * stats_ix:2 * stats_ix + 2] if stats_ix is not None else T["stat2"][:]
        va = b.op("vector", lambda e: e.bn_aggr(out=stat, in_=T["bn24"][:]))
        sc_, bi_ = (1.000001, 0) if fold_rms else (1.0, 1)
        sa = b.op("scalar", lambda e: e.activation(out=T["t1"][:], in_=stat[:, 1:2], func=AF.Sqrt,
                                                   bias=T["cst"][:, bi_:bi_ + 1], scale=sc_),
                  waits=[("vector", va), ("vector", v_cst)])
        vr = b.op("vector", lambda e: e.reciprocal(out=T["scl"][:], in_=T["t1"][:]),
                  waits=[("scalar", sa)])
        b.wait("vector", "vector", vr)
        sz = b.op("vector", lambda e: e.tensor_scalar(
            out=T[zb][:], in0=src_ap, scalar1=stat[:, 0:1], scalar2=T["scl"][:],
            op0=OP.subtract, op1=OP.mult),
            waits=[("tensor", zbuf_rd_pe)])
        b.wait("vector", "vector", sz)
        ev = prev_tev
        for c in range(16):
            bk = 4 + (c % 4)
            tp = b.op("tensor", lambda e, c=c, bk=bk: e.transpose(
                bank(bk)[:, 0:128], T[zb][:, 128 * c:128 * (c + 1)], T["ident"][:]),
                waits=[("vector", sz), g_id, ("vector", ev)])
            dst, use_rms = dst_fn(c)
            if use_rms:
                ev = b.op("vector", lambda e, dst=dst, bk=bk, c=c: e.tensor_scalar_mul(
                    out=dst, in0=bank(bk)[:, 0:128], scalar1=T["rms16"][:, c:c + 1]),
                    waits=[("tensor", tp), ("vector", rms_v)])
            else:
                ev = b.op("vector", lambda e, dst=dst, bk=bk: e.tensor_copy(
                    out=dst, in_=bank(bk)[:, 0:128]), waits=[("tensor", tp)])
        zbuf_rd_pe = b.ct["tensor"]
        prev_tev = ev
        return ev, sz

    def rope_evac(psum_ap, tab, dst_top, dst_bot, mm_done, extra):
        w = [("tensor", mm_done)] + list(extra)
        b.op("vector", lambda e: e.tensor_mul(out=T["rtu"][:], in0=psum_ap[64:128, :], in1=tab[64:128, :]), waits=w)
        b.op("vector", lambda e: e.tensor_mul(out=T["rtv"][:], in0=psum_ap[0:64, :], in1=tab[64:128, :]))
        b.op("vector", lambda e: e.tensor_mul(out=T["rtw"][:], in0=psum_ap[0:64, :], in1=tab[0:64, :]))
        v4 = b.op("vector", lambda e: e.tensor_mul(out=T["rtz"][:], in0=psum_ap[64:128, :], in1=tab[0:64, :]))
        b.wait("vector", "vector", v4)
        b.op("vector", lambda e: e.tensor_sub(out=dst_top, in0=T["rtw"][:], in1=T["rtu"][:]))
        return b.op("vector", lambda e: e.tensor_add(out=dst_bot, in0=T["rtz"][:], in1=T["rtv"][:]))

    for tb in range(1 if phases == 0.2 else NTB):
        own = tb >= 6
        dst_zT = T["z_own"][tb - 6] if own else T["zT"]
        tp_ = tb % 2
        g_tab = ("qt%d" % tp_, b.dma("gpsimd", "qt%d" % tp_, T["tabbuf"][tp_][:],
                 C["in"]["ktab"][:, 512 * tb:512 * (tb + 1)], waits=[("vector", tab_rd[tp_])]))
        ev_last = 0
        for tt in range(4):
            p = (tb * 4 + tt) % 2
            g_x = ("qx%d" % p, b.dma("gpsimd", "qx%d" % p, T["xbuf"][p][:],
                   C["in"]["xrows"][512 * tb + 128 * tt:512 * tb + 128 * (tt + 1), :],
                   waits=[("vector", x_rd_v[p])]))
            def dst_fn(c, dst_zT=dst_zT, tt=tt):
                return dst_zT[:, 512 * c + 128 * tt:512 * c + 128 * (tt + 1)], True
            ev_last, sz = ln_tile(T["xbuf"][p][:], [g_x], dst_fn, True,
                                  (tb - 6) * 4 + tt if own else None)
            x_rd_v[p] = b.ct["vector"]
            x_rd_s[p] = sz
        for m in range(4):
            bk = m % 2
            pe_k = 0
            for k in range(16):
                pe_k = b.op("tensor", lambda e, m=m, k=k, bk=bk, dst_zT=dst_zT: e.matmul(
                    bank(bk), T["wkvb"][:, 1024 * k + 128 * m:1024 * k + 128 * (m + 1)],
                    dst_zT[:, 512 * k:512 * (k + 1)], start=(k == 0), stop=(k == 15)),
                    waits=([("vector", ev_last), ("vector", last_rope)] if k == 0 else []),
                    inc=(k == 15))
            cols = slice(4096 * m + 512 * tb, 4096 * m + 512 * (tb + 1))
            last_rope = rope_evac(bank(bk), T["tabbuf"][tp_], T["kst"][0:64, cols],
                                  T["kst"][64:128, cols], pe_k, [g_tab])
        tab_rd[tp_] = b.ct["vector"]
        for tt in range(4):
            bk = 2 + tt % 2
            pe_v = 0
            for k in range(16):
                pe_v = b.op("tensor", lambda e, tt=tt, k=k, bk=bk, dst_zT=dst_zT: e.matmul(
                    bank(bk), dst_zT[:, 512 * k + 128 * tt:512 * k + 128 * (tt + 1)],
                    T["wkvb"][:, 1024 * k + 512:1024 * (k + 1)], start=(k == 0), stop=(k == 15)),
                    waits=([("scalar", sc_v)] if k == 0 else []), inc=(k == 15))
            base = 520 * (tb * 4 + tt)
            for g in range(NG):
                sc_v = b.op("scalar", lambda e, g=g, base=base, bk=bk: e.copy(
                    out=T["vst"][:, base + 65 * g:base + 65 * g + 64],
                    in_=bank(bk)[:, 64 * g:64 * (g + 1)]),
                    waits=([("tensor", pe_v)] if g == 0 else []))

    if phases in (0.2, 0.7):
        dump_and_stop()
        return
    fQ = b.fence()
    qland_rd = 0
    wqb_rd = [0, 0]
    for m in range(16):
        mp = m % 2
        q_q = b.dma("sync", "qq", T["qland"][:], C["in"]["wq"][m],
                    waits=[("vector", max(qland_rd, fQ["vector"])), ("scalar", fQ["scalar"])])
        qcv = b.op("vector", lambda e, mp=mp: e.tensor_copy(out=T["wqb"][mp][:], in_=T["qland"][:]),
                   waits=[("qq", q_q), ("tensor", wqb_rd[mp]), ("scalar", fQ["scalar"])])
        qland_rd = qcv
        for tbq in range(2):
            bk = tbq
            pe_q = 0
            for k in range(16):
                pe_q = b.op("tensor", lambda e, k=k, tbq=tbq, bk=bk, mp=mp: e.matmul(
                    bank(bk), T["wqb"][mp][:, 128 * k:128 * (k + 1)],
                    T["z_own"][tbq][:, 512 * k:512 * (k + 1)], start=(k == 0), stop=(k == 15)),
                    waits=([("vector", qcv), ("vector", last_rope)] if k == 0 else []), inc=(k == 15))
            cols = slice(1024 * m + 512 * tbq, 1024 * m + 512 * (tbq + 1))
            last_rope = rope_evac(bank(bk), T["qtab"][:, 512 * tbq:512 * (tbq + 1)],
                                  T["qot"][0:64, cols], T["qot"][64:128, cols], pe_q,
                                  [g_qtab])
        wqb_rd[mp] = b.ct["tensor"]

    if phases <= 1:
        dump_and_stop()
        return
    fP2 = b.fence()
    v_kones = b.op("vector", lambda e: e.memset(T["ktil"][64:65, :], 1.0),
                   waits=[("tensor", fP2["tensor"])])
    last_av = 0
    act_exp = 0
    osb_wr = 0
    last_vevac = last_rope
    pool_r1 = 0
    s_kq = 0
    sc_p1 = fP2["scalar"]
    for g in range(NG):
        mk, half = g // 2, g % 2
        kcols = slice(4096 * mk, 4096 * (mk + 1))
        sv = b.op("vector", lambda e, half=half, kcols=kcols: e.tensor_copy(
            out=T["ktil"][0:32, :], in_=T["kst"][32 * half:32 * half + 32, kcols]),
            waits=[("vector", last_vevac), ("tensor", max(last_av, fP2["tensor"])),
                   ("gpsimd", pool_r1), ("scalar", s_kq)])
        sv = b.op("vector", lambda e, half=half, kcols=kcols: e.tensor_copy(
            out=T["ktil"][32:64, :], in_=T["kst"][64 + 32 * half:96 + 32 * half, kcols]))
        for h in range(HPG):
            hh = 4 * g + h
            c, sub = hh // 2, hh % 2
            qc = slice(1024 * c, 1024 * (c + 1))
            sv = b.op("vector", lambda e, h=h, sub=sub, qc=qc: e.tensor_copy(
                out=T["qtil"][0:32, 1024 * h:1024 * (h + 1)], in_=T["qot"][32 * sub:32 * sub + 32, qc]))
            sv = b.op("vector", lambda e, h=h, sub=sub, qc=qc: e.tensor_copy(
                out=T["qtil"][32:64, 1024 * h:1024 * (h + 1)], in_=T["qot"][64 + 32 * sub:96 + 32 * sub, qc]))
        b.wait("vector", "vector", sv)
        v1 = b.op("vector", lambda e: e.tensor_mul(out=T["sqs"][:], in0=T["ktil"][0:64, :], in1=T["ktil"][0:64, :]))
        p1 = b.op("gpsimd", lambda e: e.tensor_reduce(out=T["r1"][:], in_=T["sqs"][:], axis=AX.C, op=OP.add),
                  waits=[("vector", v1), ("scalar", s_kq)])
        v2 = b.op("vector", lambda e: e.reduce_max(out=T["km2"][:], in_=T["r1"][:], axis=AX.X),
                  waits=[("gpsimd", p1)])
        s1 = b.op("scalar", lambda e: e.activation(out=T["km"][:], in_=T["km2"][:], func=AF.Sqrt),
                  waits=[("vector", v2)])
        v3 = b.op("vector", lambda e: e.tensor_scalar_mul(out=T["kms"][:], in0=T["km"][:], scalar1=-BOUND_MARGIN),
                  waits=[("scalar", s1)])
        v4 = b.op("vector", lambda e: e.tensor_mul(out=T["sqs"][:], in0=T["qtil"][0:64, :], in1=T["qtil"][0:64, :]),
                  waits=[("gpsimd", p1)])
        p2 = b.op("gpsimd", lambda e: e.tensor_reduce(out=T["r1"][:], in_=T["sqs"][:], axis=AX.C, op=OP.add),
                  waits=[("vector", v4), ("vector", v2)])
        pool_r1 = p2
        s_kq = b.op("scalar", lambda e: e.activation(out=T["qtil"][64:65, :], in_=T["r1"][:], func=AF.Sqrt),
                    waits=[("gpsimd", p2), ("vector", sv)])
        vb_ = b.op("vector", lambda e: e.tensor_scalar_mul(out=T["qtil"][64:65, :], in0=T["qtil"][64:65, :],
                                                           scalar1=T["kms"][:]),
                   waits=[("scalar", s_kq), ("vector", v3)])
        bound_ready = vb_

        for qb in range(2):
            tiles = [("o", kt) for kt in range(12 if qb == 0 else 24)] + \
                    [("s", j) for j in range(4 if qb == 0 else 8)]
            for idx, (kind, kt) in enumerate(tiles):
                if kind == "o":
                    krows = slice(128 * kt, 128 * (kt + 1))
                    bias_ap = T["biascol"][:, 24 * qb + kt:24 * qb + kt + 1]
                    vtile = kt
                else:
                    krows = slice(128 * (24 + kt), 128 * (25 + kt))
                    bias_ap = T["biascol"][:, 48:49]
                    vtile = 24 + kt
                pp = idx % 2
                qk = 0
                for h in range(HPG):
                    qk = b.op("tensor", lambda e, h=h, krows=krows, qb=qb: e.matmul(
                        bank(h), T["ktil"][:, krows],
                        T["qtil"][:, 1024 * h + 512 * qb:1024 * h + 512 * (qb + 1)],
                        start=True, stop=True),
                        waits=([("vector", max(bound_ready, last_vevac)), ("scalar", max(act_exp, osb_wr))]
                               if h == 0 else []), inc=(h == 3))
                act_exp = b.op("scalar", lambda e, pp=pp, bias_ap=bias_ap: e.activation(
                    out=T["pbuf"][pp][:], in_=pb[:, 0:2048], func=AF.Exp, bias=bias_ap, scale=1.0),
                    waits=[("tensor", max(qk, last_av)), g_bias])
                p_ready = [("scalar", act_exp)]
                if kind == "s" and not (qb == 1 and kt < 4):
                    pm = b.op("gpsimd", lambda e, pp=pp, qb=qb, kt=kt: e.affine_select(
                        out=T["pbuf"][pp][:], in_=T["pbuf"][pp][:], pattern=[[0, 4], [1, 512]],
                        compare_op=OP.is_ge, fill=0.0, base=512 * qb - 128 * kt, channel_multiplier=-1),
                        waits=[("scalar", act_exp)])
                    p_ready = [("gpsimd", pm)]
                for h in range(HPG):
                    last_av = b.op("tensor", lambda e, h=h, vtile=vtile, g=g, pp=pp, idx=idx, ntile=len(tiles): e.matmul(
                        pb[0:65, 2048 + 512 * h:2048 + 512 * (h + 1)],
                        T["vst"][:, 520 * vtile + 65 * g:520 * vtile + 65 * (g + 1)],
                        T["pbuf"][pp][:, 512 * h:512 * (h + 1)], start=(idx == 0), stop=(idx == ntile - 1)),
                        waits=((p_ready + [("vector", v_ones), ("scalar", max(osb_wr, sc_p1))])
                               if h == 0 else []), inc=(h == 3))
            so = b.op("scalar", lambda e: e.copy(out=T["osb"][:], in_=pb[0:65, 2048:4096]),
                      waits=[("tensor", last_av), ("vector", last_vevac), ("gpsimd", pool_r1)])
            osb_wr = so
            vrc = b.op("vector", lambda e: e.reciprocal(out=T["rrow32"][:], in_=T["osb"][64:65, :]),
                       waits=[("scalar", so)])
            b.wait("vector", "vector", vrc)
            vrc = b.op("vector", lambda e: e.tensor_copy(out=T["rrow"][:], in_=T["rrow32"][:]))
            bc = 0
            for h in range(HPG):
                bc = b.op("tensor", lambda e, h=h: e.matmul(
                    pb[0:64, 512 * h:512 * (h + 1)], T["ones164"][:],
                    T["rrow"][:, 512 * h:512 * (h + 1)], start=True, stop=True),
                    waits=([("vector", max(vrc, v_o164)), ("scalar", act_exp)] if h == 0 else []),
                    inc=(h == 3))
            for h in range(HPG):
                hh = 4 * g + h
                c, sub = hh // 2, hh % 2
                last_vevac = b.op("vector", lambda e, h=h, c=c, sub=sub, qb=qb: e.tensor_mul(
                    out=T["qot"][64 * sub:64 * (sub + 1), 1024 * c + 512 * qb:1024 * c + 512 * (qb + 1)],
                    in0=T["osb"][0:64, 512 * h:512 * (h + 1)], in1=pb[0:64, 512 * h:512 * (h + 1)]),
                    waits=[("tensor", bc), ("scalar", so)])

    if phases <= 2:
        dump_and_stop()
        return
    fP3 = b.fence()
    acc = T["acc"].ap()
    xq_rd_v = [fP3["vector"], fP3["vector"]]
    xq_rd_s = [0, 0]
    acc_ev = 0
    ybuf_rd = 0
    for tt in range(8):
        p = tt % 2
        g_xq = b.dma("gpsimd", "qxq%d" % p, T["xq"][p][:],
                     C["in"]["xrows"][3072 + 128 * tt:3072 + 128 * (tt + 1), :],
                     waits=[("vector", xq_rd_v[p]), ("tensor", fP3["tensor"])])
        sa = b.op("scalar", lambda e, tt=tt: e.activation(
            out=T["t1"][:], in_=T["stats"][:, 2 * tt + 1:2 * tt + 2], func=AF.Sqrt,
            bias=T["cst"][:, 1:2], scale=1.0),
            waits=[("vector", acc_ev)])
        vr = b.op("vector", lambda e: e.reciprocal(out=T["scl"][:], in_=T["t1"][:]),
                  waits=[("scalar", sa), ("vector", fP3["vector"])])
        b.wait("vector", "vector", vr)
        sy = b.op("vector", lambda e, p=p, tt=tt: e.tensor_scalar(
            out=T["ybuf"][:], in0=T["xq"][p][:], scalar1=T["stats"][:, 2 * tt:2 * tt + 1],
            scalar2=T["scl"][:], op0=OP.subtract, op1=OP.mult),
            waits=[("vector", ybuf_rd), ("qxq%d" % p, g_xq)])
        b.wait("vector", "vector", sy)
        acc_ev = b.op("vector", lambda e, p=p, tt=tt: e.tensor_add(
            out=acc[:, 2048 * tt:2048 * (tt + 1)], in0=T["xq"][p][:], in1=T["ybuf"][:]),
            waits=[("tensor", fP3["tensor"]), ("gpsimd", fP3["gpsimd"])])
        ybuf_rd = acc_ev
        xq_rd_v[p] = acc_ev
        xq_rd_s[p] = 0
    wol_rd = [0, 0]
    wob_rd_pe = 0
    for n in range(4):
        cvt = 0
        for k in range(16):
            kp = k % 2
            q_wo = b.dma("sync", "qwo%d" % kp, T["woland"][kp][:], C["in"]["wout"][n, k],
                         waits=[("vector", max(wol_rd[kp], fP3["vector"])), ("tensor", fP3["tensor"])])
            cvt = b.op("vector", lambda e, k=k, kp=kp: e.tensor_copy(
                out=T["wob"][:, 512 * k:512 * (k + 1)], in_=T["woland"][kp][:]),
                waits=[("qwo%d" % kp, q_wo), ("tensor", wob_rd_pe)])
            wol_rd[kp] = cvt
        for th in range(2):
            pe_wo = 0
            for k in range(16):
                for t4 in range(4):
                    tt = 4 * th + t4
                    pe_wo = b.op("tensor", lambda e, k=k, tt=tt, t4=t4: e.matmul(
                        bank(t4), T["qot"][:, 1024 * k + 128 * tt:1024 * k + 128 * (tt + 1)],
                        T["wob"][:, 512 * k:512 * (k + 1)], start=(k == 0), stop=(k == 15)),
                        waits=([("vector", max(cvt, acc_ev, last_vevac))] if (k == 0 and t4 == 0) else []),
                        inc=(k == 15 and t4 == 3))
            for t4 in range(4):
                tt = 4 * th + t4
                sl = slice(2048 * tt + 512 * n, 2048 * tt + 512 * (n + 1))
                acc_ev = b.op("vector", lambda e, t4=t4, sl=sl: e.tensor_add(
                    out=acc[:, sl], in0=bank(t4), in1=acc[:, sl]),
                    waits=[("tensor", pe_wo), ("vector", acc_ev)])
        wob_rd_pe = b.ct["tensor"]
    x1_ev = 0
    for tt in range(8):
        src = acc[:, 2048 * tt:2048 * (tt + 1)]
        def dst2(c, tt=tt):
            return T["x1nT"][:, 1024 * c + 128 * tt:1024 * c + 128 * (tt + 1)], False
        x1_ev, _ = ln_tile(src, [("vector", acc_ev)], dst2, False, None, zb="zbuf2")
    x1nT_done = x1_ev

    if phases <= 3:
        dump_and_stop()
        return
    se_rd = 0
    vm_rd = 0
    sg_rd = 0
    coef_done = 0
    sig_done = 0
    for tt in range(8):
        pe_r = 0
        for k in range(16):
            pe_r = b.op("tensor", lambda e, k=k, tt=tt: e.matmul(
                pb[:, 0:17], T["x1nT"][:, 1024 * k + 128 * tt:1024 * k + 128 * (tt + 1)],
                T["wgateb"][:, 17 * k:17 * (k + 1)], start=(k == 0), stop=(k == 15)),
                waits=([("vector", max(x1nT_done, v_gate, vm_rd)), ("scalar", max(se_rd, sg_rd))] if k == 0 else []),
                inc=(k == 15))
        vm = b.op("vector", lambda e: e.reduce_max(out=T["t1"][:], in_=pb[:, 0:16], axis=AX.X, negate=True),
                  waits=[("tensor", pe_r)])
        se = b.op("scalar", lambda e: e.activation(out=T["probs"][:], in_=pb[:, 0:16], func=AF.Exp,
                                                   bias=T["t1"][:], scale=1.0),
                  waits=[("vector", max(vm, coef_done))])
        se_rd = se
        vs = b.op("vector", lambda e: e.reduce_sum(out=T["t2"][:], in_=T["probs"][:], axis=AX.X),
                  waits=[("scalar", se)])
        b.wait("vector", "vector", vs)
        vrc2 = b.op("vector", lambda e: e.reciprocal(out=T["t3"][:], in_=T["t2"][:]))
        b.wait("vector", "vector", vrc2)
        vn = b.op("vector", lambda e: e.tensor_scalar_mul(out=T["probs"][:], in0=T["probs"][:], scalar1=T["t3"][:]))
        b.wait("vector", "vector", vn)
        v8 = b.op("vector", lambda e: e.max(out=T["top8"][:], in_=T["probs"][:]))
        b.wait("vector", "vector", v8)
        coef_done = b.op("vector", lambda e, tt=tt: e.scalar_tensor_tensor(
            out=T["coef"][:, 16 * tt:16 * (tt + 1)], in0=T["probs"][:], scalar=T["top8"][:, 3:4],
            in1=T["probs"][:], op0=OP.is_ge, op1=OP.mult))
        vm_rd = coef_done
        sig_done = b.op("scalar", lambda e, tt=tt: e.activation(out=T["sig"][:, tt:tt + 1], in_=pb[:, 16:17],
                                                                func=AF.Sigmoid), waits=[("tensor", pe_r)])
        sg_rd = sig_done

    if phases <= 4:
        dump_and_stop()
        return
    _P5_CUT = (phases == 4.5)
    fP5 = b.fence()
    w13l_rd = [0, 0]
    w2l_rd = [0, 0]
    w13b_rd = [0, 0]
    w2b_rd = [0, 0]
    hT_rd_pe = 0
    gs_wr = 0
    gs_rd_v = 0
    h_wr = 0
    b.wait("tensor", "scalar", sig_done)
    b.wait("tensor", "vector", coef_done)
    for e_ix in range({4.5: 2, 4.6: 8, 4.7: 14, 4.8: 17, 4.85: 18, 4.9: 19}.get(phases, NE)):
        for jm in range(22):
            mp = jm % 2
            cvt = 0
            for pc in range(2):
                kp = pc % 2
                q13 = b.dma("sync", "q13a" if kp == 0 else "q13b", T["w13land"][kp][:],
                            C["in"]["w13"][e_ix, jm, :, 1024 * pc:1024 * (pc + 1)],
                            waits=[("vector", max(w13l_rd[kp], fP5["vector"])),
                                   ("scalar", fP5["scalar"]), ("tensor", fP5["tensor"])])
                cvt = b.op("vector", lambda e, pc=pc, kp=kp, mp=mp: e.tensor_copy(
                    out=T["w13b"][mp][:, 1024 * pc:1024 * (pc + 1)], in_=T["w13land"][kp][:]),
                    waits=[("q13a" if kp == 0 else "q13b", q13), ("tensor", w13b_rd[mp])])
                w13l_rd[kp] = cvt
            is_g = (jm % 2 == 0)
            j = jm // 2
            pe_m = 0
            for tb in range(2):
                bk = (0 if is_g else 2) + tb
                for k in range(16):
                    pe_m = b.op("tensor", lambda e, k=k, tb=tb, bk=bk, mp=mp: e.matmul(
                        bank(bk), T["w13b"][mp][:, 128 * k:128 * (k + 1)],
                        T["x1nT"][:, 1024 * k + 512 * tb:1024 * k + 512 * (tb + 1)],
                        start=(k == 0), stop=(k == 15)),
                        waits=(([("vector", cvt)] + ([("scalar", gs_wr)] if is_g else [("vector", h_wr)]))
                               if (k == 0 and tb == 0) else []), inc=(k == 15))
            w13b_rd[mp] = pe_m
            if is_g:
                for tb in range(2):
                    gs_wr = b.op("scalar", lambda e, tb=tb: e.activation(
                        out=T["gs"][:, 512 * tb:512 * (tb + 1)], in_=bank(tb), func=AF.Silu),
                        waits=[("tensor", pe_m), ("vector", gs_rd_v)] if tb == 0 else [("tensor", pe_m)])
            else:
                for tb in range(2):
                    h_wr = b.op("vector", lambda e, tb=tb, j=j: e.tensor_mul(
                        out=T["hT"][:, 1024 * j + 512 * tb:1024 * j + 512 * (tb + 1)],
                        in0=T["gs"][:, 512 * tb:512 * (tb + 1)], in1=bank(2 + tb)),
                        waits=[("scalar", gs_wr), ("tensor", max(pe_m, hT_rd_pe))])
                gs_rd_v = h_wr
        acc_ev_l = acc_ev
        for n in range(4):
            np_ = n % 2
            w2cv = 0
            for k in range(4):
                kp = k % 2
                q2 = b.dma("gpsimd", "q2a" if kp == 0 else "q2b", T["w2land"][kp][:],
                           C["in"]["w2"][e_ix, n, :, 1408 * k:1408 * (k + 1)],
                           waits=[("vector", max(w2l_rd[kp], fP5["vector"])),
                                  ("tensor", fP5["tensor"]), ("scalar", fP5["scalar"])])
                w2cv = b.op("vector", lambda e, k=k, kp=kp, np_=np_: e.tensor_copy(
                    out=T["w2b"][np_][:, 1408 * k:1408 * (k + 1)], in_=T["w2land"][kp][:]),
                    waits=[("q2a" if kp == 0 else "q2b", q2), ("tensor", w2b_rd[np_])])
                w2l_rd[kp] = w2cv
            for th in range(2):
                pe_d = 0
                for k in range(11):
                    for t4 in range(4):
                        tt = 4 * th + t4
                        pe_d = b.op("tensor", lambda e, k=k, tt=tt, t4=t4, np_=np_: e.matmul(
                            bank(4 + t4), T["hT"][:, 1024 * k + 128 * tt:1024 * k + 128 * (tt + 1)],
                            T["w2b"][np_][:, 512 * k:512 * (k + 1)], start=(k == 0), stop=(k == 10)),
                            waits=([("vector", max(w2cv, h_wr, acc_ev))] if (k == 0 and t4 == 0) else []),
                            inc=(k == 10 and t4 == 3))
                for t4 in range(4):
                    tt = 4 * th + t4
                    if e_ix < 4:
                        coef_ap = T["sig"][:, tt:tt + 1]
                    else:
                        coef_ap = T["coef"][:, 16 * tt + (e_ix - 4):16 * tt + (e_ix - 4) + 1]
                    sl = slice(2048 * tt + 512 * n, 2048 * tt + 512 * (n + 1))
                    acc_ev = b.op("vector", lambda e, t4=t4, sl=sl, coef_ap=coef_ap: e.scalar_tensor_tensor(
                        out=acc[:, sl], in0=bank(4 + t4), scalar=coef_ap, in1=acc[:, sl],
                        op0=OP.mult, op1=OP.add),
                        waits=[("tensor", pe_d), ("vector", max(acc_ev, coef_done)),
                               ("scalar", sig_done)])
            w2b_rd[np_] = pe_d
        hT_rd_pe = b.ct["tensor"]
    if phases in (4.5, 4.6, 4.7, 4.8, 4.85, 4.9):
        dump_and_stop()
        return

    qo = 0
    for tt in range(8):
        qo = b.dma("gpsimd", "qo", C["out_ap"][128 * tt:128 * (tt + 1), :], acc[:, 2048 * tt:2048 * (tt + 1)],
                   waits=[("vector", acc_ev)])
    fE = b.fence()
    for s_ in QSEMS:
        b.wait("gpsimd", s_, fE[s_])



def build_program(phases=6, dumps=()):
    nc = bass.Bass("TRN2", target_bir_lowering=False)
    CIN = {}
    CIN["xrows"] = nc.dram_tensor("xrows", [S, D], F32, kind="ExternalInput").ap()
    CIN["ktab"] = nc.dram_tensor("ktab", [128, S], F32, kind="ExternalInput").ap()
    CIN["qtab"] = nc.dram_tensor("qtab", [128, OWN], F32, kind="ExternalInput").ap()
    CIN["biascol"] = nc.dram_tensor("biascol", [128, 49], F32, kind="ExternalInput").ap()
    CIN["wkv"] = nc.dram_tensor("wkv", [16, 128, 1024], F32, kind="ExternalInput").ap()
    CIN["wq"] = nc.dram_tensor("wq", [16, 128, 2048], F32, kind="ExternalInput").ap()
    CIN["wout"] = nc.dram_tensor("wout", [4, 16, 128, 512], F32, kind="ExternalInput").ap()
    CIN["rms16"] = nc.dram_tensor("rms16", [128, 16], F32, kind="ExternalInput").ap()
    CIN["wgate"] = nc.dram_tensor("wgate", [128, 272], F32, kind="ExternalInput").ap()
    CIN["w13"] = nc.dram_tensor("w13", [NE, 22, 128, 2048], F32, kind="ExternalInput").ap()
    CIN["w2"] = nc.dram_tensor("w2", [NE, 4, 128, 5632], F32, kind="ExternalInput").ap()
    out_ap = nc.dram_tensor("out", [OWN, D], F32, kind="ExternalOutput").ap()
    ident_dram = nc.inline_tensor(np.eye(128, dtype=np.float32), "identf32")

    SBASE = 16512
    def at(name, shape, dtype, off):
        return nc.alloc_sbuf_tensor_at(name, shape, dtype, offset=SBASE + off)

    T = {}
    EB, FB, MB = 98816, 164352, 197120
    T["kst"] = at("kst", [128, 16384], BF16, 0).ap()
    T["vst"] = at("vst", [128, 16640], BF16, 32768)
    T["qot"] = at("qot", [128, 16384], BF16, 66048).ap()
    T["acc"] = at("acc", [128, 16384], F32, EB)
    T["x1nT"] = at("x1nT", [128, 16384], BF16, FB).ap()
    T["z_own"] = [at("z_own0", [128, 8192], BF16, EB).ap(), at("z_own1", [128, 8192], BF16, EB + 16384).ap()]
    T["rtu"] = at("rtu", [64, 512], F32, EB + 32768)
    T["rtv"] = at("rtv", [64, 512], F32, EB + 34816)
    T["rtw"] = at("rtw", [64, 512], F32, EB + 36864)
    T["rtz"] = at("rtz", [64, 512], F32, EB + 38912)
    T["xbuf"] = [at("xbuf0", [128, 2048], F32, EB + 40960), at("xbuf1", [128, 2048], F32, EB + 49152)]
    T["qland"] = at("qland", [128, 2048], F32, EB + 40960)
    T["wqb"] = [at("wqb0", [128, 2048], BF16, EB + 49152), at("wqb1", [128, 2048], BF16, EB + 53248)]
    T["zT"] = at("zT", [128, 8192], BF16, FB).ap()
    T["zbuf"] = at("zbuf", [128, 2048], F32, FB + 16384)
    T["kvland"] = at("kvland", [128, 1024], F32, FB + 24576)
    T["tabbuf"] = [at("tabbuf0", [128, 512], F32, FB + 28672), at("tabbuf1", [128, 512], F32, FB + 30720)]
    T["wkvb"] = at("wkvb", [128, 16384], BF16, 66048).ap()
    T["qtil"] = at("qtil", [65, 4096], BF16, EB).ap()
    T["ktil"] = at("ktil", [65, 4096], BF16, EB + 8192).ap()
    T["sqs"] = at("sqs", [64, 4096], BF16, EB + 16384)
    T["r1"] = at("r1", [1, 4096], F32, EB + 24576)
    T["pbuf"] = [at("pbuf0", [128, 2048], BF16, EB + 40960), at("pbuf1", [128, 2048], BF16, EB + 45056)]
    T["osb"] = at("osb", [65, 2048], F32, EB + 16384).ap()
    T["rrow"] = at("rrow", [1, 2048], BF16, EB + 49152)
    T["rrow32"] = at("rrow32", [1, 2048], F32, EB + 53248)
    T["xq"] = [at("xq0", [128, 2048], F32, 0), at("xq1", [128, 2048], F32, 8192)]
    T["wob"] = at("wob", [128, 8192], BF16, 16384)
    T["woland"] = [at("woland0", [128, 512], F32, 32768), at("woland1", [128, 512], F32, 34816)]
    T["ybuf"] = at("ybuf", [128, 2048], F32, 36864)
    T["zbuf2"] = at("zbuf2", [128, 2048], F32, 45056)
    T["w13land"] = [at("w13land0", [128, 1024], F32, 0), at("w13land1", [128, 1024], F32, 4096)]
    T["w13b"] = [at("w13b0", [128, 2048], BF16, 8192), at("w13b1", [128, 2048], BF16, 12288)]
    T["w2land"] = [at("w2land0", [128, 1408], F32, 16384), at("w2land1", [128, 1408], F32, 22016)]
    T["w2b"] = [at("w2b0", [128, 5632], BF16, 27648), at("w2b1", [128, 5632], BF16, 38912)]
    T["hT"] = at("hT", [128, 11264], BF16, 50176).ap()
    T["gs"] = at("gs", [128, 1024], F32, 72704)
    off = [MB]
    def misc(name, shape, dtype):
        t = at(name, shape, dtype, off[0])
        nbytes = int(np.prod(shape[1:])) * mybir.dt.size(dtype)
        off[0] += (nbytes + 31) // 32 * 32
        return t
    T["ident"] = misc("ident", [128, 128], F32)
    T["biascol"] = misc("biascolS", [128, 49], F32)
    T["rms16land"] = misc("rms16land", [128, 16], F32)
    T["rms16"] = misc("rms16S", [128, 16], F32)
    T["stats"] = misc("stats", [128, 16], F32)
    T["stat2"] = misc("stat2", [128, 2], F32)
    T["bn24"] = misc("bn24", [128, 24], F32)
    T["t1"] = misc("t1", [128, 1], F32)
    T["scl"] = misc("scl", [128, 1], F32)
    T["bia"] = misc("bia", [128, 1], F32)
    T["t2"] = misc("t2", [128, 1], F32)
    T["t3"] = misc("t3", [128, 1], F32)
    T["km2"] = misc("km2", [1, 1], F32)
    T["km"] = misc("km", [1, 1], F32)
    T["kms"] = misc("kms", [1, 1], F32)
    T["qtab"] = misc("qtabS", [128, 1024], F32)
    T["wgateland"] = misc("wgateland", [128, 272], F32)
    T["wgateb"] = misc("wgateb", [128, 272], BF16)
    T["probs"] = misc("probs", [128, 16], F32)
    T["top8"] = misc("top8", [128, 8], F32)
    T["coef"] = misc("coef", [128, 128], F32)
    T["sig"] = misc("sig", [128, 8], F32)
    T["ones164"] = misc("ones164", [1, 64], BF16)
    T["cst"] = misc("cst", [128, 2], F32)
    assert off[0] <= 212700, f"misc overflow: {off[0]}"
    vst_h = T["vst"]
    T["vst"] = vst_h.ap()
    T["vst_ones"] = bass.AP(vst_h, 64, [[16640, 128], [520, 32], [65, 8], [1, 1]])
    T["acc"] = T["acc"]
    T["pb"] = nc.alloc_psum_tensor("pb", [128, 4096], F32).ap()

    C = {"in": CIN, "out_ap": out_ap, "ident_dram": ident_dram}
    dump_specs = {
        "kst": (T["kst"][:, :], [128, 16384], BF16),
        "vst": (T["vst"][:, :], [128, 16640], BF16),
        "qot": (T["qot"][:, :], [128, 16384], BF16),
        "zT0": (T["z_own"][0][:, :], [128, 8192], BF16),
        "zT1": (T["z_own"][1][:, :], [128, 8192], BF16),
        "wkvb": (T["wkvb"][:, :], [128, 16384], BF16),
        "acc": (T["acc"][:], [128, 16384], F32),
        "x1nT": (T["x1nT"][:, :], [128, 16384], BF16),
        "coef": (T["coef"][:], [128, 128], F32),
        "sig": (T["sig"][:], [128, 8], F32),
        "stats": (T["stats"][:], [128, 16], F32),
        "hT": (T["hT"][:, :], [128, 11264], BF16),
    }
    C["dumps"] = []
    C["dbg_aps"] = {}
    for name in dumps:
        ap_, shp, dt = dump_specs[name]
        C["dbg_aps"][name] = nc.dram_tensor("dbg_" + name, shp, dt, kind="ExternalOutput").ap()
        C["dumps"].append((name, ap_))

    import contextlib
    with contextlib.ExitStack() as st:
        sems = {k: st.enter_context(nc.semaphore("sem_" + k)) for k in ALLSEMS}
        block = st.enter_context(nc.Block())

        @block.tensor
        def _(eng):
            build_walk(T, C, Bld("tensor", eng, sems), phases)

        @block.scalar
        def _(eng):
            build_walk(T, C, Bld("scalar", eng, sems), phases)

        @block.vector
        def _(eng):
            build_walk(T, C, Bld("vector", eng, sems), phases)

        @block.gpsimd
        def _(eng):
            build_walk(T, C, Bld("gpsimd", eng, sems), phases)

        @block.sync
        def _(eng):
            build_walk(T, C, Bld("sync", eng, sems), phases)

    return nc


_PROG_CACHE = {}

def kernel(**inputs):
    shared = _shared_layouts(inputs)
    if "nc" not in _PROG_CACHE:
        _PROG_CACHE["nc"] = build_program()
    nc = _PROG_CACHE["nc"]
    in_maps = []
    for core in range(NCORES):
        m = dict(shared)
        m.update(_per_core(inputs, core))
        in_maps.append(m)
    res = run_bass_kernel_spmd(nc, in_maps, core_ids=list(range(NCORES)))
    out = np.zeros((BATCH, S, D), np.float32)
    for core in range(NCORES):
        bb, i = core // 4, core % 4
        o = res.results[core]["out"]
        out[bb, i * 512:(i + 1) * 512] = o[0:512]
        out[bb, (7 - i) * 512:(8 - i) * 512] = o[512:1024]
    return out


# version 04
# speedup vs baseline: 1.6102x; 1.6102x over previous
import numpy as np
import ml_dtypes
import concourse.bass as bass
import concourse.mybir as mybir
from concourse.bass_utils import run_bass_kernel_spmd
from concourse.alu_op_type import AluOpType as OP

F32 = mybir.dt.float32
BF16 = mybir.dt.bfloat16
AF = mybir.ActivationFunctionType
AX = mybir.AxisListType

NCORES = 8
D = 2048
S = 4096
BATCH = 2
NG, HPG, DH = 8, 4, 64
NE_REAL = 16
I_EXP, I_SH = 1408, 5632
NE = 20
OWN = 1024
NTB = 8
NEG = -1e30
BOUND_MARGIN = 1.05


def _mblocks_2816(W):
    A = W.reshape(22, 128, 16, 128).transpose(0, 3, 2, 1).reshape(22, 128, 2048)
    order = []
    for j in range(11):
        order += [j, 11 + j]
    return np.ascontiguousarray(A[order])

def _nblocks_1408(W2):
    return np.ascontiguousarray(W2.reshape(4, 512, 11, 128).transpose(0, 3, 2, 1).reshape(4, 128, 5632))

def _shared_layouts(inp):
    out = {}
    w_in = np.asarray(inp["L0_attn_w_in"], np.float32)
    permk = []
    for m in range(4):
        for half in (0, 1):
            for g in (2 * m, 2 * m + 1):
                base = 2048 + g * 64 + 32 * half
                permk += list(range(base, base + 32))
    WK = w_in[permk]
    WV = w_in[2560:3072]
    out["wkv"] = np.ascontiguousarray(np.concatenate(
        [WK.reshape(512, 16, 128).transpose(1, 2, 0),
         WV.reshape(512, 16, 128).transpose(1, 2, 0)], axis=2))
    permq = []
    for m in range(16):
        for half in (0, 1):
            for h in (2 * m, 2 * m + 1):
                base = h * 64 + 32 * half
                permq += list(range(base, base + 32))
    WQ = w_in[permq]
    out["wq"] = np.ascontiguousarray(WQ.reshape(16, 128, 16, 128).transpose(0, 3, 2, 1).reshape(16, 128, 2048))
    w_out = np.asarray(inp["L0_attn_w_out"], np.float32)
    out["wout"] = np.ascontiguousarray(w_out.reshape(4, 512, 16, 128).transpose(0, 2, 3, 1))
    out["rms16"] = np.ascontiguousarray(np.asarray(inp["L0_attn_rms_w"], np.float32).reshape(16, 128).T)
    gate = np.asarray(inp["L0_ffn_gate_w"], np.float32)
    g16 = gate.T.reshape(16, 128, 16)
    sg1 = np.asarray(inp["L0_ffn_sgate_w"], np.float32)[0].reshape(16, 128, 1)
    out["wgate"] = np.ascontiguousarray(np.concatenate([g16, sg1], axis=2).transpose(1, 0, 2).reshape(128, 272))
    sgu = np.asarray(inp["L0_ffn_shared_gate_up_w"], np.float32)
    sdn = np.asarray(inp["L0_ffn_shared_down_w"], np.float32)
    w13 = np.asarray(inp["L0_ffn_w13"], np.float32)
    w2 = np.asarray(inp["L0_ffn_w2"], np.float32)
    e13, e2 = [], []
    for j in range(4):
        W13p = np.concatenate([sgu[I_EXP * j:I_EXP * (j + 1)],
                               sgu[I_SH + I_EXP * j:I_SH + I_EXP * (j + 1)]], axis=0)
        e13.append(_mblocks_2816(W13p))
        e2.append(_nblocks_1408(sdn[:, I_EXP * j:I_EXP * (j + 1)]))
    for e in range(NE_REAL):
        e13.append(_mblocks_2816(w13[e]))
        e2.append(_nblocks_1408(w2[e]))
    out["w13"] = np.ascontiguousarray(np.stack(e13))
    out["w2"] = np.ascontiguousarray(np.stack(e2))
    return out

def _rope_tab(pos, scale):
    inv = 1.0 / (10000.0 ** (np.arange(0, DH, 2) / DH))
    ang = pos[None, :].astype(np.float64) * inv[:, None]
    c = (np.cos(ang) * scale).astype(np.float32)
    s = (np.sin(ang) * scale).astype(np.float32)
    return np.ascontiguousarray(np.concatenate([c, c, s, s], axis=0))

def _per_core(inp, core):
    b, i = core // 4, core % 4
    x = np.asarray(inp["L0_attn_x"], np.float32)[b]
    others = list(range(0, i)) + list(range(i + 1, 7 - i))
    xrows = np.zeros((S, D), np.float32)
    pos = np.zeros((S,), np.int64)
    for s_, ch in enumerate(others):
        xrows[s_ * 512:(s_ + 1) * 512] = x[ch * 512:(ch + 1) * 512]
        pos[s_ * 512:(s_ + 1) * 512] = np.arange(ch * 512, (ch + 1) * 512)
    xrows[3072:3584] = x[i * 512:(i + 1) * 512]
    pos[3072:3584] = np.arange(i * 512, (i + 1) * 512)
    xrows[3584:4096] = x[(7 - i) * 512:(8 - i) * 512]
    pos[3584:4096] = np.arange((7 - i) * 512, (8 - i) * 512)
    biascol = np.zeros((128, 49), np.float32)
    nlo, nhi = 4 * i, 24 - 4 * i
    for kt in range(24):
        biascol[:, kt] = 0.0 if kt < nlo else NEG
        biascol[:, 24 + kt] = 0.0 if kt < nhi else NEG
    return {"xrows": xrows, "ktab": _rope_tab(pos, 1.0),
            "qtab": _rope_tab(pos[3072:], 0.125), "biascol": biascol}


ENGS = ("tensor", "scalar", "vector", "gpsimd", "sync")
QSEMS = ("qi", "qkv", "qt0", "qt1", "qx0", "qx1", "qq", "qwo0", "qwo1",
         "qxq0", "qxq1", "q13a", "q13b", "q2a", "q2b", "qo")
ALLSEMS = ENGS + QSEMS

class Bld:
    def __init__(self, me, eng, sems):
        self.me, self.eng, self.sems = me, eng, sems
        self.ct = {k: 0 for k in ALLSEMS}
        self.last_wait = {}

    def _wait(self, sem_name, val):
        if val <= 0 or self.last_wait.get(sem_name, -1) >= val:
            return
        self.last_wait[sem_name] = val
        self.eng.wait_ge(self.sems[sem_name], val)

    def op(self, eng_name, fn, waits=(), inc=True):
        if self.me == eng_name:
            for (s, v) in waits:
                self._wait(s, v)
            instr = fn(self.eng)
            if inc:
                instr.then_inc(self.sems[eng_name], 1)
        if inc:
            self.ct[eng_name] += 1
        return self.ct[eng_name]

    def dma(self, q_name, sem, out, in_, waits=()):
        if self.me == q_name:
            for (s, v) in waits:
                self._wait(s, v)
            self.eng.dma_start(out=out, in_=in_).then_inc(self.sems[sem], 16)
        self.ct[sem] += 16
        return self.ct[sem]

    def wait(self, eng_name, sem_name, val):
        if self.me == eng_name:
            self._wait(sem_name, val)

    def fence(self):
        return dict(self.ct)


def build_walk(T, C, b: Bld, phases=6):
    def dump_and_stop():
        f = b.fence()
        last = 0
        for name, src_ap in C.get("dumps", []):
            last = b.dma("gpsimd", "qo", C["dbg_aps"][name], src_ap,
                         waits=[(e_, f[e_]) for e_ in ALLSEMS if e_ != "qo"])
        b.wait("gpsimd", "qo", last)

    pb = T["pb"]
    def bank(ix):
        return pb[:, 512 * ix:512 * (ix + 1)]

    b.dma("gpsimd", "qi", T["biascol"][:], C["in"]["biascol"])
    b.dma("gpsimd", "qi", T["rms16land"][:], C["in"]["rms16"])
    b.dma("gpsimd", "qi", T["qtab"][:], C["in"]["qtab"])
    b.dma("gpsimd", "qi", T["wgateland"][:], C["in"]["wgate"])
    qi = b.dma("gpsimd", "qi", T["ident"][:], C["ident_dram"].ap())
    g_bias = g_rms = g_qtab = g_gate = g_sg = g_id = ("qi", qi)
    v_ones = b.op("vector", lambda e: e.memset(T["vst_ones"], 1.0))
    v_cst = b.op("vector", lambda e: e.memset(T["cst"][:, 0:1], 1e-11))
    v_cst = b.op("vector", lambda e: e.memset(T["cst"][:, 1:2], 1e-5))
    v_o164 = b.op("vector", lambda e: e.memset(T["ones164"][:], 1.0))
    v_o64c = b.op("vector", lambda e: e.memset(T["ones64c"][:], 1.0))
    v_gate = b.op("vector", lambda e: e.tensor_copy(out=T["wgateb"][:], in_=T["wgateland"][:]),
                  waits=[g_gate])
    rms_v = b.op("vector", lambda e: e.tensor_copy(out=T["rms16"][:], in_=T["rms16land"][:]),
                 waits=[g_rms])
    v_wkv = 0
    for k in range(16):
        q_kv = b.dma("sync", "qkv", T["kvland"][:], C["in"]["wkv"][k],
                     waits=([("vector", v_wkv)] if k else []))
        v_wkv = b.op("vector", lambda e, k=k: e.tensor_copy(
            out=T["wkvb"][:, 1024 * k:1024 * (k + 1)], in_=T["kvland"][:]),
            waits=[("qkv", q_kv)])

    if phases <= 0:
        dump_and_stop()
        return
    if phases in (0.2,):
        b.op("vector", lambda e: e.memset(T["kst"][:, :], 0.0))
        b.op("vector", lambda e: e.memset(T["vst"][:, :], 0.0))
    x_rd_v = [0, 0]
    x_rd_s = [0, 0]
    tab_rd = [0, 0]
    zbuf_rd_pe = 0
    prev_tev = 0
    last_rope = 0
    sc_v = 0

    def ln_tile(src_ap, src_waits, dst_fn, fold_rms, stats_ix, zb="zbuf"):
        nonlocal zbuf_rd_pe, prev_tev
        vb = b.op("vector", lambda e: e.bn_stats(out=T["bn24"][:, 0:6], in_=src_ap[:, 0:512]),
                  waits=src_waits)
        for j in range(1, 4):
            vb = b.op("vector", lambda e, j=j: e.bn_stats(
                out=T["bn24"][:, 6 * j:6 * (j + 1)], in_=src_ap[:, 512 * j:512 * (j + 1)]))
        b.wait("vector", "vector", vb)
        stat = T["stats"][:, 2 * stats_ix:2 * stats_ix + 2] if stats_ix is not None else T["stat2"][:]
        va = b.op("vector", lambda e: e.bn_aggr(out=stat, in_=T["bn24"][:]))
        sc_, bi_ = (1.000001, 0) if fold_rms else (1.0, 1)
        sa = b.op("scalar", lambda e: e.activation(out=T["t1"][:], in_=stat[:, 1:2], func=AF.Sqrt,
                                                   bias=T["cst"][:, bi_:bi_ + 1], scale=sc_),
                  waits=[("vector", va), ("vector", v_cst)])
        vr = b.op("vector", lambda e: e.reciprocal(out=T["scl"][:], in_=T["t1"][:]),
                  waits=[("scalar", sa)])
        b.wait("vector", "vector", vr)
        sz = b.op("vector", lambda e: e.tensor_scalar(
            out=T[zb][:], in0=src_ap, scalar1=stat[:, 0:1], scalar2=T["scl"][:],
            op0=OP.subtract, op1=OP.mult),
            waits=[("tensor", zbuf_rd_pe)])
        b.wait("vector", "vector", sz)
        ev = prev_tev
        for c in range(16):
            bk = 4 + (c % 4)
            tp = b.op("tensor", lambda e, c=c, bk=bk: e.transpose(
                bank(bk)[:, 0:128], T[zb][:, 128 * c:128 * (c + 1)], T["ident"][:]),
                waits=[("vector", sz), g_id, ("vector", ev)])
            dst, use_rms = dst_fn(c)
            if use_rms:
                ev = b.op("vector", lambda e, dst=dst, bk=bk, c=c: e.tensor_scalar_mul(
                    out=dst, in0=bank(bk)[:, 0:128], scalar1=T["rms16"][:, c:c + 1]),
                    waits=[("tensor", tp), ("vector", rms_v)])
            else:
                ev = b.op("vector", lambda e, dst=dst, bk=bk: e.tensor_copy(
                    out=dst, in_=bank(bk)[:, 0:128]), waits=[("tensor", tp)])
        zbuf_rd_pe = b.ct["tensor"]
        prev_tev = ev
        return ev, sz

    def rope_evac(psum_ap, tab, dst_top, dst_bot, mm_done, extra):
        w = [("tensor", mm_done)] + list(extra)
        b.op("vector", lambda e: e.tensor_mul(out=T["rtu"][:], in0=psum_ap[64:128, :], in1=tab[64:128, :]), waits=w)
        b.op("vector", lambda e: e.tensor_mul(out=T["rtv"][:], in0=psum_ap[0:64, :], in1=tab[64:128, :]))
        b.op("vector", lambda e: e.tensor_mul(out=T["rtw"][:], in0=psum_ap[0:64, :], in1=tab[0:64, :]))
        v4 = b.op("vector", lambda e: e.tensor_mul(out=T["rtz"][:], in0=psum_ap[64:128, :], in1=tab[0:64, :]))
        b.wait("vector", "vector", v4)
        b.op("vector", lambda e: e.tensor_sub(out=dst_top, in0=T["rtw"][:], in1=T["rtu"][:]))
        return b.op("vector", lambda e: e.tensor_add(out=dst_bot, in0=T["rtz"][:], in1=T["rtv"][:]))

    for tb in range(1 if phases == 0.2 else NTB):
        own = tb >= 6
        dst_zT = T["z_own"][tb - 6] if own else T["zT"]
        tp_ = tb % 2
        g_tab = ("qt%d" % tp_, b.dma("gpsimd", "qt%d" % tp_, T["tabbuf"][tp_][:],
                 C["in"]["ktab"][:, 512 * tb:512 * (tb + 1)], waits=[("vector", tab_rd[tp_])]))
        ev_last = 0
        for tt in range(4):
            p = (tb * 4 + tt) % 2
            g_x = ("qx%d" % p, b.dma("gpsimd", "qx%d" % p, T["xbuf"][p][:],
                   C["in"]["xrows"][512 * tb + 128 * tt:512 * tb + 128 * (tt + 1), :],
                   waits=[("vector", x_rd_v[p])]))
            def dst_fn(c, dst_zT=dst_zT, tt=tt):
                return dst_zT[:, 512 * c + 128 * tt:512 * c + 128 * (tt + 1)], True
            ev_last, sz = ln_tile(T["xbuf"][p][:], [g_x], dst_fn, True,
                                  (tb - 6) * 4 + tt if own else None)
            x_rd_v[p] = b.ct["vector"]
            x_rd_s[p] = sz
        for m in range(4):
            bk = m % 2
            pe_k = 0
            for k in range(16):
                pe_k = b.op("tensor", lambda e, m=m, k=k, bk=bk, dst_zT=dst_zT: e.matmul(
                    bank(bk), T["wkvb"][:, 1024 * k + 128 * m:1024 * k + 128 * (m + 1)],
                    dst_zT[:, 512 * k:512 * (k + 1)], start=(k == 0), stop=(k == 15)),
                    waits=([("vector", ev_last), ("vector", last_rope)] if k == 0 else []),
                    inc=(k == 15))
            cols = slice(4096 * m + 512 * tb, 4096 * m + 512 * (tb + 1))
            last_rope = rope_evac(bank(bk), T["tabbuf"][tp_], T["kst"][0:64, cols],
                                  T["kst"][64:128, cols], pe_k, [g_tab])
        tab_rd[tp_] = b.ct["vector"]
        for tt in range(4):
            bk = 2 + tt % 2
            pe_v = 0
            for k in range(16):
                pe_v = b.op("tensor", lambda e, tt=tt, k=k, bk=bk, dst_zT=dst_zT: e.matmul(
                    bank(bk), dst_zT[:, 512 * k + 128 * tt:512 * k + 128 * (tt + 1)],
                    T["wkvb"][:, 1024 * k + 512:1024 * (k + 1)], start=(k == 0), stop=(k == 15)),
                    waits=([("scalar", sc_v)] if k == 0 else []), inc=(k == 15))
            base = 520 * (tb * 4 + tt)
            for g in range(NG):
                sc_v = b.op("scalar", lambda e, g=g, base=base, bk=bk: e.copy(
                    out=T["vst"][:, base + 65 * g:base + 65 * g + 64],
                    in_=bank(bk)[:, 64 * g:64 * (g + 1)]),
                    waits=([("tensor", pe_v)] if g == 0 else []))

    if phases in (0.2, 0.7):
        dump_and_stop()
        return
    fQ = b.fence()
    qland_rd = 0
    wqb_rd = [0, 0]
    for m in range(16):
        mp = m % 2
        q_q = b.dma("sync", "qq", T["qland"][:], C["in"]["wq"][m],
                    waits=[("vector", max(qland_rd, fQ["vector"])), ("scalar", fQ["scalar"])])
        qcv = b.op("vector", lambda e, mp=mp: e.tensor_copy(out=T["wqb"][mp][:], in_=T["qland"][:]),
                   waits=[("qq", q_q), ("tensor", wqb_rd[mp]), ("scalar", fQ["scalar"])])
        qland_rd = qcv
        for tbq in range(2):
            bk = tbq
            pe_q = 0
            for k in range(16):
                pe_q = b.op("tensor", lambda e, k=k, tbq=tbq, bk=bk, mp=mp: e.matmul(
                    bank(bk), T["wqb"][mp][:, 128 * k:128 * (k + 1)],
                    T["z_own"][tbq][:, 512 * k:512 * (k + 1)], start=(k == 0), stop=(k == 15)),
                    waits=([("vector", qcv), ("vector", last_rope)] if k == 0 else []), inc=(k == 15))
            cols = slice(1024 * m + 512 * tbq, 1024 * m + 512 * (tbq + 1))
            last_rope = rope_evac(bank(bk), T["qtab"][:, 512 * tbq:512 * (tbq + 1)],
                                  T["qot"][0:64, cols], T["qot"][64:128, cols], pe_q,
                                  [g_qtab])
        wqb_rd[mp] = b.ct["tensor"]

    if phases <= 1:
        dump_and_stop()
        return
    fP2 = b.fence()
    v_kones = b.op("vector", lambda e: e.memset(T["ktil"][64:65, :], 1.0),
                   waits=[("tensor", fP2["tensor"])])
    last_av = 0
    act_exp = 0
    osb_wr = 0
    last_vevac = last_rope
    pool_r1 = 0
    s_kq = 0
    sc_p1 = fP2["scalar"]
    for g in range(NG):
        mk, half = g // 2, g % 2
        kcols = slice(4096 * mk, 4096 * (mk + 1))
        sv = b.op("vector", lambda e, half=half, kcols=kcols: e.tensor_copy(
            out=T["ktil"][0:32, :], in_=T["kst"][32 * half:32 * half + 32, kcols]),
            waits=[("vector", last_vevac), ("tensor", max(last_av, fP2["tensor"])),
                   ("gpsimd", pool_r1), ("scalar", s_kq)])
        sv = b.op("vector", lambda e, half=half, kcols=kcols: e.tensor_copy(
            out=T["ktil"][32:64, :], in_=T["kst"][64 + 32 * half:96 + 32 * half, kcols]))
        for h in range(HPG):
            hh = 4 * g + h
            c, sub = hh // 2, hh % 2
            qc = slice(1024 * c, 1024 * (c + 1))
            sv = b.op("vector", lambda e, h=h, sub=sub, qc=qc: e.tensor_copy(
                out=T["qtil"][0:32, 1024 * h:1024 * (h + 1)], in_=T["qot"][32 * sub:32 * sub + 32, qc]))
            sv = b.op("vector", lambda e, h=h, sub=sub, qc=qc: e.tensor_copy(
                out=T["qtil"][32:64, 1024 * h:1024 * (h + 1)], in_=T["qot"][64 + 32 * sub:96 + 32 * sub, qc]))
        b.wait("vector", "vector", sv)
        v1 = b.op("vector", lambda e: e.tensor_mul(out=T["sqs"][:], in0=T["ktil"][0:64, :], in1=T["ktil"][0:64, :]))
        b.wait("vector", "vector", v1)
        km_prev = 0
        for half in range(2):
            mmn = 0
            for c4 in range(4):
                mmn = b.op("tensor", lambda e, half=half, c4=c4: e.matmul(
                    pb[0:1, 512 * c4:512 * (c4 + 1)], T["ones64c"][:],
                    T["sqs"][:, 2048 * half + 512 * c4:2048 * half + 512 * (c4 + 1)],
                    start=True, stop=True),
                    waits=([("vector", max(v1, v_o64c, km_prev)), ("scalar", max(act_exp, s_kq))]
                           if c4 == 0 else []), inc=(c4 == 3))
            km_prev = b.op("vector", lambda e, half=half: e.reduce_max(
                out=T["km2"][half][:], in_=pb[0:1, 0:2048], axis=AX.X), waits=[("tensor", mmn)])
        b.wait("vector", "vector", km_prev)
        v2 = b.op("vector", lambda e: e.tensor_max(out=T["km2"][0][:], in0=T["km2"][0][:], in1=T["km2"][1][:]))
        s1 = b.op("scalar", lambda e: e.activation(out=T["km"][:], in_=T["km2"][0][:], func=AF.Sqrt),
                  waits=[("vector", v2)])
        v3 = b.op("vector", lambda e: e.tensor_scalar_mul(out=T["kms"][:], in0=T["km"][:], scalar1=-BOUND_MARGIN),
                  waits=[("scalar", s1)])
        v4 = b.op("vector", lambda e: e.tensor_mul(out=T["sqs"][:], in0=T["qtil"][0:64, :], in1=T["qtil"][0:64, :]),
                  waits=[("vector", km_prev)])
        b.wait("vector", "vector", v4)
        s_kq = 0
        for half in range(2):
            mmn = 0
            for c4 in range(4):
                mmn = b.op("tensor", lambda e, half=half, c4=c4: e.matmul(
                    pb[0:1, 512 * c4:512 * (c4 + 1)], T["ones64c"][:],
                    T["sqs"][:, 2048 * half + 512 * c4:2048 * half + 512 * (c4 + 1)],
                    start=True, stop=True),
                    waits=([("vector", v4 if half == 0 else 0), ("scalar", s_kq)] if c4 == 0 else []),
                    inc=(c4 == 3))
            s_kq = b.op("scalar", lambda e, half=half: e.activation(
                out=T["qtil"][64:65, 2048 * half:2048 * (half + 1)], in_=pb[0:1, 0:2048], func=AF.Sqrt),
                waits=[("tensor", mmn), ("vector", sv)])
        vb_ = b.op("vector", lambda e: e.tensor_scalar_mul(out=T["qtil"][64:65, :], in0=T["qtil"][64:65, :],
                                                           scalar1=T["kms"][:]),
                   waits=[("scalar", s_kq), ("vector", v3)])
        bound_ready = vb_
        pool_r1 = 0

        for qb in range(2):
            tiles = [("o", kt) for kt in range(12 if qb == 0 else 24)] + \
                    [("s", j) for j in range(4 if qb == 0 else 8)]
            for idx, (kind, kt) in enumerate(tiles):
                if kind == "o":
                    krows = slice(128 * kt, 128 * (kt + 1))
                    bias_ap = T["biascol"][:, 24 * qb + kt:24 * qb + kt + 1]
                    vtile = kt
                else:
                    krows = slice(128 * (24 + kt), 128 * (25 + kt))
                    bias_ap = T["biascol"][:, 48:49]
                    vtile = 24 + kt
                pp = idx % 2
                qk = 0
                for h in range(HPG):
                    qk = b.op("tensor", lambda e, h=h, krows=krows, qb=qb: e.matmul(
                        bank(h), T["ktil"][:, krows],
                        T["qtil"][:, 1024 * h + 512 * qb:1024 * h + 512 * (qb + 1)],
                        start=True, stop=True),
                        waits=([("vector", max(bound_ready, last_vevac)), ("scalar", max(act_exp, osb_wr, s_kq))]
                               if h == 0 else []), inc=(h == 3))
                act_exp = b.op("scalar", lambda e, pp=pp, bias_ap=bias_ap: e.activation(
                    out=T["pbuf"][pp][:], in_=pb[:, 0:2048], func=AF.Exp, bias=bias_ap, scale=1.0),
                    waits=[("tensor", max(qk, last_av)), g_bias])
                p_ready = [("scalar", act_exp)]
                if kind == "s" and not (qb == 1 and kt < 4):
                    pm = b.op("gpsimd", lambda e, pp=pp, qb=qb, kt=kt: e.affine_select(
                        out=T["pbuf"][pp][:], in_=T["pbuf"][pp][:], pattern=[[0, 4], [1, 512]],
                        compare_op=OP.is_ge, fill=0.0, base=512 * qb - 128 * kt, channel_multiplier=-1),
                        waits=[("scalar", act_exp)])
                    p_ready = [("gpsimd", pm)]
                for h in range(HPG):
                    last_av = b.op("tensor", lambda e, h=h, vtile=vtile, g=g, pp=pp, idx=idx, ntile=len(tiles): e.matmul(
                        pb[0:65, 2048 + 512 * h:2048 + 512 * (h + 1)],
                        T["vst"][:, 520 * vtile + 65 * g:520 * vtile + 65 * (g + 1)],
                        T["pbuf"][pp][:, 512 * h:512 * (h + 1)], start=(idx == 0), stop=(idx == ntile - 1)),
                        waits=((p_ready + [("vector", v_ones), ("scalar", max(osb_wr, sc_p1))])
                               if h == 0 else []), inc=(h == 3))
            so = b.op("scalar", lambda e: e.copy(out=T["osb"][:], in_=pb[0:65, 2048:4096]),
                      waits=[("tensor", last_av), ("vector", last_vevac), ("gpsimd", pool_r1)])
            osb_wr = so
            vrc = b.op("vector", lambda e: e.reciprocal(out=T["rrow32"][:], in_=T["osb"][64:65, :]),
                       waits=[("scalar", so)])
            b.wait("vector", "vector", vrc)
            vrc = b.op("vector", lambda e: e.tensor_copy(out=T["rrow"][:], in_=T["rrow32"][:]))
            bc = 0
            for h in range(HPG):
                bc = b.op("tensor", lambda e, h=h: e.matmul(
                    pb[0:64, 512 * h:512 * (h + 1)], T["ones164"][:],
                    T["rrow"][:, 512 * h:512 * (h + 1)], start=True, stop=True),
                    waits=([("vector", max(vrc, v_o164)), ("scalar", act_exp)] if h == 0 else []),
                    inc=(h == 3))
            for h in range(HPG):
                hh = 4 * g + h
                c, sub = hh // 2, hh % 2
                last_vevac = b.op("vector", lambda e, h=h, c=c, sub=sub, qb=qb: e.tensor_mul(
                    out=T["qot"][64 * sub:64 * (sub + 1), 1024 * c + 512 * qb:1024 * c + 512 * (qb + 1)],
                    in0=T["osb"][0:64, 512 * h:512 * (h + 1)], in1=pb[0:64, 512 * h:512 * (h + 1)]),
                    waits=[("tensor", bc), ("scalar", so)])

    if phases <= 2:
        dump_and_stop()
        return
    fP3 = b.fence()
    acc = T["acc"].ap()
    xq_rd_v = [fP3["vector"], fP3["vector"]]
    xq_rd_s = [0, 0]
    acc_ev = 0
    ybuf_rd = 0
    for tt in range(8):
        p = tt % 2
        g_xq = b.dma("gpsimd", "qxq%d" % p, T["xq"][p][:],
                     C["in"]["xrows"][3072 + 128 * tt:3072 + 128 * (tt + 1), :],
                     waits=[("vector", xq_rd_v[p]), ("tensor", fP3["tensor"])])
        sa = b.op("scalar", lambda e, tt=tt: e.activation(
            out=T["t1"][:], in_=T["stats"][:, 2 * tt + 1:2 * tt + 2], func=AF.Sqrt,
            bias=T["cst"][:, 1:2], scale=1.0),
            waits=[("vector", acc_ev)])
        vr = b.op("vector", lambda e: e.reciprocal(out=T["scl"][:], in_=T["t1"][:]),
                  waits=[("scalar", sa), ("vector", fP3["vector"])])
        b.wait("vector", "vector", vr)
        sy = b.op("vector", lambda e, p=p, tt=tt: e.tensor_scalar(
            out=T["ybuf"][:], in0=T["xq"][p][:], scalar1=T["stats"][:, 2 * tt:2 * tt + 1],
            scalar2=T["scl"][:], op0=OP.subtract, op1=OP.mult),
            waits=[("vector", ybuf_rd), ("qxq%d" % p, g_xq)])
        b.wait("vector", "vector", sy)
        acc_ev = b.op("vector", lambda e, p=p, tt=tt: e.tensor_add(
            out=acc[:, 2048 * tt:2048 * (tt + 1)], in0=T["xq"][p][:], in1=T["ybuf"][:]),
            waits=[("tensor", fP3["tensor"]), ("gpsimd", fP3["gpsimd"])])
        ybuf_rd = acc_ev
        xq_rd_v[p] = acc_ev
        xq_rd_s[p] = 0
    wol_rd = [0, 0]
    wob_rd_pe = 0
    for n in range(4):
        cvt = 0
        for k in range(16):
            kp = k % 2
            q_wo = b.dma("sync", "qwo%d" % kp, T["woland"][kp][:], C["in"]["wout"][n, k],
                         waits=[("vector", max(wol_rd[kp], fP3["vector"])), ("tensor", fP3["tensor"])])
            cvt = b.op("vector", lambda e, k=k, kp=kp: e.tensor_copy(
                out=T["wob"][:, 512 * k:512 * (k + 1)], in_=T["woland"][kp][:]),
                waits=[("qwo%d" % kp, q_wo), ("tensor", wob_rd_pe)])
            wol_rd[kp] = cvt
        for th in range(2):
            pe_wo = 0
            for k in range(16):
                for t4 in range(4):
                    tt = 4 * th + t4
                    pe_wo = b.op("tensor", lambda e, k=k, tt=tt, t4=t4: e.matmul(
                        bank(t4), T["qot"][:, 1024 * k + 128 * tt:1024 * k + 128 * (tt + 1)],
                        T["wob"][:, 512 * k:512 * (k + 1)], start=(k == 0), stop=(k == 15)),
                        waits=([("vector", max(cvt, acc_ev, last_vevac))] if (k == 0 and t4 == 0) else []),
                        inc=(k == 15 and t4 == 3))
            for t4 in range(4):
                tt = 4 * th + t4
                sl = slice(2048 * tt + 512 * n, 2048 * tt + 512 * (n + 1))
                acc_ev = b.op("vector", lambda e, t4=t4, sl=sl: e.tensor_add(
                    out=acc[:, sl], in0=bank(t4), in1=acc[:, sl]),
                    waits=[("tensor", pe_wo), ("vector", acc_ev)])
        wob_rd_pe = b.ct["tensor"]
    x1_ev = 0
    for tt in range(8):
        src = acc[:, 2048 * tt:2048 * (tt + 1)]
        def dst2(c, tt=tt):
            return T["x1nT"][:, 1024 * c + 128 * tt:1024 * c + 128 * (tt + 1)], False
        x1_ev, _ = ln_tile(src, [("vector", acc_ev)], dst2, False, None, zb="zbuf2")
    x1nT_done = x1_ev

    if phases <= 3:
        dump_and_stop()
        return
    se_rd = 0
    vm_rd = 0
    sg_rd = 0
    coef_done = 0
    sig_done = 0
    for tt in range(8):
        pe_r = 0
        for k in range(16):
            pe_r = b.op("tensor", lambda e, k=k, tt=tt: e.matmul(
                pb[:, 0:17], T["x1nT"][:, 1024 * k + 128 * tt:1024 * k + 128 * (tt + 1)],
                T["wgateb"][:, 17 * k:17 * (k + 1)], start=(k == 0), stop=(k == 15)),
                waits=([("vector", max(x1nT_done, v_gate, vm_rd)), ("scalar", max(se_rd, sg_rd))] if k == 0 else []),
                inc=(k == 15))
        vm = b.op("vector", lambda e: e.reduce_max(out=T["t1"][:], in_=pb[:, 0:16], axis=AX.X, negate=True),
                  waits=[("tensor", pe_r)])
        se = b.op("scalar", lambda e: e.activation(out=T["probs"][:], in_=pb[:, 0:16], func=AF.Exp,
                                                   bias=T["t1"][:], scale=1.0),
                  waits=[("vector", max(vm, coef_done))])
        se_rd = se
        vs = b.op("vector", lambda e: e.reduce_sum(out=T["t2"][:], in_=T["probs"][:], axis=AX.X),
                  waits=[("scalar", se)])
        b.wait("vector", "vector", vs)
        vrc2 = b.op("vector", lambda e: e.reciprocal(out=T["t3"][:], in_=T["t2"][:]))
        b.wait("vector", "vector", vrc2)
        vn = b.op("vector", lambda e: e.tensor_scalar_mul(out=T["probs"][:], in0=T["probs"][:], scalar1=T["t3"][:]))
        b.wait("vector", "vector", vn)
        v8 = b.op("vector", lambda e: e.max(out=T["top8"][:], in_=T["probs"][:]))
        b.wait("vector", "vector", v8)
        coef_done = b.op("vector", lambda e, tt=tt: e.scalar_tensor_tensor(
            out=T["coef"][:, 16 * tt:16 * (tt + 1)], in0=T["probs"][:], scalar=T["top8"][:, 3:4],
            in1=T["probs"][:], op0=OP.is_ge, op1=OP.mult))
        vm_rd = coef_done
        sig_done = b.op("scalar", lambda e, tt=tt: e.activation(out=T["sig"][:, tt:tt + 1], in_=pb[:, 16:17],
                                                                func=AF.Sigmoid), waits=[("tensor", pe_r)])
        sg_rd = sig_done

    if phases <= 4:
        dump_and_stop()
        return
    _P5_CUT = (phases == 4.5)
    fP5 = b.fence()
    w13l_rd = [0, 0]
    w2l_rd = [0, 0]
    w13b_rd = [0, 0]
    w2b_rd = [0, 0]
    hT_rd_pe = 0
    gs_wr = 0
    gs_rd_v = 0
    h_wr = 0
    b.wait("tensor", "scalar", sig_done)
    b.wait("tensor", "vector", coef_done)
    for e_ix in range({4.5: 2, 4.6: 8, 4.7: 14, 4.8: 17, 4.85: 18, 4.9: 19}.get(phases, NE)):
        for jm in range(22):
            mp = jm % 2
            cvt = 0
            for pc in range(2):
                kp = pc % 2
                q13 = b.dma("sync", "q13a" if kp == 0 else "q13b", T["w13land"][kp][:],
                            C["in"]["w13"][e_ix, jm, :, 1024 * pc:1024 * (pc + 1)],
                            waits=[("vector", max(w13l_rd[kp], fP5["vector"])),
                                   ("scalar", fP5["scalar"]), ("tensor", fP5["tensor"])])
                cvt = b.op("vector", lambda e, pc=pc, kp=kp, mp=mp: e.tensor_copy(
                    out=T["w13b"][mp][:, 1024 * pc:1024 * (pc + 1)], in_=T["w13land"][kp][:]),
                    waits=[("q13a" if kp == 0 else "q13b", q13), ("tensor", w13b_rd[mp])])
                w13l_rd[kp] = cvt
            is_g = (jm % 2 == 0)
            j = jm // 2
            pe_m = 0
            for tb in range(2):
                bk = (0 if is_g else 2) + tb
                for k in range(16):
                    pe_m = b.op("tensor", lambda e, k=k, tb=tb, bk=bk, mp=mp: e.matmul(
                        bank(bk), T["w13b"][mp][:, 128 * k:128 * (k + 1)],
                        T["x1nT"][:, 1024 * k + 512 * tb:1024 * k + 512 * (tb + 1)],
                        start=(k == 0), stop=(k == 15)),
                        waits=(([("vector", cvt)] + ([("scalar", gs_wr)] if is_g else [("vector", h_wr)]))
                               if (k == 0 and tb == 0) else []), inc=(k == 15))
            w13b_rd[mp] = pe_m
            if is_g:
                for tb in range(2):
                    gs_wr = b.op("scalar", lambda e, tb=tb: e.activation(
                        out=T["gs"][:, 512 * tb:512 * (tb + 1)], in_=bank(tb), func=AF.Silu),
                        waits=[("tensor", pe_m), ("vector", gs_rd_v)] if tb == 0 else [("tensor", pe_m)])
            else:
                for tb in range(2):
                    h_wr = b.op("vector", lambda e, tb=tb, j=j: e.tensor_mul(
                        out=T["hT"][:, 1024 * j + 512 * tb:1024 * j + 512 * (tb + 1)],
                        in0=T["gs"][:, 512 * tb:512 * (tb + 1)], in1=bank(2 + tb)),
                        waits=[("scalar", gs_wr), ("tensor", max(pe_m, hT_rd_pe))])
                gs_rd_v = h_wr
        acc_ev_l = acc_ev
        for n in range(4):
            np_ = n % 2
            w2cv = 0
            for k in range(4):
                kp = k % 2
                q2 = b.dma("gpsimd", "q2a" if kp == 0 else "q2b", T["w2land"][kp][:],
                           C["in"]["w2"][e_ix, n, :, 1408 * k:1408 * (k + 1)],
                           waits=[("vector", max(w2l_rd[kp], fP5["vector"])),
                                  ("tensor", fP5["tensor"]), ("scalar", fP5["scalar"])])
                w2cv = b.op("vector", lambda e, k=k, kp=kp, np_=np_: e.tensor_copy(
                    out=T["w2b"][np_][:, 1408 * k:1408 * (k + 1)], in_=T["w2land"][kp][:]),
                    waits=[("q2a" if kp == 0 else "q2b", q2), ("tensor", w2b_rd[np_])])
                w2l_rd[kp] = w2cv
            for th in range(2):
                pe_d = 0
                for k in range(11):
                    for t4 in range(4):
                        tt = 4 * th + t4
                        pe_d = b.op("tensor", lambda e, k=k, tt=tt, t4=t4, np_=np_: e.matmul(
                            bank(4 + t4), T["hT"][:, 1024 * k + 128 * tt:1024 * k + 128 * (tt + 1)],
                            T["w2b"][np_][:, 512 * k:512 * (k + 1)], start=(k == 0), stop=(k == 10)),
                            waits=([("vector", max(w2cv, h_wr, acc_ev))] if (k == 0 and t4 == 0) else []),
                            inc=(k == 10 and t4 == 3))
                for t4 in range(4):
                    tt = 4 * th + t4
                    if e_ix < 4:
                        coef_ap = T["sig"][:, tt:tt + 1]
                    else:
                        coef_ap = T["coef"][:, 16 * tt + (e_ix - 4):16 * tt + (e_ix - 4) + 1]
                    sl = slice(2048 * tt + 512 * n, 2048 * tt + 512 * (n + 1))
                    acc_ev = b.op("vector", lambda e, t4=t4, sl=sl, coef_ap=coef_ap: e.scalar_tensor_tensor(
                        out=acc[:, sl], in0=bank(4 + t4), scalar=coef_ap, in1=acc[:, sl],
                        op0=OP.mult, op1=OP.add),
                        waits=[("tensor", pe_d), ("vector", max(acc_ev, coef_done)),
                               ("scalar", sig_done)])
            w2b_rd[np_] = pe_d
        hT_rd_pe = b.ct["tensor"]
    if phases in (4.5, 4.6, 4.7, 4.8, 4.85, 4.9):
        dump_and_stop()
        return

    qo = 0
    for tt in range(8):
        qo = b.dma("gpsimd", "qo", C["out_ap"][128 * tt:128 * (tt + 1), :], acc[:, 2048 * tt:2048 * (tt + 1)],
                   waits=[("vector", acc_ev)])
    fE = b.fence()
    for s_ in QSEMS:
        b.wait("gpsimd", s_, fE[s_])



def build_program(phases=6, dumps=()):
    nc = bass.Bass("TRN2", target_bir_lowering=False)
    CIN = {}
    CIN["xrows"] = nc.dram_tensor("xrows", [S, D], F32, kind="ExternalInput").ap()
    CIN["ktab"] = nc.dram_tensor("ktab", [128, S], F32, kind="ExternalInput").ap()
    CIN["qtab"] = nc.dram_tensor("qtab", [128, OWN], F32, kind="ExternalInput").ap()
    CIN["biascol"] = nc.dram_tensor("biascol", [128, 49], F32, kind="ExternalInput").ap()
    CIN["wkv"] = nc.dram_tensor("wkv", [16, 128, 1024], F32, kind="ExternalInput").ap()
    CIN["wq"] = nc.dram_tensor("wq", [16, 128, 2048], F32, kind="ExternalInput").ap()
    CIN["wout"] = nc.dram_tensor("wout", [4, 16, 128, 512], F32, kind="ExternalInput").ap()
    CIN["rms16"] = nc.dram_tensor("rms16", [128, 16], F32, kind="ExternalInput").ap()
    CIN["wgate"] = nc.dram_tensor("wgate", [128, 272], F32, kind="ExternalInput").ap()
    CIN["w13"] = nc.dram_tensor("w13", [NE, 22, 128, 2048], F32, kind="ExternalInput").ap()
    CIN["w2"] = nc.dram_tensor("w2", [NE, 4, 128, 5632], F32, kind="ExternalInput").ap()
    out_ap = nc.dram_tensor("out", [OWN, D], F32, kind="ExternalOutput").ap()
    ident_dram = nc.inline_tensor(np.eye(128, dtype=np.float32), "identf32")

    SBASE = 16512
    def at(name, shape, dtype, off):
        return nc.alloc_sbuf_tensor_at(name, shape, dtype, offset=SBASE + off)

    T = {}
    EB, FB, MB = 98816, 164352, 197120
    T["kst"] = at("kst", [128, 16384], BF16, 0).ap()
    T["vst"] = at("vst", [128, 16640], BF16, 32768)
    T["qot"] = at("qot", [128, 16384], BF16, 66048).ap()
    T["acc"] = at("acc", [128, 16384], F32, EB)
    T["x1nT"] = at("x1nT", [128, 16384], BF16, FB).ap()
    T["z_own"] = [at("z_own0", [128, 8192], BF16, EB).ap(), at("z_own1", [128, 8192], BF16, EB + 16384).ap()]
    T["rtu"] = at("rtu", [64, 512], F32, EB + 32768)
    T["rtv"] = at("rtv", [64, 512], F32, EB + 34816)
    T["rtw"] = at("rtw", [64, 512], F32, EB + 36864)
    T["rtz"] = at("rtz", [64, 512], F32, EB + 38912)
    T["xbuf"] = [at("xbuf0", [128, 2048], F32, EB + 40960), at("xbuf1", [128, 2048], F32, EB + 49152)]
    T["qland"] = at("qland", [128, 2048], F32, EB + 40960)
    T["wqb"] = [at("wqb0", [128, 2048], BF16, EB + 49152), at("wqb1", [128, 2048], BF16, EB + 53248)]
    T["zT"] = at("zT", [128, 8192], BF16, FB).ap()
    T["zbuf"] = at("zbuf", [128, 2048], F32, FB + 16384)
    T["kvland"] = at("kvland", [128, 1024], F32, FB + 24576)
    T["tabbuf"] = [at("tabbuf0", [128, 512], F32, FB + 28672), at("tabbuf1", [128, 512], F32, FB + 30720)]
    T["wkvb"] = at("wkvb", [128, 16384], BF16, 66048).ap()
    T["qtil"] = at("qtil", [65, 4096], BF16, EB).ap()
    T["ktil"] = at("ktil", [65, 4096], BF16, EB + 8192).ap()
    T["sqs"] = at("sqs", [64, 4096], BF16, EB + 16384)
    T["r1"] = at("r1", [1, 4096], F32, EB + 24576)
    T["pbuf"] = [at("pbuf0", [128, 2048], BF16, EB + 40960), at("pbuf1", [128, 2048], BF16, EB + 45056)]
    T["osb"] = at("osb", [65, 2048], F32, EB + 16384).ap()
    T["rrow"] = at("rrow", [1, 2048], BF16, EB + 49152)
    T["rrow32"] = at("rrow32", [1, 2048], F32, EB + 53248)
    T["xq"] = [at("xq0", [128, 2048], F32, 0), at("xq1", [128, 2048], F32, 8192)]
    T["wob"] = at("wob", [128, 8192], BF16, 16384)
    T["woland"] = [at("woland0", [128, 512], F32, 32768), at("woland1", [128, 512], F32, 34816)]
    T["ybuf"] = at("ybuf", [128, 2048], F32, 36864)
    T["zbuf2"] = at("zbuf2", [128, 2048], F32, 45056)
    T["w13land"] = [at("w13land0", [128, 1024], F32, 0), at("w13land1", [128, 1024], F32, 4096)]
    T["w13b"] = [at("w13b0", [128, 2048], BF16, 8192), at("w13b1", [128, 2048], BF16, 12288)]
    T["w2land"] = [at("w2land0", [128, 1408], F32, 16384), at("w2land1", [128, 1408], F32, 22016)]
    T["w2b"] = [at("w2b0", [128, 5632], BF16, 27648), at("w2b1", [128, 5632], BF16, 38912)]
    T["hT"] = at("hT", [128, 11264], BF16, 50176).ap()
    T["gs"] = at("gs", [128, 1024], F32, 72704)
    off = [MB]
    def misc(name, shape, dtype):
        t = at(name, shape, dtype, off[0])
        nbytes = int(np.prod(shape[1:])) * mybir.dt.size(dtype)
        off[0] += (nbytes + 31) // 32 * 32
        return t
    T["ident"] = misc("ident", [128, 128], F32)
    T["biascol"] = misc("biascolS", [128, 49], F32)
    T["rms16land"] = misc("rms16land", [128, 16], F32)
    T["rms16"] = misc("rms16S", [128, 16], F32)
    T["stats"] = misc("stats", [128, 16], F32)
    T["stat2"] = misc("stat2", [128, 2], F32)
    T["bn24"] = misc("bn24", [128, 24], F32)
    T["t1"] = misc("t1", [128, 1], F32)
    T["scl"] = misc("scl", [128, 1], F32)
    T["bia"] = misc("bia", [128, 1], F32)
    T["t2"] = misc("t2", [128, 1], F32)
    T["t3"] = misc("t3", [128, 1], F32)
    T["km2"] = [misc("km2a", [1, 1], F32), misc("km2b", [1, 1], F32)]
    T["km"] = misc("km", [1, 1], F32)
    T["kms"] = misc("kms", [1, 1], F32)
    T["qtab"] = misc("qtabS", [128, 1024], F32)
    T["wgateland"] = misc("wgateland", [128, 272], F32)
    T["wgateb"] = misc("wgateb", [128, 272], BF16)
    T["probs"] = misc("probs", [128, 16], F32)
    T["top8"] = misc("top8", [128, 8], F32)
    T["coef"] = misc("coef", [128, 128], F32)
    T["sig"] = misc("sig", [128, 8], F32)
    T["ones164"] = misc("ones164", [1, 64], BF16)
    T["ones64c"] = misc("ones64c", [64, 1], BF16)
    T["cst"] = misc("cst", [128, 2], F32)
    assert off[0] <= 212700, f"misc overflow: {off[0]}"
    vst_h = T["vst"]
    T["vst"] = vst_h.ap()
    T["vst_ones"] = bass.AP(vst_h, 64, [[16640, 128], [520, 32], [65, 8], [1, 1]])
    T["acc"] = T["acc"]
    T["pb"] = nc.alloc_psum_tensor("pb", [128, 4096], F32).ap()

    C = {"in": CIN, "out_ap": out_ap, "ident_dram": ident_dram}
    dump_specs = {
        "kst": (T["kst"][:, :], [128, 16384], BF16),
        "vst": (T["vst"][:, :], [128, 16640], BF16),
        "qot": (T["qot"][:, :], [128, 16384], BF16),
        "zT0": (T["z_own"][0][:, :], [128, 8192], BF16),
        "zT1": (T["z_own"][1][:, :], [128, 8192], BF16),
        "wkvb": (T["wkvb"][:, :], [128, 16384], BF16),
        "acc": (T["acc"][:], [128, 16384], F32),
        "x1nT": (T["x1nT"][:, :], [128, 16384], BF16),
        "coef": (T["coef"][:], [128, 128], F32),
        "sig": (T["sig"][:], [128, 8], F32),
        "stats": (T["stats"][:], [128, 16], F32),
        "hT": (T["hT"][:, :], [128, 11264], BF16),
    }
    C["dumps"] = []
    C["dbg_aps"] = {}
    for name in dumps:
        ap_, shp, dt = dump_specs[name]
        C["dbg_aps"][name] = nc.dram_tensor("dbg_" + name, shp, dt, kind="ExternalOutput").ap()
        C["dumps"].append((name, ap_))

    import contextlib
    with contextlib.ExitStack() as st:
        sems = {k: st.enter_context(nc.semaphore("sem_" + k)) for k in ALLSEMS}
        block = st.enter_context(nc.Block())

        @block.tensor
        def _(eng):
            build_walk(T, C, Bld("tensor", eng, sems), phases)

        @block.scalar
        def _(eng):
            build_walk(T, C, Bld("scalar", eng, sems), phases)

        @block.vector
        def _(eng):
            build_walk(T, C, Bld("vector", eng, sems), phases)

        @block.gpsimd
        def _(eng):
            build_walk(T, C, Bld("gpsimd", eng, sems), phases)

        @block.sync
        def _(eng):
            build_walk(T, C, Bld("sync", eng, sems), phases)

    return nc


_PROG_CACHE = {}

def kernel(**inputs):
    shared = _shared_layouts(inputs)
    if "nc" not in _PROG_CACHE:
        _PROG_CACHE["nc"] = build_program()
    nc = _PROG_CACHE["nc"]
    in_maps = []
    for core in range(NCORES):
        m = dict(shared)
        m.update(_per_core(inputs, core))
        in_maps.append(m)
    res = run_bass_kernel_spmd(nc, in_maps, core_ids=list(range(NCORES)))
    out = np.zeros((BATCH, S, D), np.float32)
    for core in range(NCORES):
        bb, i = core // 4, core % 4
        o = res.results[core]["out"]
        out[bb, i * 512:(i + 1) * 512] = o[0:512]
        out[bb, (7 - i) * 512:(8 - i) * 512] = o[512:1024]
    return out


# version 05
# speedup vs baseline: 1.7226x; 1.0698x over previous
import numpy as np
import ml_dtypes
import concourse.bass as bass
import concourse.mybir as mybir
from concourse.bass_utils import run_bass_kernel_spmd
from concourse.alu_op_type import AluOpType as OP

F32 = mybir.dt.float32
BF16 = mybir.dt.bfloat16
AF = mybir.ActivationFunctionType
AX = mybir.AxisListType

NCORES = 8
D = 2048
S = 4096
BATCH = 2
NG, HPG, DH = 8, 4, 64
NE_REAL = 16
I_EXP, I_SH = 1408, 5632
NE = 20
OWN = 1024
NTB = 8
NEG = -1e30
BOUND_MARGIN = 1.05


def _mblocks_2816(W):
    A = W.reshape(22, 128, 16, 128).transpose(0, 3, 2, 1).reshape(22, 128, 2048)
    order = []
    for j in range(11):
        order += [j, 11 + j]
    return np.ascontiguousarray(A[order])

def _nblocks_1408(W2):
    return np.ascontiguousarray(W2.reshape(4, 512, 11, 128).transpose(0, 3, 2, 1).reshape(4, 128, 5632))

def _shared_layouts(inp):
    out = {}
    w_in = np.asarray(inp["L0_attn_w_in"], np.float32)
    permk = []
    for m in range(4):
        for half in (0, 1):
            for g in (2 * m, 2 * m + 1):
                base = 2048 + g * 64 + 32 * half
                permk += list(range(base, base + 32))
    WK = w_in[permk]
    WV = w_in[2560:3072]
    out["wkv"] = np.ascontiguousarray(np.concatenate(
        [WK.reshape(512, 16, 128).transpose(1, 2, 0),
         WV.reshape(512, 16, 128).transpose(1, 2, 0)], axis=2))
    permq = []
    for m in range(16):
        for half in (0, 1):
            for h in (2 * m, 2 * m + 1):
                base = h * 64 + 32 * half
                permq += list(range(base, base + 32))
    WQ = w_in[permq]
    out["wq"] = np.ascontiguousarray(WQ.reshape(16, 128, 16, 128).transpose(0, 3, 2, 1).reshape(16, 128, 2048))
    w_out = np.asarray(inp["L0_attn_w_out"], np.float32)
    out["wout"] = np.ascontiguousarray(w_out.reshape(4, 512, 16, 128).transpose(0, 2, 3, 1))
    out["rms16"] = np.ascontiguousarray(np.asarray(inp["L0_attn_rms_w"], np.float32).reshape(16, 128).T)
    gate = np.asarray(inp["L0_ffn_gate_w"], np.float32)
    g16 = gate.T.reshape(16, 128, 16)
    sg1 = np.asarray(inp["L0_ffn_sgate_w"], np.float32)[0].reshape(16, 128, 1)
    out["wgate"] = np.ascontiguousarray(np.concatenate([g16, sg1], axis=2).transpose(1, 0, 2).reshape(128, 272))
    sgu = np.asarray(inp["L0_ffn_shared_gate_up_w"], np.float32)
    sdn = np.asarray(inp["L0_ffn_shared_down_w"], np.float32)
    w13 = np.asarray(inp["L0_ffn_w13"], np.float32)
    w2 = np.asarray(inp["L0_ffn_w2"], np.float32)
    e13, e2 = [], []
    for j in range(4):
        W13p = np.concatenate([sgu[I_EXP * j:I_EXP * (j + 1)],
                               sgu[I_SH + I_EXP * j:I_SH + I_EXP * (j + 1)]], axis=0)
        e13.append(_mblocks_2816(W13p))
        e2.append(_nblocks_1408(sdn[:, I_EXP * j:I_EXP * (j + 1)]))
    for e in range(NE_REAL):
        e13.append(_mblocks_2816(w13[e]))
        e2.append(_nblocks_1408(w2[e]))
    out["w13"] = np.ascontiguousarray(np.stack(e13))
    out["w2"] = np.ascontiguousarray(np.stack(e2))
    return out

def _rope_tab(pos, scale):
    inv = 1.0 / (10000.0 ** (np.arange(0, DH, 2) / DH))
    ang = pos[None, :].astype(np.float64) * inv[:, None]
    c = (np.cos(ang) * scale).astype(np.float32)
    s = (np.sin(ang) * scale).astype(np.float32)
    return np.ascontiguousarray(np.concatenate([c, c, s, s], axis=0))

def _per_core(inp, core):
    b, i = core // 4, core % 4
    x = np.asarray(inp["L0_attn_x"], np.float32)[b]
    others = list(range(0, i)) + list(range(i + 1, 7 - i))
    xrows = np.zeros((S, D), np.float32)
    pos = np.zeros((S,), np.int64)
    for s_, ch in enumerate(others):
        xrows[s_ * 512:(s_ + 1) * 512] = x[ch * 512:(ch + 1) * 512]
        pos[s_ * 512:(s_ + 1) * 512] = np.arange(ch * 512, (ch + 1) * 512)
    xrows[3072:3584] = x[i * 512:(i + 1) * 512]
    pos[3072:3584] = np.arange(i * 512, (i + 1) * 512)
    xrows[3584:4096] = x[(7 - i) * 512:(8 - i) * 512]
    pos[3584:4096] = np.arange((7 - i) * 512, (8 - i) * 512)
    biascol = np.zeros((128, 49), np.float32)
    nlo, nhi = 4 * i, 24 - 4 * i
    for kt in range(24):
        biascol[:, kt] = 0.0 if kt < nlo else NEG
        biascol[:, 24 + kt] = 0.0 if kt < nhi else NEG
    return {"xrows": xrows, "ktab": _rope_tab(pos, 1.0),
            "qtab": _rope_tab(pos[3072:], 0.125), "biascol": biascol}


ENGS = ("tensor", "scalar", "vector", "gpsimd", "sync")
QSEMS = ("qi", "qkv", "qt0", "qt1", "qx0", "qx1", "qq", "qwo0", "qwo1",
         "qxq0", "qxq1", "q13a", "q13b", "q2a", "q2b", "qo")
ALLSEMS = ENGS + QSEMS

class Bld:
    def __init__(self, me, eng, sems):
        self.me, self.eng, self.sems = me, eng, sems
        self.ct = {k: 0 for k in ALLSEMS}
        self.last_wait = {}

    def _wait(self, sem_name, val):
        if val <= 0 or self.last_wait.get(sem_name, -1) >= val:
            return
        self.last_wait[sem_name] = val
        self.eng.wait_ge(self.sems[sem_name], val)

    def op(self, eng_name, fn, waits=(), inc=True):
        if self.me == eng_name:
            for (s, v) in waits:
                self._wait(s, v)
            instr = fn(self.eng)
            if inc:
                instr.then_inc(self.sems[eng_name], 1)
        if inc:
            self.ct[eng_name] += 1
        return self.ct[eng_name]

    def dma(self, q_name, sem, out, in_, waits=()):
        if self.me == q_name:
            for (s, v) in waits:
                self._wait(s, v)
            self.eng.dma_start(out=out, in_=in_).then_inc(self.sems[sem], 16)
        self.ct[sem] += 16
        return self.ct[sem]

    def wait(self, eng_name, sem_name, val):
        if self.me == eng_name:
            self._wait(sem_name, val)

    def fence(self):
        return dict(self.ct)


def build_walk(T, C, b: Bld, phases=6):
    def dump_and_stop():
        f = b.fence()
        last = 0
        for name, src_ap in C.get("dumps", []):
            last = b.dma("gpsimd", "qo", C["dbg_aps"][name], src_ap,
                         waits=[(e_, f[e_]) for e_ in ALLSEMS if e_ != "qo"])
        b.wait("gpsimd", "qo", last)

    pb = T["pb"]
    def bank(ix):
        return pb[:, 512 * ix:512 * (ix + 1)]

    b.dma("gpsimd", "qi", T["biascol"][:], C["in"]["biascol"])
    b.dma("gpsimd", "qi", T["rms16land"][:], C["in"]["rms16"])
    b.dma("gpsimd", "qi", T["qtab"][:], C["in"]["qtab"])
    b.dma("gpsimd", "qi", T["wgateland"][:], C["in"]["wgate"])
    qi = b.dma("gpsimd", "qi", T["ident"][:], C["ident_dram"].ap())
    g_bias = g_rms = g_qtab = g_gate = g_sg = g_id = ("qi", qi)
    v_ones = b.op("vector", lambda e: e.memset(T["vst_ones"], 1.0))
    v_cst = b.op("vector", lambda e: e.memset(T["cst"][:, 0:1], 1e-11))
    v_cst = b.op("vector", lambda e: e.memset(T["cst"][:, 1:2], 1e-5))
    v_o164 = b.op("vector", lambda e: e.memset(T["ones164"][:], 1.0))
    v_o64c = b.op("vector", lambda e: e.memset(T["ones64c"][:], 1.0))
    v_gate = b.op("vector", lambda e: e.tensor_copy(out=T["wgateb"][:], in_=T["wgateland"][:]),
                  waits=[g_gate])
    rms_v = b.op("vector", lambda e: e.tensor_copy(out=T["rms16"][:], in_=T["rms16land"][:]),
                 waits=[g_rms])
    v_wkv = 0
    for k in range(16):
        q_kv = b.dma("sync", "qkv", T["kvland"][:], C["in"]["wkv"][k],
                     waits=([("vector", v_wkv)] if k else []))
        v_wkv = b.op("vector", lambda e, k=k: e.tensor_copy(
            out=T["wkvb"][:, 1024 * k:1024 * (k + 1)], in_=T["kvland"][:]),
            waits=[("qkv", q_kv)])

    if phases <= 0:
        dump_and_stop()
        return
    if phases in (0.2,):
        b.op("vector", lambda e: e.memset(T["kst"][:, :], 0.0))
        b.op("vector", lambda e: e.memset(T["vst"][:, :], 0.0))
    x_rd_v = [0, 0]
    x_rd_s = [0, 0]
    tab_rd = [0, 0]
    zbuf_rd_pe = 0
    prev_tev = 0
    last_rope = 0
    sc_v = 0

    def ln_tile(src_ap, src_waits, dst_fn, fold_rms, stats_ix, zb="zbuf"):
        nonlocal zbuf_rd_pe, prev_tev
        vb = b.op("vector", lambda e: e.bn_stats(out=T["bn24"][:, 0:6], in_=src_ap[:, 0:512]),
                  waits=src_waits)
        for j in range(1, 4):
            vb = b.op("vector", lambda e, j=j: e.bn_stats(
                out=T["bn24"][:, 6 * j:6 * (j + 1)], in_=src_ap[:, 512 * j:512 * (j + 1)]))
        b.wait("vector", "vector", vb)
        stat = T["stats"][:, 2 * stats_ix:2 * stats_ix + 2] if stats_ix is not None else T["stat2"][:]
        va = b.op("vector", lambda e: e.bn_aggr(out=stat, in_=T["bn24"][:]))
        sc_, bi_ = (1.000001, 0) if fold_rms else (1.0, 1)
        sa = b.op("scalar", lambda e: e.activation(out=T["t1"][:], in_=stat[:, 1:2], func=AF.Sqrt,
                                                   bias=T["cst"][:, bi_:bi_ + 1], scale=sc_),
                  waits=[("vector", va), ("vector", v_cst)])
        vr = b.op("vector", lambda e: e.reciprocal(out=T["scl"][:], in_=T["t1"][:]),
                  waits=[("scalar", sa)])
        b.wait("vector", "vector", vr)
        sz = b.op("vector", lambda e: e.tensor_scalar(
            out=T[zb][:], in0=src_ap, scalar1=stat[:, 0:1], scalar2=T["scl"][:],
            op0=OP.subtract, op1=OP.mult),
            waits=[("tensor", zbuf_rd_pe)])
        b.wait("vector", "vector", sz)
        ev = prev_tev
        for c in range(16):
            bk = 4 + (c % 4)
            tp = b.op("tensor", lambda e, c=c, bk=bk: e.transpose(
                bank(bk)[:, 0:128], T[zb][:, 128 * c:128 * (c + 1)], T["ident"][:]),
                waits=[("vector", sz), g_id, ("vector", ev)])
            dst, use_rms = dst_fn(c)
            if use_rms:
                ev = b.op("vector", lambda e, dst=dst, bk=bk, c=c: e.tensor_scalar_mul(
                    out=dst, in0=bank(bk)[:, 0:128], scalar1=T["rms16"][:, c:c + 1]),
                    waits=[("tensor", tp), ("vector", rms_v)])
            else:
                ev = b.op("vector", lambda e, dst=dst, bk=bk: e.tensor_copy(
                    out=dst, in_=bank(bk)[:, 0:128]), waits=[("tensor", tp)])
        zbuf_rd_pe = b.ct["tensor"]
        prev_tev = ev
        return ev, sz

    def rope_evac(psum_ap, tab, dst_top, dst_bot, mm_done, extra):
        w = [("tensor", mm_done)] + list(extra)
        b.op("vector", lambda e: e.tensor_mul(out=T["rtu"][:], in0=psum_ap[64:128, :], in1=tab[64:128, :]), waits=w)
        b.op("vector", lambda e: e.tensor_mul(out=T["rtv"][:], in0=psum_ap[0:64, :], in1=tab[64:128, :]))
        b.op("vector", lambda e: e.tensor_mul(out=T["rtw"][:], in0=psum_ap[0:64, :], in1=tab[0:64, :]))
        v4 = b.op("vector", lambda e: e.tensor_mul(out=T["rtz"][:], in0=psum_ap[64:128, :], in1=tab[0:64, :]))
        b.wait("vector", "vector", v4)
        b.op("vector", lambda e: e.tensor_sub(out=dst_top, in0=T["rtw"][:], in1=T["rtu"][:]))
        return b.op("vector", lambda e: e.tensor_add(out=dst_bot, in0=T["rtz"][:], in1=T["rtv"][:]))

    for tb in range(1 if phases == 0.2 else NTB):
        own = tb >= 6
        dst_zT = T["z_own"][tb - 6] if own else T["zT"]
        tp_ = tb % 2
        g_tab = ("qt%d" % tp_, b.dma("gpsimd", "qt%d" % tp_, T["tabbuf"][tp_][:],
                 C["in"]["ktab"][:, 512 * tb:512 * (tb + 1)], waits=[("vector", tab_rd[tp_])]))
        ev_last = 0
        for tt in range(4):
            p = (tb * 4 + tt) % 2
            g_x = ("qx%d" % p, b.dma("gpsimd", "qx%d" % p, T["xbuf"][p][:],
                   C["in"]["xrows"][512 * tb + 128 * tt:512 * tb + 128 * (tt + 1), :],
                   waits=[("vector", x_rd_v[p])]))
            def dst_fn(c, dst_zT=dst_zT, tt=tt):
                return dst_zT[:, 512 * c + 128 * tt:512 * c + 128 * (tt + 1)], True
            ev_last, sz = ln_tile(T["xbuf"][p][:], [g_x], dst_fn, True,
                                  (tb - 6) * 4 + tt if own else None)
            x_rd_v[p] = b.ct["vector"]
            x_rd_s[p] = sz
        for m in range(4):
            bk = m % 2
            pe_k = 0
            for k in range(16):
                pe_k = b.op("tensor", lambda e, m=m, k=k, bk=bk, dst_zT=dst_zT: e.matmul(
                    bank(bk), T["wkvb"][:, 1024 * k + 128 * m:1024 * k + 128 * (m + 1)],
                    dst_zT[:, 512 * k:512 * (k + 1)], start=(k == 0), stop=(k == 15)),
                    waits=([("vector", ev_last), ("vector", last_rope)] if k == 0 else []),
                    inc=(k == 15))
            cols = slice(4096 * m + 512 * tb, 4096 * m + 512 * (tb + 1))
            last_rope = rope_evac(bank(bk), T["tabbuf"][tp_], T["kst"][0:64, cols],
                                  T["kst"][64:128, cols], pe_k, [g_tab])
        tab_rd[tp_] = b.ct["vector"]
        for tt in range(4):
            bk = 2 + tt % 2
            pe_v = 0
            for k in range(16):
                pe_v = b.op("tensor", lambda e, tt=tt, k=k, bk=bk, dst_zT=dst_zT: e.matmul(
                    bank(bk), dst_zT[:, 512 * k + 128 * tt:512 * k + 128 * (tt + 1)],
                    T["wkvb"][:, 1024 * k + 512:1024 * (k + 1)], start=(k == 0), stop=(k == 15)),
                    waits=([("scalar", sc_v)] if k == 0 else []), inc=(k == 15))
            base = 520 * (tb * 4 + tt)
            for g in range(NG):
                sc_v = b.op("scalar", lambda e, g=g, base=base, bk=bk: e.copy(
                    out=T["vst"][:, base + 65 * g:base + 65 * g + 64],
                    in_=bank(bk)[:, 64 * g:64 * (g + 1)]),
                    waits=([("tensor", pe_v)] if g == 0 else []))

    if phases in (0.2, 0.7):
        dump_and_stop()
        return
    fQ = b.fence()
    qland_rd = 0
    wqb_rd = [0, 0]
    for m in range(16):
        mp = m % 2
        q_q = b.dma("sync", "qq", T["qland"][:], C["in"]["wq"][m],
                    waits=[("vector", max(qland_rd, fQ["vector"])), ("scalar", fQ["scalar"])])
        qcv = b.op("vector", lambda e, mp=mp: e.tensor_copy(out=T["wqb"][mp][:], in_=T["qland"][:]),
                   waits=[("qq", q_q), ("tensor", wqb_rd[mp]), ("scalar", fQ["scalar"])])
        qland_rd = qcv
        for tbq in range(2):
            bk = tbq
            pe_q = 0
            for k in range(16):
                pe_q = b.op("tensor", lambda e, k=k, tbq=tbq, bk=bk, mp=mp: e.matmul(
                    bank(bk), T["wqb"][mp][:, 128 * k:128 * (k + 1)],
                    T["z_own"][tbq][:, 512 * k:512 * (k + 1)], start=(k == 0), stop=(k == 15)),
                    waits=([("vector", qcv), ("vector", last_rope)] if k == 0 else []), inc=(k == 15))
            cols = slice(1024 * m + 512 * tbq, 1024 * m + 512 * (tbq + 1))
            last_rope = rope_evac(bank(bk), T["qtab"][:, 512 * tbq:512 * (tbq + 1)],
                                  T["qot"][0:64, cols], T["qot"][64:128, cols], pe_q,
                                  [g_qtab])
        wqb_rd[mp] = b.ct["tensor"]

    if phases <= 1:
        dump_and_stop()
        return
    fP2 = b.fence()
    v_kones = b.op("vector", lambda e: e.memset(T["ktil"][64:65, :], 1.0),
                   waits=[("tensor", fP2["tensor"])])
    last_av = 0
    act_exp = 0
    exp_h = [0, 0]
    av_done = [[0, 0], [0, 0]]
    osb_wr = 0
    last_vevac = last_rope
    pool_r1 = 0
    s_kq = 0
    sc_p1 = fP2["scalar"]
    for g in range(NG):
        mk, half = g // 2, g % 2
        kcols = slice(4096 * mk, 4096 * (mk + 1))
        sv = b.op("vector", lambda e, half=half, kcols=kcols: e.tensor_copy(
            out=T["ktil"][0:32, :], in_=T["kst"][32 * half:32 * half + 32, kcols]),
            waits=[("vector", last_vevac), ("tensor", max(last_av, fP2["tensor"])),
                   ("gpsimd", pool_r1), ("scalar", s_kq)])
        sv = b.op("vector", lambda e, half=half, kcols=kcols: e.tensor_copy(
            out=T["ktil"][32:64, :], in_=T["kst"][64 + 32 * half:96 + 32 * half, kcols]))
        for h in range(HPG):
            hh = 4 * g + h
            c, sub = hh // 2, hh % 2
            qc = slice(1024 * c, 1024 * (c + 1))
            sv = b.op("vector", lambda e, h=h, sub=sub, qc=qc: e.tensor_copy(
                out=T["qtil"][0:32, 1024 * h:1024 * (h + 1)], in_=T["qot"][32 * sub:32 * sub + 32, qc]))
            sv = b.op("vector", lambda e, h=h, sub=sub, qc=qc: e.tensor_copy(
                out=T["qtil"][32:64, 1024 * h:1024 * (h + 1)], in_=T["qot"][64 + 32 * sub:96 + 32 * sub, qc]))
        b.wait("vector", "vector", sv)
        v1 = b.op("vector", lambda e: e.tensor_mul(out=T["sqs"][:], in0=T["ktil"][0:64, :], in1=T["ktil"][0:64, :]))
        b.wait("vector", "vector", v1)
        km_prev = 0
        for half in range(2):
            mmn = 0
            for c4 in range(4):
                mmn = b.op("tensor", lambda e, half=half, c4=c4: e.matmul(
                    pb[0:1, 512 * c4:512 * (c4 + 1)], T["ones64c"][:],
                    T["sqs"][:, 2048 * half + 512 * c4:2048 * half + 512 * (c4 + 1)],
                    start=True, stop=True),
                    waits=([("vector", max(v1, v_o64c, km_prev)), ("scalar", max(exp_h[1], s_kq))]
                           if c4 == 0 else []), inc=(c4 == 3))
            km_prev = b.op("vector", lambda e, half=half: e.reduce_max(
                out=T["km2"][half][:], in_=pb[0:1, 0:2048], axis=AX.X), waits=[("tensor", mmn)])
        b.wait("vector", "vector", km_prev)
        v2 = b.op("vector", lambda e: e.tensor_max(out=T["km2"][0][:], in0=T["km2"][0][:], in1=T["km2"][1][:]))
        s1 = b.op("scalar", lambda e: e.activation(out=T["km"][:], in_=T["km2"][0][:], func=AF.Sqrt),
                  waits=[("vector", v2)])
        v3 = b.op("vector", lambda e: e.tensor_scalar_mul(out=T["kms"][:], in0=T["km"][:], scalar1=-BOUND_MARGIN),
                  waits=[("scalar", s1)])
        v4 = b.op("vector", lambda e: e.tensor_mul(out=T["sqs"][:], in0=T["qtil"][0:64, :], in1=T["qtil"][0:64, :]),
                  waits=[("vector", km_prev)])
        b.wait("vector", "vector", v4)
        s_kq = 0
        for half in range(2):
            mmn = 0
            for c4 in range(4):
                mmn = b.op("tensor", lambda e, half=half, c4=c4: e.matmul(
                    pb[0:1, 512 * c4:512 * (c4 + 1)], T["ones64c"][:],
                    T["sqs"][:, 2048 * half + 512 * c4:2048 * half + 512 * (c4 + 1)],
                    start=True, stop=True),
                    waits=([("vector", v4 if half == 0 else 0), ("scalar", s_kq)] if c4 == 0 else []),
                    inc=(c4 == 3))
            s_kq = b.op("scalar", lambda e, half=half: e.activation(
                out=T["qtil"][64:65, 2048 * half:2048 * (half + 1)], in_=pb[0:1, 0:2048], func=AF.Sqrt),
                waits=[("tensor", mmn), ("vector", sv)])
        vb_ = b.op("vector", lambda e: e.tensor_scalar_mul(out=T["qtil"][64:65, :], in0=T["qtil"][64:65, :],
                                                           scalar1=T["kms"][:]),
                   waits=[("scalar", s_kq), ("vector", v3)])
        bound_ready = vb_
        pool_r1 = 0

        for qb in range(2):
            tiles = [("o", kt) for kt in range(12 if qb == 0 else 24)] + \
                    [("s", j) for j in range(4 if qb == 0 else 8)]
            for idx, (kind, kt) in enumerate(tiles):
                if kind == "o":
                    krows = slice(128 * kt, 128 * (kt + 1))
                    bias_ap = T["biascol"][:, 24 * qb + kt:24 * qb + kt + 1]
                    vtile = kt
                else:
                    krows = slice(128 * (24 + kt), 128 * (25 + kt))
                    bias_ap = T["biascol"][:, 48:49]
                    vtile = 24 + kt
                pp = idx % 2
                qkh = [0, 0]
                for h in range(HPG):
                    qkh[h // 2] = b.op("tensor", lambda e, h=h, krows=krows, qb=qb: e.matmul(
                        bank(h), T["ktil"][:, krows],
                        T["qtil"][:, 1024 * h + 512 * qb:1024 * h + 512 * (qb + 1)],
                        start=True, stop=True),
                        waits=([("vector", max(bound_ready, last_vevac)),
                                ("scalar", max(exp_h[0], osb_wr, s_kq))] if h == 0 else
                               ([("scalar", exp_h[1])] if h == 2 else [])),
                        inc=(h in (1, 3)))
                for half in range(2):
                    exp_h[half] = b.op("scalar", lambda e, pp=pp, bias_ap=bias_ap, half=half: e.activation(
                        out=T["pbuf"][pp][:, 1024 * half:1024 * (half + 1)],
                        in_=pb[:, 1024 * half:1024 * (half + 1)], func=AF.Exp, bias=bias_ap, scale=1.0),
                        waits=[("tensor", max(qkh[half], av_done[pp][half])), g_bias])
                p_ready = [[("scalar", exp_h[0])], [("scalar", exp_h[1])]]
                if kind == "s" and not (qb == 1 and kt < 4):
                    for half in range(2):
                        pm = b.op("gpsimd", lambda e, pp=pp, qb=qb, kt=kt, half=half: e.affine_select(
                            out=T["pbuf"][pp][:, 1024 * half:1024 * (half + 1)],
                            in_=T["pbuf"][pp][:, 1024 * half:1024 * (half + 1)],
                            pattern=[[0, 2], [1, 512]],
                            compare_op=OP.is_ge, fill=0.0, base=512 * qb - 128 * kt, channel_multiplier=-1),
                            waits=[("scalar", exp_h[half])])
                        p_ready[half] = [("gpsimd", pm)]
                for h in range(HPG):
                    last_av = b.op("tensor", lambda e, h=h, vtile=vtile, g=g, pp=pp, idx=idx, ntile=len(tiles): e.matmul(
                        pb[0:65, 2048 + 512 * h:2048 + 512 * (h + 1)],
                        T["vst"][:, 520 * vtile + 65 * g:520 * vtile + 65 * (g + 1)],
                        T["pbuf"][pp][:, 512 * h:512 * (h + 1)], start=(idx == 0), stop=(idx == ntile - 1)),
                        waits=((p_ready[0] + [("vector", v_ones), ("scalar", max(osb_wr, sc_p1))])
                               if h == 0 else (p_ready[1] if h == 2 else [])), inc=(h in (1, 3)))
                    if h == 1:
                        av_done[pp][0] = last_av
                av_done[pp][1] = last_av
            so = b.op("scalar", lambda e: e.copy(out=T["osb"][:], in_=pb[0:65, 2048:4096]),
                      waits=[("tensor", last_av), ("vector", last_vevac), ("gpsimd", pool_r1)])
            osb_wr = so
            vrc = b.op("vector", lambda e: e.reciprocal(out=T["rrow32"][:], in_=T["osb"][64:65, :]),
                       waits=[("scalar", so)])
            b.wait("vector", "vector", vrc)
            vrc = b.op("vector", lambda e: e.tensor_copy(out=T["rrow"][:], in_=T["rrow32"][:]))
            bc = 0
            for h in range(HPG):
                bc = b.op("tensor", lambda e, h=h: e.matmul(
                    pb[0:64, 512 * h:512 * (h + 1)], T["ones164"][:],
                    T["rrow"][:, 512 * h:512 * (h + 1)], start=True, stop=True),
                    waits=([("vector", max(vrc, v_o164)), ("scalar", exp_h[1])] if h == 0 else []),
                    inc=(h == 3))
            for h in range(HPG):
                hh = 4 * g + h
                c, sub = hh // 2, hh % 2
                last_vevac = b.op("vector", lambda e, h=h, c=c, sub=sub, qb=qb: e.tensor_mul(
                    out=T["qot"][64 * sub:64 * (sub + 1), 1024 * c + 512 * qb:1024 * c + 512 * (qb + 1)],
                    in0=T["osb"][0:64, 512 * h:512 * (h + 1)], in1=pb[0:64, 512 * h:512 * (h + 1)]),
                    waits=[("tensor", bc), ("scalar", so)])

    if phases <= 2:
        dump_and_stop()
        return
    fP3 = b.fence()
    acc = T["acc"].ap()
    xq_rd_v = [fP3["vector"], fP3["vector"]]
    xq_rd_s = [0, 0]
    acc_ev = 0
    ybuf_rd = 0
    for tt in range(8):
        p = tt % 2
        g_xq = b.dma("gpsimd", "qxq%d" % p, T["xq"][p][:],
                     C["in"]["xrows"][3072 + 128 * tt:3072 + 128 * (tt + 1), :],
                     waits=[("vector", xq_rd_v[p]), ("tensor", fP3["tensor"])])
        sa = b.op("scalar", lambda e, tt=tt: e.activation(
            out=T["t1"][:], in_=T["stats"][:, 2 * tt + 1:2 * tt + 2], func=AF.Sqrt,
            bias=T["cst"][:, 1:2], scale=1.0),
            waits=[("vector", acc_ev)])
        vr = b.op("vector", lambda e: e.reciprocal(out=T["scl"][:], in_=T["t1"][:]),
                  waits=[("scalar", sa), ("vector", fP3["vector"])])
        b.wait("vector", "vector", vr)
        sy = b.op("vector", lambda e, p=p, tt=tt: e.tensor_scalar(
            out=T["ybuf"][:], in0=T["xq"][p][:], scalar1=T["stats"][:, 2 * tt:2 * tt + 1],
            scalar2=T["scl"][:], op0=OP.subtract, op1=OP.mult),
            waits=[("vector", ybuf_rd), ("qxq%d" % p, g_xq)])
        b.wait("vector", "vector", sy)
        acc_ev = b.op("vector", lambda e, p=p, tt=tt: e.tensor_add(
            out=acc[:, 2048 * tt:2048 * (tt + 1)], in0=T["xq"][p][:], in1=T["ybuf"][:]),
            waits=[("tensor", fP3["tensor"]), ("gpsimd", fP3["gpsimd"])])
        ybuf_rd = acc_ev
        xq_rd_v[p] = acc_ev
        xq_rd_s[p] = 0
    wol_rd = [0, 0]
    wob_rd_pe = 0
    for n in range(4):
        cvt = 0
        for k in range(16):
            kp = k % 2
            q_wo = b.dma("sync", "qwo%d" % kp, T["woland"][kp][:], C["in"]["wout"][n, k],
                         waits=[("vector", max(wol_rd[kp], fP3["vector"])), ("tensor", fP3["tensor"])])
            cvt = b.op("vector", lambda e, k=k, kp=kp: e.tensor_copy(
                out=T["wob"][:, 512 * k:512 * (k + 1)], in_=T["woland"][kp][:]),
                waits=[("qwo%d" % kp, q_wo), ("tensor", wob_rd_pe)])
            wol_rd[kp] = cvt
        for th in range(2):
            pe_wo = 0
            for k in range(16):
                for t4 in range(4):
                    tt = 4 * th + t4
                    pe_wo = b.op("tensor", lambda e, k=k, tt=tt, t4=t4: e.matmul(
                        bank(t4), T["qot"][:, 1024 * k + 128 * tt:1024 * k + 128 * (tt + 1)],
                        T["wob"][:, 512 * k:512 * (k + 1)], start=(k == 0), stop=(k == 15)),
                        waits=([("vector", max(cvt, acc_ev, last_vevac))] if (k == 0 and t4 == 0) else []),
                        inc=(k == 15 and t4 == 3))
            for t4 in range(4):
                tt = 4 * th + t4
                sl = slice(2048 * tt + 512 * n, 2048 * tt + 512 * (n + 1))
                acc_ev = b.op("vector", lambda e, t4=t4, sl=sl: e.tensor_add(
                    out=acc[:, sl], in0=bank(t4), in1=acc[:, sl]),
                    waits=[("tensor", pe_wo), ("vector", acc_ev)])
        wob_rd_pe = b.ct["tensor"]
    x1_ev = 0
    for tt in range(8):
        src = acc[:, 2048 * tt:2048 * (tt + 1)]
        def dst2(c, tt=tt):
            return T["x1nT"][:, 1024 * c + 128 * tt:1024 * c + 128 * (tt + 1)], False
        x1_ev, _ = ln_tile(src, [("vector", acc_ev)], dst2, False, None, zb="zbuf2")
    x1nT_done = x1_ev

    if phases <= 3:
        dump_and_stop()
        return
    se_rd = 0
    vm_rd = 0
    sg_rd = 0
    coef_done = 0
    sig_done = 0
    for tt in range(8):
        pe_r = 0
        for k in range(16):
            pe_r = b.op("tensor", lambda e, k=k, tt=tt: e.matmul(
                pb[:, 0:17], T["x1nT"][:, 1024 * k + 128 * tt:1024 * k + 128 * (tt + 1)],
                T["wgateb"][:, 17 * k:17 * (k + 1)], start=(k == 0), stop=(k == 15)),
                waits=([("vector", max(x1nT_done, v_gate, vm_rd)), ("scalar", max(se_rd, sg_rd))] if k == 0 else []),
                inc=(k == 15))
        vm = b.op("vector", lambda e: e.reduce_max(out=T["t1"][:], in_=pb[:, 0:16], axis=AX.X, negate=True),
                  waits=[("tensor", pe_r)])
        se = b.op("scalar", lambda e: e.activation(out=T["probs"][:], in_=pb[:, 0:16], func=AF.Exp,
                                                   bias=T["t1"][:], scale=1.0),
                  waits=[("vector", max(vm, coef_done))])
        se_rd = se
        vs = b.op("vector", lambda e: e.reduce_sum(out=T["t2"][:], in_=T["probs"][:], axis=AX.X),
                  waits=[("scalar", se)])
        b.wait("vector", "vector", vs)
        vrc2 = b.op("vector", lambda e: e.reciprocal(out=T["t3"][:], in_=T["t2"][:]))
        b.wait("vector", "vector", vrc2)
        vn = b.op("vector", lambda e: e.tensor_scalar_mul(out=T["probs"][:], in0=T["probs"][:], scalar1=T["t3"][:]))
        b.wait("vector", "vector", vn)
        v8 = b.op("vector", lambda e: e.max(out=T["top8"][:], in_=T["probs"][:]))
        b.wait("vector", "vector", v8)
        coef_done = b.op("vector", lambda e, tt=tt: e.scalar_tensor_tensor(
            out=T["coef"][:, 16 * tt:16 * (tt + 1)], in0=T["probs"][:], scalar=T["top8"][:, 3:4],
            in1=T["probs"][:], op0=OP.is_ge, op1=OP.mult))
        vm_rd = coef_done
        sig_done = b.op("scalar", lambda e, tt=tt: e.activation(out=T["sig"][:, tt:tt + 1], in_=pb[:, 16:17],
                                                                func=AF.Sigmoid), waits=[("tensor", pe_r)])
        sg_rd = sig_done

    if phases <= 4:
        dump_and_stop()
        return
    _P5_CUT = (phases == 4.5)
    fP5 = b.fence()
    w13l_rd = [0, 0]
    w2l_rd = [0, 0]
    w13b_rd = [0, 0]
    w2b_rd = [0, 0]
    hT_rd_pe = 0
    gs_wr = 0
    gs_rd_v = 0
    h_wr = 0
    b.wait("tensor", "scalar", sig_done)
    b.wait("tensor", "vector", coef_done)
    for e_ix in range({4.5: 2, 4.6: 8, 4.7: 14, 4.8: 17, 4.85: 18, 4.9: 19}.get(phases, NE)):
        for jm in range(22):
            mp = jm % 2
            cvt = 0
            for pc in range(2):
                kp = pc % 2
                q13 = b.dma("sync", "q13a" if kp == 0 else "q13b", T["w13land"][kp][:],
                            C["in"]["w13"][e_ix, jm, :, 1024 * pc:1024 * (pc + 1)],
                            waits=[("vector", max(w13l_rd[kp], fP5["vector"])),
                                   ("scalar", fP5["scalar"]), ("tensor", fP5["tensor"])])
                cvt = b.op("vector", lambda e, pc=pc, kp=kp, mp=mp: e.tensor_copy(
                    out=T["w13b"][mp][:, 1024 * pc:1024 * (pc + 1)], in_=T["w13land"][kp][:]),
                    waits=[("q13a" if kp == 0 else "q13b", q13), ("tensor", w13b_rd[mp])])
                w13l_rd[kp] = cvt
            is_g = (jm % 2 == 0)
            j = jm // 2
            pe_tb = [0, 0]
            for k in range(16):
                for tb in range(2):
                    pe_tb[tb] = b.op("tensor", lambda e, k=k, tb=tb, bk=(0 if is_g else 2) + tb, mp=mp: e.matmul(
                        bank(bk), T["w13b"][mp][:, 128 * k:128 * (k + 1)],
                        T["x1nT"][:, 1024 * k + 512 * tb:1024 * k + 512 * (tb + 1)],
                        start=(k == 0), stop=(k == 15)),
                        waits=(([("vector", cvt)] + ([("scalar", gs_wr)] if is_g else [("vector", h_wr)]))
                               if (k == 0 and tb == 0) else []), inc=(k == 15))
            pe_m = pe_tb[1]
            w13b_rd[mp] = pe_m
            if is_g:
                for tb in range(2):
                    gs_wr = b.op("scalar", lambda e, tb=tb: e.activation(
                        out=T["gs"][:, 512 * tb:512 * (tb + 1)], in_=bank(tb), func=AF.Silu),
                        waits=[("tensor", pe_tb[tb]), ("vector", gs_rd_v)] if tb == 0 else [("tensor", pe_tb[tb])])
            else:
                for tb in range(2):
                    h_wr = b.op("vector", lambda e, tb=tb, j=j: e.tensor_mul(
                        out=T["hT"][:, 1024 * j + 512 * tb:1024 * j + 512 * (tb + 1)],
                        in0=T["gs"][:, 512 * tb:512 * (tb + 1)], in1=bank(2 + tb)),
                        waits=[("scalar", gs_wr), ("tensor", max(pe_m, hT_rd_pe))])
                gs_rd_v = h_wr
        acc_ev_l = acc_ev
        for n in range(4):
            np_ = n % 2
            w2cv = 0
            for k in range(4):
                kp = k % 2
                q2 = b.dma("gpsimd", "q2a" if kp == 0 else "q2b", T["w2land"][kp][:],
                           C["in"]["w2"][e_ix, n, :, 1408 * k:1408 * (k + 1)],
                           waits=[("vector", max(w2l_rd[kp], fP5["vector"])),
                                  ("tensor", fP5["tensor"]), ("scalar", fP5["scalar"])])
                w2cv = b.op("vector", lambda e, k=k, kp=kp, np_=np_: e.tensor_copy(
                    out=T["w2b"][np_][:, 1408 * k:1408 * (k + 1)], in_=T["w2land"][kp][:]),
                    waits=[("q2a" if kp == 0 else "q2b", q2), ("tensor", w2b_rd[np_])])
                w2l_rd[kp] = w2cv
            for th in range(2):
                pe_d = 0
                for k in range(11):
                    for t4 in range(4):
                        tt = 4 * th + t4
                        pe_d = b.op("tensor", lambda e, k=k, tt=tt, t4=t4, np_=np_: e.matmul(
                            bank(4 + t4), T["hT"][:, 1024 * k + 128 * tt:1024 * k + 128 * (tt + 1)],
                            T["w2b"][np_][:, 512 * k:512 * (k + 1)], start=(k == 0), stop=(k == 10)),
                            waits=([("vector", max(w2cv, h_wr, acc_ev))] if (k == 0 and t4 == 0) else []),
                            inc=(k == 10 and t4 == 3))
                for t4 in range(4):
                    tt = 4 * th + t4
                    if e_ix < 4:
                        coef_ap = T["sig"][:, tt:tt + 1]
                    else:
                        coef_ap = T["coef"][:, 16 * tt + (e_ix - 4):16 * tt + (e_ix - 4) + 1]
                    sl = slice(2048 * tt + 512 * n, 2048 * tt + 512 * (n + 1))
                    acc_ev = b.op("vector", lambda e, t4=t4, sl=sl, coef_ap=coef_ap: e.scalar_tensor_tensor(
                        out=acc[:, sl], in0=bank(4 + t4), scalar=coef_ap, in1=acc[:, sl],
                        op0=OP.mult, op1=OP.add),
                        waits=[("tensor", pe_d), ("vector", max(acc_ev, coef_done)),
                               ("scalar", sig_done)])
            w2b_rd[np_] = pe_d
        hT_rd_pe = b.ct["tensor"]
    if phases in (4.5, 4.6, 4.7, 4.8, 4.85, 4.9):
        dump_and_stop()
        return

    qo = 0
    for tt in range(8):
        qo = b.dma("gpsimd", "qo", C["out_ap"][128 * tt:128 * (tt + 1), :], acc[:, 2048 * tt:2048 * (tt + 1)],
                   waits=[("vector", acc_ev)])
    fE = b.fence()
    for s_ in QSEMS:
        b.wait("gpsimd", s_, fE[s_])



def build_program(phases=6, dumps=()):
    nc = bass.Bass("TRN2", target_bir_lowering=False)
    CIN = {}
    CIN["xrows"] = nc.dram_tensor("xrows", [S, D], F32, kind="ExternalInput").ap()
    CIN["ktab"] = nc.dram_tensor("ktab", [128, S], F32, kind="ExternalInput").ap()
    CIN["qtab"] = nc.dram_tensor("qtab", [128, OWN], F32, kind="ExternalInput").ap()
    CIN["biascol"] = nc.dram_tensor("biascol", [128, 49], F32, kind="ExternalInput").ap()
    CIN["wkv"] = nc.dram_tensor("wkv", [16, 128, 1024], F32, kind="ExternalInput").ap()
    CIN["wq"] = nc.dram_tensor("wq", [16, 128, 2048], F32, kind="ExternalInput").ap()
    CIN["wout"] = nc.dram_tensor("wout", [4, 16, 128, 512], F32, kind="ExternalInput").ap()
    CIN["rms16"] = nc.dram_tensor("rms16", [128, 16], F32, kind="ExternalInput").ap()
    CIN["wgate"] = nc.dram_tensor("wgate", [128, 272], F32, kind="ExternalInput").ap()
    CIN["w13"] = nc.dram_tensor("w13", [NE, 22, 128, 2048], F32, kind="ExternalInput").ap()
    CIN["w2"] = nc.dram_tensor("w2", [NE, 4, 128, 5632], F32, kind="ExternalInput").ap()
    out_ap = nc.dram_tensor("out", [OWN, D], F32, kind="ExternalOutput").ap()
    ident_dram = nc.inline_tensor(np.eye(128, dtype=np.float32), "identf32")

    SBASE = 16512
    def at(name, shape, dtype, off):
        return nc.alloc_sbuf_tensor_at(name, shape, dtype, offset=SBASE + off)

    T = {}
    EB, FB, MB = 98816, 164352, 197120
    T["kst"] = at("kst", [128, 16384], BF16, 0).ap()
    T["vst"] = at("vst", [128, 16640], BF16, 32768)
    T["qot"] = at("qot", [128, 16384], BF16, 66048).ap()
    T["acc"] = at("acc", [128, 16384], F32, EB)
    T["x1nT"] = at("x1nT", [128, 16384], BF16, FB).ap()
    T["z_own"] = [at("z_own0", [128, 8192], BF16, EB).ap(), at("z_own1", [128, 8192], BF16, EB + 16384).ap()]
    T["rtu"] = at("rtu", [64, 512], F32, EB + 32768)
    T["rtv"] = at("rtv", [64, 512], F32, EB + 34816)
    T["rtw"] = at("rtw", [64, 512], F32, EB + 36864)
    T["rtz"] = at("rtz", [64, 512], F32, EB + 38912)
    T["xbuf"] = [at("xbuf0", [128, 2048], F32, EB + 40960), at("xbuf1", [128, 2048], F32, EB + 49152)]
    T["qland"] = at("qland", [128, 2048], F32, EB + 40960)
    T["wqb"] = [at("wqb0", [128, 2048], BF16, EB + 49152), at("wqb1", [128, 2048], BF16, EB + 53248)]
    T["zT"] = at("zT", [128, 8192], BF16, FB).ap()
    T["zbuf"] = at("zbuf", [128, 2048], F32, FB + 16384)
    T["kvland"] = at("kvland", [128, 1024], F32, FB + 24576)
    T["tabbuf"] = [at("tabbuf0", [128, 512], F32, FB + 28672), at("tabbuf1", [128, 512], F32, FB + 30720)]
    T["wkvb"] = at("wkvb", [128, 16384], BF16, 66048).ap()
    T["qtil"] = at("qtil", [65, 4096], BF16, EB).ap()
    T["ktil"] = at("ktil", [65, 4096], BF16, EB + 8192).ap()
    T["sqs"] = at("sqs", [64, 4096], BF16, EB + 16384)
    T["r1"] = at("r1", [1, 4096], F32, EB + 24576)
    T["pbuf"] = [at("pbuf0", [128, 2048], BF16, EB + 40960), at("pbuf1", [128, 2048], BF16, EB + 45056)]
    T["osb"] = at("osb", [65, 2048], F32, EB + 16384).ap()
    T["rrow"] = at("rrow", [1, 2048], BF16, EB + 49152)
    T["rrow32"] = at("rrow32", [1, 2048], F32, EB + 53248)
    T["xq"] = [at("xq0", [128, 2048], F32, 0), at("xq1", [128, 2048], F32, 8192)]
    T["wob"] = at("wob", [128, 8192], BF16, 16384)
    T["woland"] = [at("woland0", [128, 512], F32, 32768), at("woland1", [128, 512], F32, 34816)]
    T["ybuf"] = at("ybuf", [128, 2048], F32, 36864)
    T["zbuf2"] = at("zbuf2", [128, 2048], F32, 45056)
    T["w13land"] = [at("w13land0", [128, 1024], F32, 0), at("w13land1", [128, 1024], F32, 4096)]
    T["w13b"] = [at("w13b0", [128, 2048], BF16, 8192), at("w13b1", [128, 2048], BF16, 12288)]
    T["w2land"] = [at("w2land0", [128, 1408], F32, 16384), at("w2land1", [128, 1408], F32, 22016)]
    T["w2b"] = [at("w2b0", [128, 5632], BF16, 27648), at("w2b1", [128, 5632], BF16, 38912)]
    T["hT"] = at("hT", [128, 11264], BF16, 50176).ap()
    T["gs"] = at("gs", [128, 1024], F32, 72704)
    off = [MB]
    def misc(name, shape, dtype):
        t = at(name, shape, dtype, off[0])
        nbytes = int(np.prod(shape[1:])) * mybir.dt.size(dtype)
        off[0] += (nbytes + 31) // 32 * 32
        return t
    T["ident"] = misc("ident", [128, 128], F32)
    T["biascol"] = misc("biascolS", [128, 49], F32)
    T["rms16land"] = misc("rms16land", [128, 16], F32)
    T["rms16"] = misc("rms16S", [128, 16], F32)
    T["stats"] = misc("stats", [128, 16], F32)
    T["stat2"] = misc("stat2", [128, 2], F32)
    T["bn24"] = misc("bn24", [128, 24], F32)
    T["t1"] = misc("t1", [128, 1], F32)
    T["scl"] = misc("scl", [128, 1], F32)
    T["bia"] = misc("bia", [128, 1], F32)
    T["t2"] = misc("t2", [128, 1], F32)
    T["t3"] = misc("t3", [128, 1], F32)
    T["km2"] = [misc("km2a", [1, 1], F32), misc("km2b", [1, 1], F32)]
    T["km"] = misc("km", [1, 1], F32)
    T["kms"] = misc("kms", [1, 1], F32)
    T["qtab"] = misc("qtabS", [128, 1024], F32)
    T["wgateland"] = misc("wgateland", [128, 272], F32)
    T["wgateb"] = misc("wgateb", [128, 272], BF16)
    T["probs"] = misc("probs", [128, 16], F32)
    T["top8"] = misc("top8", [128, 8], F32)
    T["coef"] = misc("coef", [128, 128], F32)
    T["sig"] = misc("sig", [128, 8], F32)
    T["ones164"] = misc("ones164", [1, 64], BF16)
    T["ones64c"] = misc("ones64c", [64, 1], BF16)
    T["cst"] = misc("cst", [128, 2], F32)
    assert off[0] <= 212700, f"misc overflow: {off[0]}"
    vst_h = T["vst"]
    T["vst"] = vst_h.ap()
    T["vst_ones"] = bass.AP(vst_h, 64, [[16640, 128], [520, 32], [65, 8], [1, 1]])
    T["acc"] = T["acc"]
    T["pb"] = nc.alloc_psum_tensor("pb", [128, 4096], F32).ap()

    C = {"in": CIN, "out_ap": out_ap, "ident_dram": ident_dram}
    dump_specs = {
        "kst": (T["kst"][:, :], [128, 16384], BF16),
        "vst": (T["vst"][:, :], [128, 16640], BF16),
        "qot": (T["qot"][:, :], [128, 16384], BF16),
        "zT0": (T["z_own"][0][:, :], [128, 8192], BF16),
        "zT1": (T["z_own"][1][:, :], [128, 8192], BF16),
        "wkvb": (T["wkvb"][:, :], [128, 16384], BF16),
        "acc": (T["acc"][:], [128, 16384], F32),
        "x1nT": (T["x1nT"][:, :], [128, 16384], BF16),
        "coef": (T["coef"][:], [128, 128], F32),
        "sig": (T["sig"][:], [128, 8], F32),
        "stats": (T["stats"][:], [128, 16], F32),
        "hT": (T["hT"][:, :], [128, 11264], BF16),
    }
    C["dumps"] = []
    C["dbg_aps"] = {}
    for name in dumps:
        ap_, shp, dt = dump_specs[name]
        C["dbg_aps"][name] = nc.dram_tensor("dbg_" + name, shp, dt, kind="ExternalOutput").ap()
        C["dumps"].append((name, ap_))

    import contextlib
    with contextlib.ExitStack() as st:
        sems = {k: st.enter_context(nc.semaphore("sem_" + k)) for k in ALLSEMS}
        block = st.enter_context(nc.Block())

        @block.tensor
        def _(eng):
            build_walk(T, C, Bld("tensor", eng, sems), phases)

        @block.scalar
        def _(eng):
            build_walk(T, C, Bld("scalar", eng, sems), phases)

        @block.vector
        def _(eng):
            build_walk(T, C, Bld("vector", eng, sems), phases)

        @block.gpsimd
        def _(eng):
            build_walk(T, C, Bld("gpsimd", eng, sems), phases)

        @block.sync
        def _(eng):
            build_walk(T, C, Bld("sync", eng, sems), phases)

    return nc


_PROG_CACHE = {}

def kernel(**inputs):
    shared = _shared_layouts(inputs)
    if "nc" not in _PROG_CACHE:
        _PROG_CACHE["nc"] = build_program()
    nc = _PROG_CACHE["nc"]
    in_maps = []
    for core in range(NCORES):
        m = dict(shared)
        m.update(_per_core(inputs, core))
        in_maps.append(m)
    res = run_bass_kernel_spmd(nc, in_maps, core_ids=list(range(NCORES)))
    out = np.zeros((BATCH, S, D), np.float32)
    for core in range(NCORES):
        bb, i = core // 4, core % 4
        o = res.results[core]["out"]
        out[bb, i * 512:(i + 1) * 512] = o[0:512]
        out[bb, (7 - i) * 512:(8 - i) * 512] = o[512:1024]
    return out


# version 06
# speedup vs baseline: 1.7520x; 1.0171x over previous
import numpy as np
import ml_dtypes
import concourse.bass as bass
import concourse.mybir as mybir
from concourse.bass_utils import run_bass_kernel_spmd
from concourse.alu_op_type import AluOpType as OP

F32 = mybir.dt.float32
BF16 = mybir.dt.bfloat16
AF = mybir.ActivationFunctionType
AX = mybir.AxisListType

NCORES = 8
D = 2048
S = 4096
BATCH = 2
NG, HPG, DH = 8, 4, 64
NE_REAL = 16
I_EXP, I_SH = 1408, 5632
NE = 20
OWN = 1024
NTB = 8
NEG = -1e30
BOUND_MARGIN = 1.05


def _mblocks_2816(W):
    A = W.reshape(22, 128, 16, 128).transpose(0, 3, 2, 1).reshape(22, 128, 2048)
    order = []
    for j in range(11):
        order += [j, 11 + j]
    return np.ascontiguousarray(A[order])

def _nblocks_1408(W2):
    return np.ascontiguousarray(W2.reshape(4, 512, 11, 128).transpose(0, 3, 2, 1).reshape(4, 128, 5632))

def _shared_layouts(inp):
    out = {}
    w_in = np.asarray(inp["L0_attn_w_in"], np.float32)
    permk = []
    for m in range(4):
        for half in (0, 1):
            for g in (2 * m, 2 * m + 1):
                base = 2048 + g * 64 + 32 * half
                permk += list(range(base, base + 32))
    WK = w_in[permk]
    WV = w_in[2560:3072]
    out["wkv"] = np.ascontiguousarray(np.concatenate(
        [WK.reshape(512, 16, 128).transpose(1, 2, 0),
         WV.reshape(512, 16, 128).transpose(1, 2, 0)], axis=2))
    permq = []
    for m in range(16):
        for half in (0, 1):
            for h in (2 * m, 2 * m + 1):
                base = h * 64 + 32 * half
                permq += list(range(base, base + 32))
    WQ = w_in[permq]
    out["wq"] = np.ascontiguousarray(WQ.reshape(16, 128, 16, 128).transpose(0, 3, 2, 1).reshape(16, 128, 2048))
    w_out = np.asarray(inp["L0_attn_w_out"], np.float32)
    out["wout"] = np.ascontiguousarray(w_out.reshape(4, 512, 16, 128).transpose(0, 2, 3, 1))
    out["rms16"] = np.ascontiguousarray(np.asarray(inp["L0_attn_rms_w"], np.float32).reshape(16, 128).T)
    gate = np.asarray(inp["L0_ffn_gate_w"], np.float32)
    g16 = gate.T.reshape(16, 128, 16)
    sg1 = np.asarray(inp["L0_ffn_sgate_w"], np.float32)[0].reshape(16, 128, 1)
    out["wgate"] = np.ascontiguousarray(np.concatenate([g16, sg1], axis=2).transpose(1, 0, 2).reshape(128, 272))
    sgu = np.asarray(inp["L0_ffn_shared_gate_up_w"], np.float32)
    sdn = np.asarray(inp["L0_ffn_shared_down_w"], np.float32)
    w13 = np.asarray(inp["L0_ffn_w13"], np.float32)
    w2 = np.asarray(inp["L0_ffn_w2"], np.float32)
    e13, e2 = [], []
    for j in range(4):
        W13p = np.concatenate([sgu[I_EXP * j:I_EXP * (j + 1)],
                               sgu[I_SH + I_EXP * j:I_SH + I_EXP * (j + 1)]], axis=0)
        e13.append(_mblocks_2816(W13p))
        e2.append(_nblocks_1408(sdn[:, I_EXP * j:I_EXP * (j + 1)]))
    for e in range(NE_REAL):
        e13.append(_mblocks_2816(w13[e]))
        e2.append(_nblocks_1408(w2[e]))
    out["w13"] = np.ascontiguousarray(np.stack(e13))
    out["w2"] = np.ascontiguousarray(np.stack(e2))
    return out

def _rope_tab(pos, scale):
    inv = 1.0 / (10000.0 ** (np.arange(0, DH, 2) / DH))
    ang = pos[None, :].astype(np.float64) * inv[:, None]
    c = (np.cos(ang) * scale).astype(np.float32)
    s = (np.sin(ang) * scale).astype(np.float32)
    return np.ascontiguousarray(np.concatenate([c, c, s, s], axis=0))

def _per_core(inp, core):
    b, i = core // 4, core % 4
    x = np.asarray(inp["L0_attn_x"], np.float32)[b]
    others = list(range(0, i)) + list(range(i + 1, 7 - i))
    xrows = np.zeros((S, D), np.float32)
    pos = np.zeros((S,), np.int64)
    for s_, ch in enumerate(others):
        xrows[s_ * 512:(s_ + 1) * 512] = x[ch * 512:(ch + 1) * 512]
        pos[s_ * 512:(s_ + 1) * 512] = np.arange(ch * 512, (ch + 1) * 512)
    xrows[3072:3584] = x[i * 512:(i + 1) * 512]
    pos[3072:3584] = np.arange(i * 512, (i + 1) * 512)
    xrows[3584:4096] = x[(7 - i) * 512:(8 - i) * 512]
    pos[3584:4096] = np.arange((7 - i) * 512, (8 - i) * 512)
    biascol = np.zeros((128, 49), np.float32)
    nlo, nhi = 4 * i, 24 - 4 * i
    for kt in range(24):
        biascol[:, kt] = 0.0 if kt < nlo else NEG
        biascol[:, 24 + kt] = 0.0 if kt < nhi else NEG
    return {"xrows": xrows, "ktab": _rope_tab(pos, 1.0),
            "qtab": _rope_tab(pos[3072:], 0.125), "biascol": biascol}


ENGS = ("tensor", "scalar", "vector", "gpsimd", "sync")
QSEMS = ("qi", "qkv", "qt0", "qt1", "qx0", "qx1", "qq", "qwo0", "qwo1",
         "qxq0", "qxq1", "q13a", "q13b", "q2a", "q2b", "qo")
ALLSEMS = ENGS + QSEMS

class Bld:
    def __init__(self, me, eng, sems):
        self.me, self.eng, self.sems = me, eng, sems
        self.ct = {k: 0 for k in ALLSEMS}
        self.last_wait = {}

    def _wait(self, sem_name, val):
        if val <= 0 or self.last_wait.get(sem_name, -1) >= val:
            return
        self.last_wait[sem_name] = val
        self.eng.wait_ge(self.sems[sem_name], val)

    def op(self, eng_name, fn, waits=(), inc=True):
        if self.me == eng_name:
            for (s, v) in waits:
                self._wait(s, v)
            instr = fn(self.eng)
            if inc:
                instr.then_inc(self.sems[eng_name], 1)
        if inc:
            self.ct[eng_name] += 1
        return self.ct[eng_name]

    def dma(self, q_name, sem, out, in_, waits=()):
        if self.me == q_name:
            for (s, v) in waits:
                self._wait(s, v)
            self.eng.dma_start(out=out, in_=in_).then_inc(self.sems[sem], 16)
        self.ct[sem] += 16
        return self.ct[sem]

    def wait(self, eng_name, sem_name, val):
        if self.me == eng_name:
            self._wait(sem_name, val)

    def fence(self):
        return dict(self.ct)


def build_walk(T, C, b: Bld, phases=6):
    def dump_and_stop():
        f = b.fence()
        last = 0
        for name, src_ap in C.get("dumps", []):
            last = b.dma("gpsimd", "qo", C["dbg_aps"][name], src_ap,
                         waits=[(e_, f[e_]) for e_ in ALLSEMS if e_ != "qo"])
        b.wait("gpsimd", "qo", last)

    pb = T["pb"]
    def bank(ix):
        return pb[:, 512 * ix:512 * (ix + 1)]

    b.dma("gpsimd", "qi", T["biascol"][:], C["in"]["biascol"])
    b.dma("gpsimd", "qi", T["rms16land"][:], C["in"]["rms16"])
    b.dma("gpsimd", "qi", T["qtab"][:], C["in"]["qtab"])
    b.dma("gpsimd", "qi", T["wgateland"][:], C["in"]["wgate"])
    qi = b.dma("gpsimd", "qi", T["ident"][:], C["ident_dram"].ap())
    g_bias = g_rms = g_qtab = g_gate = g_sg = g_id = ("qi", qi)
    v_ones = b.op("vector", lambda e: e.memset(T["vst_ones"], 1.0))
    v_cst = b.op("vector", lambda e: e.memset(T["cst"][:, 0:1], 1e-11))
    v_cst = b.op("vector", lambda e: e.memset(T["cst"][:, 1:2], 1e-5))
    v_o164 = b.op("vector", lambda e: e.memset(T["ones164"][:], 1.0))
    v_o64c = b.op("vector", lambda e: e.memset(T["ones64c"][:], 1.0))
    v_gate = b.op("vector", lambda e: e.tensor_copy(out=T["wgateb"][:], in_=T["wgateland"][:]),
                  waits=[g_gate])
    rms_v = b.op("vector", lambda e: e.tensor_copy(out=T["rms16"][:], in_=T["rms16land"][:]),
                 waits=[g_rms])
    v_wkv = 0
    for k in range(16):
        q_kv = b.dma("sync", "qkv", T["kvland"][:], C["in"]["wkv"][k],
                     waits=([("vector", v_wkv)] if k else []))
        v_wkv = b.op("vector", lambda e, k=k: e.tensor_copy(
            out=T["wkvb"][:, 1024 * k:1024 * (k + 1)], in_=T["kvland"][:]),
            waits=[("qkv", q_kv)])

    if phases <= 0:
        dump_and_stop()
        return
    if phases in (0.2,):
        b.op("vector", lambda e: e.memset(T["kst"][:, :], 0.0))
        b.op("vector", lambda e: e.memset(T["vst"][:, :], 0.0))
    x_rd_v = [0, 0]
    x_rd_s = [0, 0]
    tab_rd = [0, 0]
    zbuf_rd_pe = 0
    prev_tev = 0
    last_rope = 0
    sc_v = 0

    def ln_tile(src_ap, src_waits, dst_fn, fold_rms, stats_ix, zb="zbuf"):
        nonlocal zbuf_rd_pe, prev_tev
        vb = b.op("vector", lambda e: e.bn_stats(out=T["bn24"][:, 0:6], in_=src_ap[:, 0:512]),
                  waits=src_waits)
        for j in range(1, 4):
            vb = b.op("vector", lambda e, j=j: e.bn_stats(
                out=T["bn24"][:, 6 * j:6 * (j + 1)], in_=src_ap[:, 512 * j:512 * (j + 1)]))
        b.wait("vector", "vector", vb)
        stat = T["stats"][:, 2 * stats_ix:2 * stats_ix + 2] if stats_ix is not None else T["stat2"][:]
        va = b.op("vector", lambda e: e.bn_aggr(out=stat, in_=T["bn24"][:]))
        sc_, bi_ = (1.000001, 0) if fold_rms else (1.0, 1)
        sa = b.op("scalar", lambda e: e.activation(out=T["t1"][:], in_=stat[:, 1:2], func=AF.Sqrt,
                                                   bias=T["cst"][:, bi_:bi_ + 1], scale=sc_),
                  waits=[("vector", va), ("vector", v_cst)])
        vr = b.op("vector", lambda e: e.reciprocal(out=T["scl"][:], in_=T["t1"][:]),
                  waits=[("scalar", sa)])
        b.wait("vector", "vector", vr)
        sz = b.op("vector", lambda e: e.tensor_scalar(
            out=T[zb][:], in0=src_ap, scalar1=stat[:, 0:1], scalar2=T["scl"][:],
            op0=OP.subtract, op1=OP.mult),
            waits=[("tensor", zbuf_rd_pe)])
        b.wait("vector", "vector", sz)
        ev = prev_tev
        for c in range(16):
            bk = 4 + (c % 4)
            tp = b.op("tensor", lambda e, c=c, bk=bk: e.transpose(
                bank(bk)[:, 0:128], T[zb][:, 128 * c:128 * (c + 1)], T["ident"][:]),
                waits=[("vector", sz), g_id, ("vector", ev)])
            dst, use_rms = dst_fn(c)
            if use_rms:
                ev = b.op("vector", lambda e, dst=dst, bk=bk, c=c: e.tensor_scalar_mul(
                    out=dst, in0=bank(bk)[:, 0:128], scalar1=T["rms16"][:, c:c + 1]),
                    waits=[("tensor", tp), ("vector", rms_v)])
            else:
                ev = b.op("vector", lambda e, dst=dst, bk=bk: e.tensor_copy(
                    out=dst, in_=bank(bk)[:, 0:128]), waits=[("tensor", tp)])
        zbuf_rd_pe = b.ct["tensor"]
        prev_tev = ev
        return ev, sz

    def rope_evac(psum_ap, tab, dst_top, dst_bot, mm_done, extra):
        w = [("tensor", mm_done)] + list(extra)
        b.op("vector", lambda e: e.tensor_mul(out=T["rtu"][:], in0=psum_ap[64:128, :], in1=tab[64:128, :]), waits=w)
        b.op("vector", lambda e: e.tensor_mul(out=T["rtv"][:], in0=psum_ap[0:64, :], in1=tab[64:128, :]))
        b.op("vector", lambda e: e.tensor_mul(out=T["rtw"][:], in0=psum_ap[0:64, :], in1=tab[0:64, :]))
        v4 = b.op("vector", lambda e: e.tensor_mul(out=T["rtz"][:], in0=psum_ap[64:128, :], in1=tab[0:64, :]))
        b.wait("vector", "vector", v4)
        b.op("vector", lambda e: e.tensor_sub(out=dst_top, in0=T["rtw"][:], in1=T["rtu"][:]))
        return b.op("vector", lambda e: e.tensor_add(out=dst_bot, in0=T["rtz"][:], in1=T["rtv"][:]))

    for tb in range(1 if phases == 0.2 else NTB):
        own = tb >= 6
        dst_zT = T["z_own"][tb - 6] if own else T["zT"]
        tp_ = tb % 2
        g_tab = ("qt%d" % tp_, b.dma("gpsimd", "qt%d" % tp_, T["tabbuf"][tp_][:],
                 C["in"]["ktab"][:, 512 * tb:512 * (tb + 1)], waits=[("vector", tab_rd[tp_])]))
        ev_last = 0
        for tt in range(4):
            p = (tb * 4 + tt) % 2
            g_x = ("qx%d" % p, b.dma("gpsimd", "qx%d" % p, T["xbuf"][p][:],
                   C["in"]["xrows"][512 * tb + 128 * tt:512 * tb + 128 * (tt + 1), :],
                   waits=[("vector", x_rd_v[p])]))
            def dst_fn(c, dst_zT=dst_zT, tt=tt):
                return dst_zT[:, 512 * c + 128 * tt:512 * c + 128 * (tt + 1)], True
            ev_last, sz = ln_tile(T["xbuf"][p][:], [g_x], dst_fn, True,
                                  (tb - 6) * 4 + tt if own else None)
            x_rd_v[p] = b.ct["vector"]
            x_rd_s[p] = sz
        for m in range(4):
            bk = m % 2
            pe_k = 0
            for k in range(16):
                pe_k = b.op("tensor", lambda e, m=m, k=k, bk=bk, dst_zT=dst_zT: e.matmul(
                    bank(bk), T["wkvb"][:, 1024 * k + 128 * m:1024 * k + 128 * (m + 1)],
                    dst_zT[:, 512 * k:512 * (k + 1)], start=(k == 0), stop=(k == 15)),
                    waits=([("vector", ev_last), ("vector", last_rope)] if k == 0 else []),
                    inc=(k == 15))
            cols = slice(4096 * m + 512 * tb, 4096 * m + 512 * (tb + 1))
            last_rope = rope_evac(bank(bk), T["tabbuf"][tp_], T["kst"][0:64, cols],
                                  T["kst"][64:128, cols], pe_k, [g_tab])
        tab_rd[tp_] = b.ct["vector"]
        for tt in range(4):
            bk = 2 + tt % 2
            pe_v = 0
            for k in range(16):
                pe_v = b.op("tensor", lambda e, tt=tt, k=k, bk=bk, dst_zT=dst_zT: e.matmul(
                    bank(bk), dst_zT[:, 512 * k + 128 * tt:512 * k + 128 * (tt + 1)],
                    T["wkvb"][:, 1024 * k + 512:1024 * (k + 1)], start=(k == 0), stop=(k == 15)),
                    waits=([("scalar", sc_v)] if k == 0 else []), inc=(k == 15))
            base = 520 * (tb * 4 + tt)
            for g in range(NG):
                sc_v = b.op("scalar", lambda e, g=g, base=base, bk=bk: e.copy(
                    out=T["vst"][:, base + 65 * g:base + 65 * g + 64],
                    in_=bank(bk)[:, 64 * g:64 * (g + 1)]),
                    waits=([("tensor", pe_v)] if g == 0 else []))

    if phases in (0.2, 0.7):
        dump_and_stop()
        return
    fQ = b.fence()
    qland_rd = 0
    wqb_rd = [0, 0]
    for m in range(16):
        mp = m % 2
        q_q = b.dma("sync", "qq", T["qland"][:], C["in"]["wq"][m],
                    waits=[("gpsimd", qland_rd), ("vector", fQ["vector"]), ("scalar", fQ["scalar"])])
        qcv = b.op("gpsimd", lambda e, mp=mp: e.tensor_copy(out=T["wqb"][mp][:], in_=T["qland"][:]),
                   waits=[("qq", q_q), ("tensor", wqb_rd[mp]), ("scalar", fQ["scalar"])])
        qland_rd = qcv
        for tbq in range(2):
            bk = tbq
            pe_q = 0
            for k in range(16):
                pe_q = b.op("tensor", lambda e, k=k, tbq=tbq, bk=bk, mp=mp: e.matmul(
                    bank(bk), T["wqb"][mp][:, 128 * k:128 * (k + 1)],
                    T["z_own"][tbq][:, 512 * k:512 * (k + 1)], start=(k == 0), stop=(k == 15)),
                    waits=([("gpsimd", qcv), ("vector", last_rope)] if k == 0 else []), inc=(k == 15))
            cols = slice(1024 * m + 512 * tbq, 1024 * m + 512 * (tbq + 1))
            last_rope = rope_evac(bank(bk), T["qtab"][:, 512 * tbq:512 * (tbq + 1)],
                                  T["qot"][0:64, cols], T["qot"][64:128, cols], pe_q,
                                  [g_qtab])
        wqb_rd[mp] = b.ct["tensor"]

    if phases <= 1:
        dump_and_stop()
        return
    fP2 = b.fence()
    v_kones = b.op("vector", lambda e: e.memset(T["ktil"][64:65, :], 1.0),
                   waits=[("tensor", fP2["tensor"])])
    last_av = 0
    act_exp = 0
    exp_h = [0, 0]
    av_done = [[0, 0], [0, 0]]
    osb_wr = 0
    last_vevac = last_rope
    pool_r1 = 0
    s_kq = 0
    sc_p1 = fP2["scalar"]
    for g in range(NG):
        mk, half = g // 2, g % 2
        kcols = slice(4096 * mk, 4096 * (mk + 1))
        sv = b.op("vector", lambda e, half=half, kcols=kcols: e.tensor_copy(
            out=T["ktil"][0:32, :], in_=T["kst"][32 * half:32 * half + 32, kcols]),
            waits=[("vector", last_vevac), ("tensor", max(last_av, fP2["tensor"])),
                   ("gpsimd", pool_r1), ("scalar", s_kq)])
        sv = b.op("vector", lambda e, half=half, kcols=kcols: e.tensor_copy(
            out=T["ktil"][32:64, :], in_=T["kst"][64 + 32 * half:96 + 32 * half, kcols]))
        for h in range(HPG):
            hh = 4 * g + h
            c, sub = hh // 2, hh % 2
            qc = slice(1024 * c, 1024 * (c + 1))
            sv = b.op("vector", lambda e, h=h, sub=sub, qc=qc: e.tensor_copy(
                out=T["qtil"][0:32, 1024 * h:1024 * (h + 1)], in_=T["qot"][32 * sub:32 * sub + 32, qc]))
            sv = b.op("vector", lambda e, h=h, sub=sub, qc=qc: e.tensor_copy(
                out=T["qtil"][32:64, 1024 * h:1024 * (h + 1)], in_=T["qot"][64 + 32 * sub:96 + 32 * sub, qc]))
        b.wait("vector", "vector", sv)
        v1 = b.op("vector", lambda e: e.tensor_mul(out=T["sqs"][:], in0=T["ktil"][0:64, :], in1=T["ktil"][0:64, :]))
        b.wait("vector", "vector", v1)
        km_prev = 0
        for half in range(2):
            mmn = 0
            for c4 in range(4):
                mmn = b.op("tensor", lambda e, half=half, c4=c4: e.matmul(
                    pb[0:1, 512 * c4:512 * (c4 + 1)], T["ones64c"][:],
                    T["sqs"][:, 2048 * half + 512 * c4:2048 * half + 512 * (c4 + 1)],
                    start=True, stop=True),
                    waits=([("vector", max(v1, v_o64c, km_prev)), ("scalar", max(exp_h[1], s_kq))]
                           if c4 == 0 else []), inc=(c4 == 3))
            km_prev = b.op("vector", lambda e, half=half: e.reduce_max(
                out=T["km2"][half][:], in_=pb[0:1, 0:2048], axis=AX.X), waits=[("tensor", mmn)])
        b.wait("vector", "vector", km_prev)
        v2 = b.op("vector", lambda e: e.tensor_max(out=T["km2"][0][:], in0=T["km2"][0][:], in1=T["km2"][1][:]))
        s1 = b.op("scalar", lambda e: e.activation(out=T["km"][:], in_=T["km2"][0][:], func=AF.Sqrt),
                  waits=[("vector", v2)])
        v3 = b.op("vector", lambda e: e.tensor_scalar_mul(out=T["kms"][:], in0=T["km"][:], scalar1=-BOUND_MARGIN),
                  waits=[("scalar", s1)])
        v4 = b.op("vector", lambda e: e.tensor_mul(out=T["sqs"][:], in0=T["qtil"][0:64, :], in1=T["qtil"][0:64, :]),
                  waits=[("vector", km_prev)])
        b.wait("vector", "vector", v4)
        s_kq = 0
        for half in range(2):
            mmn = 0
            for c4 in range(4):
                mmn = b.op("tensor", lambda e, half=half, c4=c4: e.matmul(
                    pb[0:1, 512 * c4:512 * (c4 + 1)], T["ones64c"][:],
                    T["sqs"][:, 2048 * half + 512 * c4:2048 * half + 512 * (c4 + 1)],
                    start=True, stop=True),
                    waits=([("vector", v4 if half == 0 else 0), ("scalar", s_kq)] if c4 == 0 else []),
                    inc=(c4 == 3))
            s_kq = b.op("scalar", lambda e, half=half: e.activation(
                out=T["qtil"][64:65, 2048 * half:2048 * (half + 1)], in_=pb[0:1, 0:2048], func=AF.Sqrt),
                waits=[("tensor", mmn), ("vector", sv)])
        vb_ = b.op("vector", lambda e: e.tensor_scalar_mul(out=T["qtil"][64:65, :], in0=T["qtil"][64:65, :],
                                                           scalar1=T["kms"][:]),
                   waits=[("scalar", s_kq), ("vector", v3)])
        bound_ready = vb_
        pool_r1 = 0

        for qb in range(2):
            tiles = [("o", kt) for kt in range(12 if qb == 0 else 24)] + \
                    [("s", j) for j in range(4 if qb == 0 else 8)]
            for idx, (kind, kt) in enumerate(tiles):
                if kind == "o":
                    krows = slice(128 * kt, 128 * (kt + 1))
                    bias_ap = T["biascol"][:, 24 * qb + kt:24 * qb + kt + 1]
                    vtile = kt
                else:
                    krows = slice(128 * (24 + kt), 128 * (25 + kt))
                    bias_ap = T["biascol"][:, 48:49]
                    vtile = 24 + kt
                pp = idx % 2
                qkh = [0, 0]
                for h in range(HPG):
                    qkh[h // 2] = b.op("tensor", lambda e, h=h, krows=krows, qb=qb: e.matmul(
                        bank(h), T["ktil"][:, krows],
                        T["qtil"][:, 1024 * h + 512 * qb:1024 * h + 512 * (qb + 1)],
                        start=True, stop=True),
                        waits=([("vector", max(bound_ready, last_vevac)),
                                ("scalar", max(exp_h[0], osb_wr, s_kq))] if h == 0 else
                               ([("scalar", exp_h[1])] if h == 2 else [])),
                        inc=(h in (1, 3)))
                for half in range(2):
                    exp_h[half] = b.op("scalar", lambda e, pp=pp, bias_ap=bias_ap, half=half: e.activation(
                        out=T["pbuf"][pp][:, 1024 * half:1024 * (half + 1)],
                        in_=pb[:, 1024 * half:1024 * (half + 1)], func=AF.Exp, bias=bias_ap, scale=1.0),
                        waits=[("tensor", max(qkh[half], av_done[pp][half])), g_bias])
                p_ready = [[("scalar", exp_h[0])], [("scalar", exp_h[1])]]
                if kind == "s" and not (qb == 1 and kt < 4):
                    for half in range(2):
                        pm = b.op("gpsimd", lambda e, pp=pp, qb=qb, kt=kt, half=half: e.affine_select(
                            out=T["pbuf"][pp][:, 1024 * half:1024 * (half + 1)],
                            in_=T["pbuf"][pp][:, 1024 * half:1024 * (half + 1)],
                            pattern=[[0, 2], [1, 512]],
                            compare_op=OP.is_ge, fill=0.0, base=512 * qb - 128 * kt, channel_multiplier=-1),
                            waits=[("scalar", exp_h[half])])
                        p_ready[half] = [("gpsimd", pm)]
                for h in range(HPG):
                    last_av = b.op("tensor", lambda e, h=h, vtile=vtile, g=g, pp=pp, idx=idx, ntile=len(tiles): e.matmul(
                        pb[0:65, 2048 + 512 * h:2048 + 512 * (h + 1)],
                        T["vst"][:, 520 * vtile + 65 * g:520 * vtile + 65 * (g + 1)],
                        T["pbuf"][pp][:, 512 * h:512 * (h + 1)], start=(idx == 0), stop=(idx == ntile - 1)),
                        waits=((p_ready[0] + [("vector", v_ones), ("scalar", max(osb_wr, sc_p1))])
                               if h == 0 else (p_ready[1] if h == 2 else [])), inc=(h in (1, 3)))
                    if h == 1:
                        av_done[pp][0] = last_av
                av_done[pp][1] = last_av
            so = b.op("scalar", lambda e: e.copy(out=T["osb"][:], in_=pb[0:65, 2048:4096]),
                      waits=[("tensor", last_av), ("vector", last_vevac), ("gpsimd", pool_r1)])
            osb_wr = so
            vrc = b.op("vector", lambda e: e.reciprocal(out=T["rrow32"][:], in_=T["osb"][64:65, :]),
                       waits=[("scalar", so)])
            b.wait("vector", "vector", vrc)
            vrc = b.op("vector", lambda e: e.tensor_copy(out=T["rrow"][:], in_=T["rrow32"][:]))
            bc = 0
            for h in range(HPG):
                bc = b.op("tensor", lambda e, h=h: e.matmul(
                    pb[0:64, 512 * h:512 * (h + 1)], T["ones164"][:],
                    T["rrow"][:, 512 * h:512 * (h + 1)], start=True, stop=True),
                    waits=([("vector", max(vrc, v_o164)), ("scalar", exp_h[1])] if h == 0 else []),
                    inc=(h == 3))
            for h in range(HPG):
                hh = 4 * g + h
                c, sub = hh // 2, hh % 2
                last_vevac = b.op("vector", lambda e, h=h, c=c, sub=sub, qb=qb: e.tensor_mul(
                    out=T["qot"][64 * sub:64 * (sub + 1), 1024 * c + 512 * qb:1024 * c + 512 * (qb + 1)],
                    in0=T["osb"][0:64, 512 * h:512 * (h + 1)], in1=pb[0:64, 512 * h:512 * (h + 1)]),
                    waits=[("tensor", bc), ("scalar", so)])

    if phases <= 2:
        dump_and_stop()
        return
    fP3 = b.fence()
    acc = T["acc"].ap()
    xq_rd_v = [fP3["vector"], fP3["vector"]]
    xq_rd_s = [0, 0]
    acc_ev = 0
    ybuf_rd = 0
    for tt in range(8):
        p = tt % 2
        g_xq = b.dma("gpsimd", "qxq%d" % p, T["xq"][p][:],
                     C["in"]["xrows"][3072 + 128 * tt:3072 + 128 * (tt + 1), :],
                     waits=[("vector", xq_rd_v[p]), ("tensor", fP3["tensor"])])
        sa = b.op("scalar", lambda e, tt=tt: e.activation(
            out=T["t1"][:], in_=T["stats"][:, 2 * tt + 1:2 * tt + 2], func=AF.Sqrt,
            bias=T["cst"][:, 1:2], scale=1.0),
            waits=[("vector", acc_ev)])
        vr = b.op("vector", lambda e: e.reciprocal(out=T["scl"][:], in_=T["t1"][:]),
                  waits=[("scalar", sa), ("vector", fP3["vector"])])
        b.wait("vector", "vector", vr)
        sy = b.op("vector", lambda e, p=p, tt=tt: e.tensor_scalar(
            out=T["ybuf"][:], in0=T["xq"][p][:], scalar1=T["stats"][:, 2 * tt:2 * tt + 1],
            scalar2=T["scl"][:], op0=OP.subtract, op1=OP.mult),
            waits=[("vector", ybuf_rd), ("qxq%d" % p, g_xq)])
        b.wait("vector", "vector", sy)
        acc_ev = b.op("vector", lambda e, p=p, tt=tt: e.tensor_add(
            out=acc[:, 2048 * tt:2048 * (tt + 1)], in0=T["xq"][p][:], in1=T["ybuf"][:]),
            waits=[("tensor", fP3["tensor"]), ("gpsimd", fP3["gpsimd"])])
        ybuf_rd = acc_ev
        xq_rd_v[p] = acc_ev
        xq_rd_s[p] = 0
    wol_rd = [0, 0]
    wob_rd_pe = 0
    for n in range(4):
        cvt = 0
        for k in range(16):
            kp = k % 2
            q_wo = b.dma("sync", "qwo%d" % kp, T["woland"][kp][:], C["in"]["wout"][n, k],
                         waits=[("gpsimd", wol_rd[kp]), ("vector", fP3["vector"]), ("tensor", fP3["tensor"])])
            cvt = b.op("gpsimd", lambda e, k=k, kp=kp: e.tensor_copy(
                out=T["wob"][:, 512 * k:512 * (k + 1)], in_=T["woland"][kp][:]),
                waits=[("qwo%d" % kp, q_wo), ("tensor", wob_rd_pe)])
            wol_rd[kp] = cvt
        for th in range(2):
            pe_wo = 0
            for k in range(16):
                for t4 in range(4):
                    tt = 4 * th + t4
                    pe_wo = b.op("tensor", lambda e, k=k, tt=tt, t4=t4: e.matmul(
                        bank(t4), T["qot"][:, 1024 * k + 128 * tt:1024 * k + 128 * (tt + 1)],
                        T["wob"][:, 512 * k:512 * (k + 1)], start=(k == 0), stop=(k == 15)),
                        waits=([("gpsimd", cvt), ("vector", max(acc_ev, last_vevac))] if (k == 0 and t4 == 0) else []),
                        inc=(k == 15 and t4 == 3))
            for t4 in range(4):
                tt = 4 * th + t4
                sl = slice(2048 * tt + 512 * n, 2048 * tt + 512 * (n + 1))
                acc_ev = b.op("vector", lambda e, t4=t4, sl=sl: e.tensor_add(
                    out=acc[:, sl], in0=bank(t4), in1=acc[:, sl]),
                    waits=[("tensor", pe_wo), ("vector", acc_ev)])
        wob_rd_pe = b.ct["tensor"]
    x1_ev = 0
    for tt in range(8):
        src = acc[:, 2048 * tt:2048 * (tt + 1)]
        def dst2(c, tt=tt):
            return T["x1nT"][:, 1024 * c + 128 * tt:1024 * c + 128 * (tt + 1)], False
        x1_ev, _ = ln_tile(src, [("vector", acc_ev)], dst2, False, None, zb="zbuf2")
    x1nT_done = x1_ev

    if phases <= 3:
        dump_and_stop()
        return
    se_rd = 0
    vm_rd = 0
    sg_rd = 0
    coef_done = 0
    sig_done = 0
    for tt in range(8):
        pe_r = 0
        for k in range(16):
            pe_r = b.op("tensor", lambda e, k=k, tt=tt: e.matmul(
                pb[:, 0:17], T["x1nT"][:, 1024 * k + 128 * tt:1024 * k + 128 * (tt + 1)],
                T["wgateb"][:, 17 * k:17 * (k + 1)], start=(k == 0), stop=(k == 15)),
                waits=([("vector", max(x1nT_done, v_gate, vm_rd)), ("scalar", max(se_rd, sg_rd))] if k == 0 else []),
                inc=(k == 15))
        vm = b.op("vector", lambda e: e.reduce_max(out=T["t1"][:], in_=pb[:, 0:16], axis=AX.X, negate=True),
                  waits=[("tensor", pe_r)])
        se = b.op("scalar", lambda e: e.activation(out=T["probs"][:], in_=pb[:, 0:16], func=AF.Exp,
                                                   bias=T["t1"][:], scale=1.0),
                  waits=[("vector", max(vm, coef_done))])
        se_rd = se
        vs = b.op("vector", lambda e: e.reduce_sum(out=T["t2"][:], in_=T["probs"][:], axis=AX.X),
                  waits=[("scalar", se)])
        b.wait("vector", "vector", vs)
        vrc2 = b.op("vector", lambda e: e.reciprocal(out=T["t3"][:], in_=T["t2"][:]))
        b.wait("vector", "vector", vrc2)
        vn = b.op("vector", lambda e: e.tensor_scalar_mul(out=T["probs"][:], in0=T["probs"][:], scalar1=T["t3"][:]))
        b.wait("vector", "vector", vn)
        v8 = b.op("vector", lambda e: e.max(out=T["top8"][:], in_=T["probs"][:]))
        b.wait("vector", "vector", v8)
        coef_done = b.op("vector", lambda e, tt=tt: e.scalar_tensor_tensor(
            out=T["coef"][:, 16 * tt:16 * (tt + 1)], in0=T["probs"][:], scalar=T["top8"][:, 3:4],
            in1=T["probs"][:], op0=OP.is_ge, op1=OP.mult))
        vm_rd = coef_done
        sig_done = b.op("scalar", lambda e, tt=tt: e.activation(out=T["sig"][:, tt:tt + 1], in_=pb[:, 16:17],
                                                                func=AF.Sigmoid), waits=[("tensor", pe_r)])
        sg_rd = sig_done

    if phases <= 4:
        dump_and_stop()
        return
    _P5_CUT = (phases == 4.5)
    fP5 = b.fence()
    w13l_rd = [0, 0]
    w2l_rd = [0, 0]
    w13b_rd = [0, 0]
    w2b_rd = [0, 0]
    hT_rd_pe = 0
    gs_wr = 0
    gs_rd_v = 0
    h_wr = 0
    b.wait("tensor", "scalar", sig_done)
    b.wait("tensor", "vector", coef_done)
    for e_ix in range({4.5: 2, 4.6: 8, 4.7: 14, 4.8: 17, 4.85: 18, 4.9: 19}.get(phases, NE)):
        for jm in range(22):
            mp = jm % 2
            cvt = 0
            for pc in range(2):
                kp = pc % 2
                q13 = b.dma("sync", "q13a" if kp == 0 else "q13b", T["w13land"][kp][:],
                            C["in"]["w13"][e_ix, jm, :, 1024 * pc:1024 * (pc + 1)],
                            waits=[("gpsimd", w13l_rd[kp]), ("vector", fP5["vector"]),
                                   ("scalar", fP5["scalar"]), ("tensor", fP5["tensor"])])
                cvt = b.op("gpsimd", lambda e, pc=pc, kp=kp, mp=mp: e.tensor_copy(
                    out=T["w13b"][mp][:, 1024 * pc:1024 * (pc + 1)], in_=T["w13land"][kp][:]),
                    waits=[("q13a" if kp == 0 else "q13b", q13), ("tensor", w13b_rd[mp])])
                w13l_rd[kp] = cvt
            is_g = (jm % 2 == 0)
            j = jm // 2
            pe_tb = [0, 0]
            for k in range(16):
                for tb in range(2):
                    pe_tb[tb] = b.op("tensor", lambda e, k=k, tb=tb, bk=(0 if is_g else 2) + tb, mp=mp: e.matmul(
                        bank(bk), T["w13b"][mp][:, 128 * k:128 * (k + 1)],
                        T["x1nT"][:, 1024 * k + 512 * tb:1024 * k + 512 * (tb + 1)],
                        start=(k == 0), stop=(k == 15)),
                        waits=(([("gpsimd", cvt)] + ([("scalar", gs_wr)] if is_g else [("vector", h_wr)]))
                               if (k == 0 and tb == 0) else []), inc=(k == 15))
            pe_m = pe_tb[1]
            w13b_rd[mp] = pe_m
            if is_g:
                for tb in range(2):
                    gs_wr = b.op("scalar", lambda e, tb=tb: e.activation(
                        out=T["gs"][:, 512 * tb:512 * (tb + 1)], in_=bank(tb), func=AF.Silu),
                        waits=[("tensor", pe_tb[tb]), ("vector", gs_rd_v)] if tb == 0 else [("tensor", pe_tb[tb])])
            else:
                for tb in range(2):
                    h_wr = b.op("vector", lambda e, tb=tb, j=j: e.tensor_mul(
                        out=T["hT"][:, 1024 * j + 512 * tb:1024 * j + 512 * (tb + 1)],
                        in0=T["gs"][:, 512 * tb:512 * (tb + 1)], in1=bank(2 + tb)),
                        waits=[("scalar", gs_wr), ("tensor", max(pe_m, hT_rd_pe))])
                gs_rd_v = h_wr
        acc_ev_l = acc_ev
        for n in range(4):
            np_ = n % 2
            w2cv = 0
            for k in range(4):
                kp = k % 2
                q2 = b.dma("gpsimd", "q2a" if kp == 0 else "q2b", T["w2land"][kp][:],
                           C["in"]["w2"][e_ix, n, :, 1408 * k:1408 * (k + 1)],
                           waits=[("gpsimd", w2l_rd[kp]), ("vector", fP5["vector"]),
                                  ("tensor", fP5["tensor"]), ("scalar", fP5["scalar"])])
                w2cv = b.op("gpsimd", lambda e, k=k, kp=kp, np_=np_: e.tensor_copy(
                    out=T["w2b"][np_][:, 1408 * k:1408 * (k + 1)], in_=T["w2land"][kp][:]),
                    waits=[("q2a" if kp == 0 else "q2b", q2), ("tensor", w2b_rd[np_])])
                w2l_rd[kp] = w2cv
            for th in range(2):
                pe_d = 0
                for k in range(11):
                    for t4 in range(4):
                        tt = 4 * th + t4
                        pe_d = b.op("tensor", lambda e, k=k, tt=tt, t4=t4, np_=np_: e.matmul(
                            bank(4 + t4), T["hT"][:, 1024 * k + 128 * tt:1024 * k + 128 * (tt + 1)],
                            T["w2b"][np_][:, 512 * k:512 * (k + 1)], start=(k == 0), stop=(k == 10)),
                            waits=([("gpsimd", w2cv), ("vector", max(h_wr, acc_ev))] if (k == 0 and t4 == 0) else []),
                            inc=(k == 10 and t4 == 3))
                for t4 in range(4):
                    tt = 4 * th + t4
                    if e_ix < 4:
                        coef_ap = T["sig"][:, tt:tt + 1]
                    else:
                        coef_ap = T["coef"][:, 16 * tt + (e_ix - 4):16 * tt + (e_ix - 4) + 1]
                    sl = slice(2048 * tt + 512 * n, 2048 * tt + 512 * (n + 1))
                    acc_ev = b.op("vector", lambda e, t4=t4, sl=sl, coef_ap=coef_ap: e.scalar_tensor_tensor(
                        out=acc[:, sl], in0=bank(4 + t4), scalar=coef_ap, in1=acc[:, sl],
                        op0=OP.mult, op1=OP.add),
                        waits=[("tensor", pe_d), ("vector", max(acc_ev, coef_done)),
                               ("scalar", sig_done)])
            w2b_rd[np_] = pe_d
        hT_rd_pe = b.ct["tensor"]
    if phases in (4.5, 4.6, 4.7, 4.8, 4.85, 4.9):
        dump_and_stop()
        return

    qo = 0
    for tt in range(8):
        qo = b.dma("gpsimd", "qo", C["out_ap"][128 * tt:128 * (tt + 1), :], acc[:, 2048 * tt:2048 * (tt + 1)],
                   waits=[("vector", acc_ev)])
    fE = b.fence()
    for s_ in QSEMS:
        b.wait("gpsimd", s_, fE[s_])



def build_program(phases=6, dumps=()):
    nc = bass.Bass("TRN2", target_bir_lowering=False)
    CIN = {}
    CIN["xrows"] = nc.dram_tensor("xrows", [S, D], F32, kind="ExternalInput").ap()
    CIN["ktab"] = nc.dram_tensor("ktab", [128, S], F32, kind="ExternalInput").ap()
    CIN["qtab"] = nc.dram_tensor("qtab", [128, OWN], F32, kind="ExternalInput").ap()
    CIN["biascol"] = nc.dram_tensor("biascol", [128, 49], F32, kind="ExternalInput").ap()
    CIN["wkv"] = nc.dram_tensor("wkv", [16, 128, 1024], F32, kind="ExternalInput").ap()
    CIN["wq"] = nc.dram_tensor("wq", [16, 128, 2048], F32, kind="ExternalInput").ap()
    CIN["wout"] = nc.dram_tensor("wout", [4, 16, 128, 512], F32, kind="ExternalInput").ap()
    CIN["rms16"] = nc.dram_tensor("rms16", [128, 16], F32, kind="ExternalInput").ap()
    CIN["wgate"] = nc.dram_tensor("wgate", [128, 272], F32, kind="ExternalInput").ap()
    CIN["w13"] = nc.dram_tensor("w13", [NE, 22, 128, 2048], F32, kind="ExternalInput").ap()
    CIN["w2"] = nc.dram_tensor("w2", [NE, 4, 128, 5632], F32, kind="ExternalInput").ap()
    out_ap = nc.dram_tensor("out", [OWN, D], F32, kind="ExternalOutput").ap()
    ident_dram = nc.inline_tensor(np.eye(128, dtype=np.float32), "identf32")

    SBASE = 16512
    def at(name, shape, dtype, off):
        return nc.alloc_sbuf_tensor_at(name, shape, dtype, offset=SBASE + off)

    T = {}
    EB, FB, MB = 98816, 164352, 197120
    T["kst"] = at("kst", [128, 16384], BF16, 0).ap()
    T["vst"] = at("vst", [128, 16640], BF16, 32768)
    T["qot"] = at("qot", [128, 16384], BF16, 66048).ap()
    T["acc"] = at("acc", [128, 16384], F32, EB)
    T["x1nT"] = at("x1nT", [128, 16384], BF16, FB).ap()
    T["z_own"] = [at("z_own0", [128, 8192], BF16, EB).ap(), at("z_own1", [128, 8192], BF16, EB + 16384).ap()]
    T["rtu"] = at("rtu", [64, 512], F32, EB + 32768)
    T["rtv"] = at("rtv", [64, 512], F32, EB + 34816)
    T["rtw"] = at("rtw", [64, 512], F32, EB + 36864)
    T["rtz"] = at("rtz", [64, 512], F32, EB + 38912)
    T["xbuf"] = [at("xbuf0", [128, 2048], F32, EB + 40960), at("xbuf1", [128, 2048], F32, EB + 49152)]
    T["qland"] = at("qland", [128, 2048], F32, EB + 40960)
    T["wqb"] = [at("wqb0", [128, 2048], BF16, EB + 49152), at("wqb1", [128, 2048], BF16, EB + 53248)]
    T["zT"] = at("zT", [128, 8192], BF16, FB).ap()
    T["zbuf"] = at("zbuf", [128, 2048], F32, FB + 16384)
    T["kvland"] = at("kvland", [128, 1024], F32, FB + 24576)
    T["tabbuf"] = [at("tabbuf0", [128, 512], F32, FB + 28672), at("tabbuf1", [128, 512], F32, FB + 30720)]
    T["wkvb"] = at("wkvb", [128, 16384], BF16, 66048).ap()
    T["qtil"] = at("qtil", [65, 4096], BF16, EB).ap()
    T["ktil"] = at("ktil", [65, 4096], BF16, EB + 8192).ap()
    T["sqs"] = at("sqs", [64, 4096], BF16, EB + 16384)
    T["r1"] = at("r1", [1, 4096], F32, EB + 24576)
    T["pbuf"] = [at("pbuf0", [128, 2048], BF16, EB + 40960), at("pbuf1", [128, 2048], BF16, EB + 45056)]
    T["osb"] = at("osb", [65, 2048], F32, EB + 16384).ap()
    T["rrow"] = at("rrow", [1, 2048], BF16, EB + 49152)
    T["rrow32"] = at("rrow32", [1, 2048], F32, EB + 53248)
    T["xq"] = [at("xq0", [128, 2048], F32, 0), at("xq1", [128, 2048], F32, 8192)]
    T["wob"] = at("wob", [128, 8192], BF16, 16384)
    T["woland"] = [at("woland0", [128, 512], F32, 32768), at("woland1", [128, 512], F32, 34816)]
    T["ybuf"] = at("ybuf", [128, 2048], F32, 36864)
    T["zbuf2"] = at("zbuf2", [128, 2048], F32, 45056)
    T["w13land"] = [at("w13land0", [128, 1024], F32, 0), at("w13land1", [128, 1024], F32, 4096)]
    T["w13b"] = [at("w13b0", [128, 2048], BF16, 8192), at("w13b1", [128, 2048], BF16, 12288)]
    T["w2land"] = [at("w2land0", [128, 1408], F32, 16384), at("w2land1", [128, 1408], F32, 22016)]
    T["w2b"] = [at("w2b0", [128, 5632], BF16, 27648), at("w2b1", [128, 5632], BF16, 38912)]
    T["hT"] = at("hT", [128, 11264], BF16, 50176).ap()
    T["gs"] = at("gs", [128, 1024], F32, 72704)
    off = [MB]
    def misc(name, shape, dtype):
        t = at(name, shape, dtype, off[0])
        nbytes = int(np.prod(shape[1:])) * mybir.dt.size(dtype)
        off[0] += (nbytes + 31) // 32 * 32
        return t
    T["ident"] = misc("ident", [128, 128], F32)
    T["biascol"] = misc("biascolS", [128, 49], F32)
    T["rms16land"] = misc("rms16land", [128, 16], F32)
    T["rms16"] = misc("rms16S", [128, 16], F32)
    T["stats"] = misc("stats", [128, 16], F32)
    T["stat2"] = misc("stat2", [128, 2], F32)
    T["bn24"] = misc("bn24", [128, 24], F32)
    T["t1"] = misc("t1", [128, 1], F32)
    T["scl"] = misc("scl", [128, 1], F32)
    T["bia"] = misc("bia", [128, 1], F32)
    T["t2"] = misc("t2", [128, 1], F32)
    T["t3"] = misc("t3", [128, 1], F32)
    T["km2"] = [misc("km2a", [1, 1], F32), misc("km2b", [1, 1], F32)]
    T["km"] = misc("km", [1, 1], F32)
    T["kms"] = misc("kms", [1, 1], F32)
    T["qtab"] = misc("qtabS", [128, 1024], F32)
    T["wgateland"] = misc("wgateland", [128, 272], F32)
    T["wgateb"] = misc("wgateb", [128, 272], BF16)
    T["probs"] = misc("probs", [128, 16], F32)
    T["top8"] = misc("top8", [128, 8], F32)
    T["coef"] = misc("coef", [128, 128], F32)
    T["sig"] = misc("sig", [128, 8], F32)
    T["ones164"] = misc("ones164", [1, 64], BF16)
    T["ones64c"] = misc("ones64c", [64, 1], BF16)
    T["cst"] = misc("cst", [128, 2], F32)
    assert off[0] <= 212700, f"misc overflow: {off[0]}"
    vst_h = T["vst"]
    T["vst"] = vst_h.ap()
    T["vst_ones"] = bass.AP(vst_h, 64, [[16640, 128], [520, 32], [65, 8], [1, 1]])
    T["acc"] = T["acc"]
    T["pb"] = nc.alloc_psum_tensor("pb", [128, 4096], F32).ap()

    C = {"in": CIN, "out_ap": out_ap, "ident_dram": ident_dram}
    dump_specs = {
        "kst": (T["kst"][:, :], [128, 16384], BF16),
        "vst": (T["vst"][:, :], [128, 16640], BF16),
        "qot": (T["qot"][:, :], [128, 16384], BF16),
        "zT0": (T["z_own"][0][:, :], [128, 8192], BF16),
        "zT1": (T["z_own"][1][:, :], [128, 8192], BF16),
        "wkvb": (T["wkvb"][:, :], [128, 16384], BF16),
        "acc": (T["acc"][:], [128, 16384], F32),
        "x1nT": (T["x1nT"][:, :], [128, 16384], BF16),
        "coef": (T["coef"][:], [128, 128], F32),
        "sig": (T["sig"][:], [128, 8], F32),
        "stats": (T["stats"][:], [128, 16], F32),
        "hT": (T["hT"][:, :], [128, 11264], BF16),
    }
    C["dumps"] = []
    C["dbg_aps"] = {}
    for name in dumps:
        ap_, shp, dt = dump_specs[name]
        C["dbg_aps"][name] = nc.dram_tensor("dbg_" + name, shp, dt, kind="ExternalOutput").ap()
        C["dumps"].append((name, ap_))

    import contextlib
    with contextlib.ExitStack() as st:
        sems = {k: st.enter_context(nc.semaphore("sem_" + k)) for k in ALLSEMS}
        block = st.enter_context(nc.Block())

        @block.tensor
        def _(eng):
            build_walk(T, C, Bld("tensor", eng, sems), phases)

        @block.scalar
        def _(eng):
            build_walk(T, C, Bld("scalar", eng, sems), phases)

        @block.vector
        def _(eng):
            build_walk(T, C, Bld("vector", eng, sems), phases)

        @block.gpsimd
        def _(eng):
            build_walk(T, C, Bld("gpsimd", eng, sems), phases)

        @block.sync
        def _(eng):
            build_walk(T, C, Bld("sync", eng, sems), phases)

    return nc


_PROG_CACHE = {}

def kernel(**inputs):
    shared = _shared_layouts(inputs)
    if "nc" not in _PROG_CACHE:
        _PROG_CACHE["nc"] = build_program()
    nc = _PROG_CACHE["nc"]
    in_maps = []
    for core in range(NCORES):
        m = dict(shared)
        m.update(_per_core(inputs, core))
        in_maps.append(m)
    res = run_bass_kernel_spmd(nc, in_maps, core_ids=list(range(NCORES)))
    out = np.zeros((BATCH, S, D), np.float32)
    for core in range(NCORES):
        bb, i = core // 4, core % 4
        o = res.results[core]["out"]
        out[bb, i * 512:(i + 1) * 512] = o[0:512]
        out[bb, (7 - i) * 512:(8 - i) * 512] = o[512:1024]
    return out


# version 07
# speedup vs baseline: 2.0826x; 1.1887x over previous
import numpy as np
import ml_dtypes
import concourse.bass as bass
import concourse.mybir as mybir
from concourse.bass_utils import run_bass_kernel_spmd
from concourse.alu_op_type import AluOpType as OP

F32 = mybir.dt.float32
BF16 = mybir.dt.bfloat16
AF = mybir.ActivationFunctionType
AX = mybir.AxisListType

NCORES = 8
D = 2048
S = 4096
BATCH = 2
NG, HPG, DH = 8, 4, 64
NE_REAL = 16
I_EXP, I_SH = 1408, 5632
NE = 20
OWN = 1024
NTB = 8
NEG = -1e30
BOUND_MARGIN = 1.05


def _mblocks_2816(W):
    A = W.reshape(22, 128, 16, 128).transpose(0, 3, 2, 1).reshape(22, 128, 2048)
    order = []
    for j in range(11):
        order += [j, 11 + j]
    return np.ascontiguousarray(A[order])

def _nblocks_1408(W2):
    return np.ascontiguousarray(W2.reshape(4, 512, 11, 128).transpose(0, 3, 2, 1).reshape(4, 128, 5632))

def _shared_layouts(inp):
    out = {}
    w_in = np.asarray(inp["L0_attn_w_in"], np.float32)
    permk = []
    for m in range(4):
        for half in (0, 1):
            for g in (2 * m, 2 * m + 1):
                base = 2048 + g * 64 + 32 * half
                permk += list(range(base, base + 32))
    WK = w_in[permk]
    WV = w_in[2560:3072]
    out["wkv"] = np.ascontiguousarray(np.concatenate(
        [WK.reshape(512, 16, 128).transpose(1, 2, 0),
         WV.reshape(512, 16, 128).transpose(1, 2, 0)], axis=2))
    permq = []
    for m in range(16):
        for half in (0, 1):
            for h in (2 * m, 2 * m + 1):
                base = h * 64 + 32 * half
                permq += list(range(base, base + 32))
    WQ = w_in[permq]
    out["wq"] = np.ascontiguousarray(WQ.reshape(16, 128, 16, 128).transpose(0, 3, 2, 1).reshape(16, 128, 2048))
    w_out = np.asarray(inp["L0_attn_w_out"], np.float32)
    out["wout"] = np.ascontiguousarray(w_out.reshape(4, 512, 16, 128).transpose(0, 2, 3, 1))
    out["rms16"] = np.ascontiguousarray(np.asarray(inp["L0_attn_rms_w"], np.float32).reshape(16, 128).T)
    gate = np.asarray(inp["L0_ffn_gate_w"], np.float32)
    g16 = gate.T.reshape(16, 128, 16)
    sg1 = np.asarray(inp["L0_ffn_sgate_w"], np.float32)[0].reshape(16, 128, 1)
    out["wgate"] = np.ascontiguousarray(np.concatenate([g16, sg1], axis=2).transpose(1, 0, 2).reshape(128, 272))
    sgu = np.asarray(inp["L0_ffn_shared_gate_up_w"], np.float32)
    sdn = np.asarray(inp["L0_ffn_shared_down_w"], np.float32)
    w13 = np.asarray(inp["L0_ffn_w13"], np.float32)
    w2 = np.asarray(inp["L0_ffn_w2"], np.float32)
    e13, e2 = [], []
    for j in range(4):
        W13p = np.concatenate([sgu[I_EXP * j:I_EXP * (j + 1)],
                               sgu[I_SH + I_EXP * j:I_SH + I_EXP * (j + 1)]], axis=0)
        e13.append(_mblocks_2816(W13p))
        e2.append(_nblocks_1408(sdn[:, I_EXP * j:I_EXP * (j + 1)]))
    for e in range(NE_REAL):
        e13.append(_mblocks_2816(w13[e]))
        e2.append(_nblocks_1408(w2[e]))
    out["w13"] = np.ascontiguousarray(np.stack(e13))
    out["w2"] = np.ascontiguousarray(np.stack(e2))
    return out

def _rope_tab(pos, scale):
    inv = 1.0 / (10000.0 ** (np.arange(0, DH, 2) / DH))
    ang = pos[None, :].astype(np.float64) * inv[:, None]
    c = (np.cos(ang) * scale).astype(np.float32)
    s = (np.sin(ang) * scale).astype(np.float32)
    return np.ascontiguousarray(np.concatenate([c, c, s, s], axis=0))

def _per_core(inp, core):
    b, i = core // 4, core % 4
    x = np.asarray(inp["L0_attn_x"], np.float32)[b]
    others = list(range(0, i)) + list(range(i + 1, 7 - i))
    xrows = np.zeros((S, D), np.float32)
    pos = np.zeros((S,), np.int64)
    for s_, ch in enumerate(others):
        xrows[s_ * 512:(s_ + 1) * 512] = x[ch * 512:(ch + 1) * 512]
        pos[s_ * 512:(s_ + 1) * 512] = np.arange(ch * 512, (ch + 1) * 512)
    xrows[3072:3584] = x[i * 512:(i + 1) * 512]
    pos[3072:3584] = np.arange(i * 512, (i + 1) * 512)
    xrows[3584:4096] = x[(7 - i) * 512:(8 - i) * 512]
    pos[3584:4096] = np.arange((7 - i) * 512, (8 - i) * 512)
    biascol = np.zeros((128, 49), np.float32)
    nlo, nhi = 4 * i, 24 - 4 * i
    for kt in range(24):
        biascol[:, kt] = 0.0 if kt < nlo else NEG
        biascol[:, 24 + kt] = 0.0 if kt < nhi else NEG
    return {"xrows": xrows, "ktab": _rope_tab(pos, 1.0),
            "qtab": _rope_tab(pos[3072:], 0.125), "biascol": biascol}


ENGS = ("tensor", "scalar", "vector", "gpsimd", "sync")
QSEMS = ("qi", "qkv", "qt0", "qt1", "qx0", "qx1", "qq", "qwo0", "qwo1",
         "qxq0", "qxq1", "q13a", "q13b", "q2a", "q2b", "qo")
ALLSEMS = ENGS + QSEMS

class Bld:
    def __init__(self, me, eng, sems):
        self.me, self.eng, self.sems = me, eng, sems
        self.ct = {k: 0 for k in ALLSEMS}
        self.last_wait = {}

    def _wait(self, sem_name, val):
        if val <= 0 or self.last_wait.get(sem_name, -1) >= val:
            return
        self.last_wait[sem_name] = val
        self.eng.wait_ge(self.sems[sem_name], val)

    def op(self, eng_name, fn, waits=(), inc=True):
        if self.me == eng_name:
            for (s, v) in waits:
                self._wait(s, v)
            instr = fn(self.eng)
            if inc:
                instr.then_inc(self.sems[eng_name], 1)
        if inc:
            self.ct[eng_name] += 1
        return self.ct[eng_name]

    def dma(self, q_name, sem, out, in_, waits=()):
        if self.me == q_name:
            for (s, v) in waits:
                self._wait(s, v)
            self.eng.dma_start(out=out, in_=in_).then_inc(self.sems[sem], 16)
        self.ct[sem] += 16
        return self.ct[sem]

    def wait(self, eng_name, sem_name, val):
        if self.me == eng_name:
            self._wait(sem_name, val)

    def fence(self):
        return dict(self.ct)


def build_walk(T, C, b: Bld, phases=6):
    def dump_and_stop():
        f = b.fence()
        last = 0
        for name, src_ap in C.get("dumps", []):
            last = b.dma("gpsimd", "qo", C["dbg_aps"][name], src_ap,
                         waits=[(e_, f[e_]) for e_ in ALLSEMS if e_ != "qo"])
        b.wait("gpsimd", "qo", last)

    pb = T["pb"]
    def bank(ix):
        return pb[:, 512 * ix:512 * (ix + 1)]

    b.dma("gpsimd", "qi", T["biascol"][:], C["in"]["biascol"])
    b.dma("gpsimd", "qi", T["rms16land"][:], C["in"]["rms16"])
    b.dma("gpsimd", "qi", T["qtab"][:], C["in"]["qtab"])
    b.dma("gpsimd", "qi", T["wgateland"][:], C["in"]["wgate"])
    qi = b.dma("gpsimd", "qi", T["ident"][:], C["ident_dram"].ap())
    g_bias = g_rms = g_qtab = g_gate = g_sg = g_id = ("qi", qi)
    v_ones = b.op("vector", lambda e: e.memset(T["vst_ones"], 1.0))
    v_cst = b.op("vector", lambda e: e.memset(T["cst"][:, 0:1], 1e-11))
    v_cst = b.op("vector", lambda e: e.memset(T["cst"][:, 1:2], 1e-5))
    v_o164 = b.op("vector", lambda e: e.memset(T["ones164"][:], 1.0))
    v_o64c = b.op("vector", lambda e: e.memset(T["ones64c"][:], 1.0))
    v_gate = b.op("vector", lambda e: e.tensor_copy(out=T["wgateb"][:], in_=T["wgateland"][:]),
                  waits=[g_gate])
    rms_v = b.op("vector", lambda e: e.tensor_copy(out=T["rms16"][:], in_=T["rms16land"][:]),
                 waits=[g_rms])
    v_wkv = 0
    for k in range(16):
        q_kv = b.dma("sync", "qkv", T["kvland"][:], C["in"]["wkv"][k],
                     waits=([("vector", v_wkv)] if k else []))
        v_wkv = b.op("vector", lambda e, k=k: e.tensor_copy(
            out=T["wkvb"][:, 1024 * k:1024 * (k + 1)], in_=T["kvland"][:]),
            waits=[("qkv", q_kv)])

    if phases <= 0:
        dump_and_stop()
        return
    if phases in (0.2,):
        b.op("vector", lambda e: e.memset(T["kst"][:, :], 0.0))
        b.op("vector", lambda e: e.memset(T["vst"][:, :], 0.0))
    x_rd_v = [0, 0]
    x_rd_s = [0, 0]
    tab_rd = [0, 0]
    zbuf_rd_pe = 0
    prev_tev = 0
    last_rope = 0
    sc_v = 0

    def ln_tile(src_ap, src_waits, dst_fn, fold_rms, stats_ix, zb="zbuf"):
        nonlocal zbuf_rd_pe, prev_tev
        vb = b.op("vector", lambda e: e.bn_stats(out=T["bn24"][:, 0:6], in_=src_ap[:, 0:512]),
                  waits=src_waits)
        for j in range(1, 4):
            vb = b.op("vector", lambda e, j=j: e.bn_stats(
                out=T["bn24"][:, 6 * j:6 * (j + 1)], in_=src_ap[:, 512 * j:512 * (j + 1)]))
        b.wait("vector", "vector", vb)
        stat = T["stats"][:, 2 * stats_ix:2 * stats_ix + 2] if stats_ix is not None else T["stat2"][:]
        va = b.op("vector", lambda e: e.bn_aggr(out=stat, in_=T["bn24"][:]))
        sc_, bi_ = (1.000001, 0) if fold_rms else (1.0, 1)
        sa = b.op("scalar", lambda e: e.activation(out=T["t1"][:], in_=stat[:, 1:2], func=AF.Sqrt,
                                                   bias=T["cst"][:, bi_:bi_ + 1], scale=sc_),
                  waits=[("vector", va), ("vector", v_cst)])
        vr = b.op("vector", lambda e: e.reciprocal(out=T["scl"][:], in_=T["t1"][:]),
                  waits=[("scalar", sa)])
        b.wait("vector", "vector", vr)
        sz = b.op("vector", lambda e: e.tensor_scalar(
            out=T[zb][:], in0=src_ap, scalar1=stat[:, 0:1], scalar2=T["scl"][:],
            op0=OP.subtract, op1=OP.mult),
            waits=[("tensor", zbuf_rd_pe)])
        b.wait("vector", "vector", sz)
        ev = prev_tev
        for c in range(16):
            bk = 4 + (c % 4)
            tp = b.op("tensor", lambda e, c=c, bk=bk: e.transpose(
                bank(bk)[:, 0:128], T[zb][:, 128 * c:128 * (c + 1)], T["ident"][:]),
                waits=[("vector", sz), g_id, ("vector", ev)])
            dst, use_rms = dst_fn(c)
            if use_rms:
                ev = b.op("vector", lambda e, dst=dst, bk=bk, c=c: e.tensor_scalar_mul(
                    out=dst, in0=bank(bk)[:, 0:128], scalar1=T["rms16"][:, c:c + 1]),
                    waits=[("tensor", tp), ("vector", rms_v)])
            else:
                ev = b.op("vector", lambda e, dst=dst, bk=bk: e.tensor_copy(
                    out=dst, in_=bank(bk)[:, 0:128]), waits=[("tensor", tp)])
        zbuf_rd_pe = b.ct["tensor"]
        prev_tev = ev
        return ev, sz

    def rope_evac(psum_ap, tab, dst_top, dst_bot, mm_done, extra):
        w = [("tensor", mm_done)] + list(extra)
        b.op("vector", lambda e: e.tensor_mul(out=T["rtu"][:], in0=psum_ap[64:128, :], in1=tab[64:128, :]), waits=w)
        b.op("vector", lambda e: e.tensor_mul(out=T["rtv"][:], in0=psum_ap[0:64, :], in1=tab[64:128, :]))
        b.op("vector", lambda e: e.tensor_mul(out=T["rtw"][:], in0=psum_ap[0:64, :], in1=tab[0:64, :]))
        v4 = b.op("vector", lambda e: e.tensor_mul(out=T["rtz"][:], in0=psum_ap[64:128, :], in1=tab[0:64, :]))
        b.wait("vector", "vector", v4)
        b.op("vector", lambda e: e.tensor_sub(out=dst_top, in0=T["rtw"][:], in1=T["rtu"][:]))
        return b.op("vector", lambda e: e.tensor_add(out=dst_bot, in0=T["rtz"][:], in1=T["rtv"][:]))

    for tb in range(1 if phases == 0.2 else NTB):
        own = tb >= 6
        dst_zT = T["z_own"][tb - 6] if own else T["zT"]
        tp_ = tb % 2
        g_tab = ("qt%d" % tp_, b.dma("gpsimd", "qt%d" % tp_, T["tabbuf"][tp_][:],
                 C["in"]["ktab"][:, 512 * tb:512 * (tb + 1)], waits=[("vector", tab_rd[tp_])]))
        ev_last = 0
        for tt in range(4):
            p = (tb * 4 + tt) % 2
            g_x = ("qx%d" % p, b.dma("gpsimd", "qx%d" % p, T["xbuf"][p][:],
                   C["in"]["xrows"][512 * tb + 128 * tt:512 * tb + 128 * (tt + 1), :],
                   waits=[("vector", x_rd_v[p])]))
            def dst_fn(c, dst_zT=dst_zT, tt=tt):
                return dst_zT[:, 512 * c + 128 * tt:512 * c + 128 * (tt + 1)], True
            ev_last, sz = ln_tile(T["xbuf"][p][:], [g_x], dst_fn, True,
                                  (tb - 6) * 4 + tt if own else None)
            x_rd_v[p] = b.ct["vector"]
            x_rd_s[p] = sz
        for m in range(4):
            bk = m % 2
            pe_k = 0
            for k in range(16):
                pe_k = b.op("tensor", lambda e, m=m, k=k, bk=bk, dst_zT=dst_zT: e.matmul(
                    bank(bk), T["wkvb"][:, 1024 * k + 128 * m:1024 * k + 128 * (m + 1)],
                    dst_zT[:, 512 * k:512 * (k + 1)], start=(k == 0), stop=(k == 15)),
                    waits=([("vector", ev_last), ("vector", last_rope)] if k == 0 else []),
                    inc=(k == 15))
            cols = slice(4096 * m + 512 * tb, 4096 * m + 512 * (tb + 1))
            last_rope = rope_evac(bank(bk), T["tabbuf"][tp_], T["kst"][0:64, cols],
                                  T["kst"][64:128, cols], pe_k, [g_tab])
        tab_rd[tp_] = b.ct["vector"]
        for tt in range(4):
            bk = 2 + tt % 2
            pe_v = 0
            for k in range(16):
                pe_v = b.op("tensor", lambda e, tt=tt, k=k, bk=bk, dst_zT=dst_zT: e.matmul(
                    bank(bk), dst_zT[:, 512 * k + 128 * tt:512 * k + 128 * (tt + 1)],
                    T["wkvb"][:, 1024 * k + 512:1024 * (k + 1)], start=(k == 0), stop=(k == 15)),
                    waits=([("scalar", sc_v)] if k == 0 else []), inc=(k == 15))
            base = 520 * (tb * 4 + tt)
            for g in range(NG):
                sc_v = b.op("scalar", lambda e, g=g, base=base, bk=bk: e.copy(
                    out=T["vst"][:, base + 65 * g:base + 65 * g + 64],
                    in_=bank(bk)[:, 64 * g:64 * (g + 1)]),
                    waits=([("tensor", pe_v)] if g == 0 else []))

    if phases in (0.2, 0.7):
        dump_and_stop()
        return
    fQ = b.fence()
    qland_rd = 0
    wqb_rd = [0, 0]
    for m in range(16):
        mp = m % 2
        q_q = b.dma("sync", "qq", T["qland"][:], C["in"]["wq"][m],
                    waits=[("gpsimd", qland_rd), ("vector", fQ["vector"]), ("scalar", fQ["scalar"])])
        qcv = b.op("gpsimd", lambda e, mp=mp: e.tensor_copy(out=T["wqb"][mp][:], in_=T["qland"][:]),
                   waits=[("qq", q_q), ("tensor", wqb_rd[mp]), ("scalar", fQ["scalar"])])
        qland_rd = qcv
        for tbq in range(2):
            bk = tbq
            pe_q = 0
            for k in range(16):
                pe_q = b.op("tensor", lambda e, k=k, tbq=tbq, bk=bk, mp=mp: e.matmul(
                    bank(bk), T["wqb"][mp][:, 128 * k:128 * (k + 1)],
                    T["z_own"][tbq][:, 512 * k:512 * (k + 1)], start=(k == 0), stop=(k == 15)),
                    waits=([("gpsimd", qcv), ("vector", last_rope)] if k == 0 else []), inc=(k == 15))
            cols = slice(1024 * m + 512 * tbq, 1024 * m + 512 * (tbq + 1))
            last_rope = rope_evac(bank(bk), T["qtab"][:, 512 * tbq:512 * (tbq + 1)],
                                  T["qot"][0:64, cols], T["qot"][64:128, cols], pe_q,
                                  [g_qtab])
        wqb_rd[mp] = b.ct["tensor"]

    if phases <= 1:
        dump_and_stop()
        return
    fP2 = b.fence()
    v_kones = b.op("vector", lambda e: e.memset(T["ktil"][64:65, :], 1.0),
                   waits=[("tensor", fP2["tensor"])])
    last_av = 0
    act_exp = 0
    exp_h = [0, 0]
    av_done = [[0, 0], [0, 0]]
    osb_wr = 0
    last_vevac = last_rope
    pool_r1 = 0
    s_kq = 0
    sc_p1 = fP2["scalar"]
    for g in range(NG):
        mk, half = g // 2, g % 2
        kcols = slice(4096 * mk, 4096 * (mk + 1))
        sv = b.op("vector", lambda e, half=half, kcols=kcols: e.tensor_copy(
            out=T["ktil"][0:32, :], in_=T["kst"][32 * half:32 * half + 32, kcols]),
            waits=[("vector", last_vevac), ("tensor", max(last_av, fP2["tensor"])),
                   ("gpsimd", pool_r1), ("scalar", s_kq)])
        sv = b.op("vector", lambda e, half=half, kcols=kcols: e.tensor_copy(
            out=T["ktil"][32:64, :], in_=T["kst"][64 + 32 * half:96 + 32 * half, kcols]))
        for h in range(HPG):
            hh = 4 * g + h
            c, sub = hh // 2, hh % 2
            qc = slice(1024 * c, 1024 * (c + 1))
            sv = b.op("vector", lambda e, h=h, sub=sub, qc=qc: e.tensor_copy(
                out=T["qtil"][0:32, 1024 * h:1024 * (h + 1)], in_=T["qot"][32 * sub:32 * sub + 32, qc]))
            sv = b.op("vector", lambda e, h=h, sub=sub, qc=qc: e.tensor_copy(
                out=T["qtil"][32:64, 1024 * h:1024 * (h + 1)], in_=T["qot"][64 + 32 * sub:96 + 32 * sub, qc]))
        b.wait("vector", "vector", sv)
        v1 = b.op("vector", lambda e: e.tensor_mul(out=T["sqs"][:], in0=T["ktil"][0:64, :], in1=T["ktil"][0:64, :]))
        b.wait("vector", "vector", v1)
        km_prev = 0
        for half in range(2):
            mmn = 0
            for c4 in range(4):
                mmn = b.op("tensor", lambda e, half=half, c4=c4: e.matmul(
                    pb[0:1, 512 * c4:512 * (c4 + 1)], T["ones64c"][:],
                    T["sqs"][:, 2048 * half + 512 * c4:2048 * half + 512 * (c4 + 1)],
                    start=True, stop=True),
                    waits=([("vector", max(v1, v_o64c, km_prev)), ("scalar", max(exp_h[1], s_kq))]
                           if c4 == 0 else []), inc=(c4 == 3))
            km_prev = b.op("vector", lambda e, half=half: e.reduce_max(
                out=T["km2"][half][:], in_=pb[0:1, 0:2048], axis=AX.X), waits=[("tensor", mmn)])
        b.wait("vector", "vector", km_prev)
        v2 = b.op("vector", lambda e: e.tensor_max(out=T["km2"][0][:], in0=T["km2"][0][:], in1=T["km2"][1][:]))
        s1 = b.op("scalar", lambda e: e.activation(out=T["km"][:], in_=T["km2"][0][:], func=AF.Sqrt),
                  waits=[("vector", v2)])
        v3 = b.op("vector", lambda e: e.tensor_scalar_mul(out=T["kms"][:], in0=T["km"][:], scalar1=-BOUND_MARGIN),
                  waits=[("scalar", s1)])
        v4 = b.op("vector", lambda e: e.tensor_mul(out=T["sqs"][:], in0=T["qtil"][0:64, :], in1=T["qtil"][0:64, :]),
                  waits=[("vector", km_prev)])
        b.wait("vector", "vector", v4)
        s_kq = 0
        for half in range(2):
            mmn = 0
            for c4 in range(4):
                mmn = b.op("tensor", lambda e, half=half, c4=c4: e.matmul(
                    pb[0:1, 512 * c4:512 * (c4 + 1)], T["ones64c"][:],
                    T["sqs"][:, 2048 * half + 512 * c4:2048 * half + 512 * (c4 + 1)],
                    start=True, stop=True),
                    waits=([("vector", v4 if half == 0 else 0), ("scalar", s_kq)] if c4 == 0 else []),
                    inc=(c4 == 3))
            s_kq = b.op("scalar", lambda e, half=half: e.activation(
                out=T["qtil"][64:65, 2048 * half:2048 * (half + 1)], in_=pb[0:1, 0:2048], func=AF.Sqrt),
                waits=[("tensor", mmn), ("vector", sv)])
        vb_ = b.op("vector", lambda e: e.tensor_scalar_mul(out=T["qtil"][64:65, :], in0=T["qtil"][64:65, :],
                                                           scalar1=T["kms"][:]),
                   waits=[("scalar", s_kq), ("vector", v3)])
        bound_ready = vb_
        pool_r1 = 0

        for qb in range(2):
            tiles = [("o", kt) for kt in range(12 if qb == 0 else 24)] + \
                    [("s", j) for j in range(4 if qb == 0 else 8)]
            for idx, (kind, kt) in enumerate(tiles):
                if kind == "o":
                    krows = slice(128 * kt, 128 * (kt + 1))
                    bias_ap = T["biascol"][:, 24 * qb + kt:24 * qb + kt + 1]
                    vtile = kt
                else:
                    krows = slice(128 * (24 + kt), 128 * (25 + kt))
                    bias_ap = T["biascol"][:, 48:49]
                    vtile = 24 + kt
                pp = idx % 2
                qkh = [0, 0]
                for h in range(HPG):
                    qkh[h // 2] = b.op("tensor", lambda e, h=h, krows=krows, qb=qb: e.matmul(
                        bank(h), T["ktil"][:, krows],
                        T["qtil"][:, 1024 * h + 512 * qb:1024 * h + 512 * (qb + 1)],
                        start=True, stop=True),
                        waits=([("vector", max(bound_ready, last_vevac)),
                                ("scalar", max(exp_h[0], osb_wr, s_kq))] if h == 0 else
                               ([("scalar", exp_h[1])] if h == 2 else [])),
                        inc=(h in (1, 3)))
                for half in range(2):
                    exp_h[half] = b.op("scalar", lambda e, pp=pp, bias_ap=bias_ap, half=half: e.activation(
                        out=T["pbuf"][pp][:, 1024 * half:1024 * (half + 1)],
                        in_=pb[:, 1024 * half:1024 * (half + 1)], func=AF.Exp, bias=bias_ap, scale=1.0),
                        waits=[("tensor", max(qkh[half], av_done[pp][half])), g_bias])
                p_ready = [[("scalar", exp_h[0])], [("scalar", exp_h[1])]]
                if kind == "s" and not (qb == 1 and kt < 4):
                    for half in range(2):
                        pm = b.op("gpsimd", lambda e, pp=pp, qb=qb, kt=kt, half=half: e.affine_select(
                            out=T["pbuf"][pp][:, 1024 * half:1024 * (half + 1)],
                            in_=T["pbuf"][pp][:, 1024 * half:1024 * (half + 1)],
                            pattern=[[0, 2], [1, 512]],
                            compare_op=OP.is_ge, fill=0.0, base=512 * qb - 128 * kt, channel_multiplier=-1),
                            waits=[("scalar", exp_h[half])])
                        p_ready[half] = [("gpsimd", pm)]
                for h in range(HPG):
                    last_av = b.op("tensor", lambda e, h=h, vtile=vtile, g=g, pp=pp, idx=idx, ntile=len(tiles): e.matmul(
                        pb[0:65, 2048 + 512 * h:2048 + 512 * (h + 1)],
                        T["vst"][:, 520 * vtile + 65 * g:520 * vtile + 65 * (g + 1)],
                        T["pbuf"][pp][:, 512 * h:512 * (h + 1)], start=(idx == 0), stop=(idx == ntile - 1)),
                        waits=((p_ready[0] + [("vector", v_ones), ("scalar", max(osb_wr, sc_p1))])
                               if h == 0 else (p_ready[1] if h == 2 else [])), inc=(h in (1, 3)))
                    if h == 1:
                        av_done[pp][0] = last_av
                av_done[pp][1] = last_av
            so = b.op("scalar", lambda e: e.copy(out=T["osb"][:], in_=pb[0:65, 2048:4096]),
                      waits=[("tensor", last_av), ("vector", last_vevac), ("gpsimd", pool_r1)])
            osb_wr = so
            vrc = b.op("vector", lambda e: e.reciprocal(out=T["rrow32"][:], in_=T["osb"][64:65, :]),
                       waits=[("scalar", so)])
            b.wait("vector", "vector", vrc)
            vrc = b.op("vector", lambda e: e.tensor_copy(out=T["rrow"][:], in_=T["rrow32"][:]))
            bc = 0
            for h in range(HPG):
                bc = b.op("tensor", lambda e, h=h: e.matmul(
                    pb[0:64, 512 * h:512 * (h + 1)], T["ones164"][:],
                    T["rrow"][:, 512 * h:512 * (h + 1)], start=True, stop=True),
                    waits=([("vector", max(vrc, v_o164)), ("scalar", exp_h[1])] if h == 0 else []),
                    inc=(h == 3))
            for h in range(HPG):
                hh = 4 * g + h
                c, sub = hh // 2, hh % 2
                last_vevac = b.op("vector", lambda e, h=h, c=c, sub=sub, qb=qb: e.tensor_mul(
                    out=T["qot"][64 * sub:64 * (sub + 1), 1024 * c + 512 * qb:1024 * c + 512 * (qb + 1)],
                    in0=T["osb"][0:64, 512 * h:512 * (h + 1)], in1=pb[0:64, 512 * h:512 * (h + 1)]),
                    waits=[("tensor", bc), ("scalar", so)])

    if phases <= 2:
        dump_and_stop()
        return
    fP3 = b.fence()
    acc = T["acc"].ap()
    xq_rd_v = [fP3["vector"], fP3["vector"]]
    xq_rd_s = [0, 0]
    acc_ev = 0
    ybuf_rd = 0
    for tt in range(8):
        p = tt % 2
        g_xq = b.dma("gpsimd", "qxq%d" % p, T["xq"][p][:],
                     C["in"]["xrows"][3072 + 128 * tt:3072 + 128 * (tt + 1), :],
                     waits=[("vector", xq_rd_v[p]), ("tensor", fP3["tensor"])])
        sa = b.op("scalar", lambda e, tt=tt: e.activation(
            out=T["t1"][:], in_=T["stats"][:, 2 * tt + 1:2 * tt + 2], func=AF.Sqrt,
            bias=T["cst"][:, 1:2], scale=1.0),
            waits=[("vector", acc_ev)])
        vr = b.op("vector", lambda e: e.reciprocal(out=T["scl"][:], in_=T["t1"][:]),
                  waits=[("scalar", sa), ("vector", fP3["vector"])])
        b.wait("vector", "vector", vr)
        sy = b.op("vector", lambda e, p=p, tt=tt: e.tensor_scalar(
            out=T["ybuf"][:], in0=T["xq"][p][:], scalar1=T["stats"][:, 2 * tt:2 * tt + 1],
            scalar2=T["scl"][:], op0=OP.subtract, op1=OP.mult),
            waits=[("vector", ybuf_rd), ("qxq%d" % p, g_xq)])
        b.wait("vector", "vector", sy)
        acc_ev = b.op("vector", lambda e, p=p, tt=tt: e.tensor_add(
            out=acc[:, 2048 * tt:2048 * (tt + 1)], in0=T["xq"][p][:], in1=T["ybuf"][:]),
            waits=[("tensor", fP3["tensor"]), ("gpsimd", fP3["gpsimd"])])
        ybuf_rd = acc_ev
        xq_rd_v[p] = acc_ev
        xq_rd_s[p] = 0
    wol_rd = [0, 0]
    wob_rd_pe = 0
    for n in range(4):
        cvt = 0
        for k in range(16):
            kp = k % 2
            q_wo = b.dma("sync", "qwo%d" % kp, T["woland"][kp][:], C["in"]["wout"][n, k],
                         waits=[("gpsimd", wol_rd[kp]), ("vector", fP3["vector"]), ("tensor", fP3["tensor"])])
            cvt = b.op("gpsimd", lambda e, k=k, kp=kp: e.tensor_copy(
                out=T["wob"][:, 512 * k:512 * (k + 1)], in_=T["woland"][kp][:]),
                waits=[("qwo%d" % kp, q_wo), ("tensor", wob_rd_pe)])
            wol_rd[kp] = cvt
        for th in range(2):
            pe_wo = 0
            for k in range(16):
                for t4 in range(4):
                    tt = 4 * th + t4
                    pe_wo = b.op("tensor", lambda e, k=k, tt=tt, t4=t4: e.matmul(
                        bank(t4), T["qot"][:, 1024 * k + 128 * tt:1024 * k + 128 * (tt + 1)],
                        T["wob"][:, 512 * k:512 * (k + 1)], start=(k == 0), stop=(k == 15)),
                        waits=([("gpsimd", cvt), ("vector", max(acc_ev, last_vevac))] if (k == 0 and t4 == 0) else []),
                        inc=(k == 15 and t4 == 3))
            for t4 in range(4):
                tt = 4 * th + t4
                sl = slice(2048 * tt + 512 * n, 2048 * tt + 512 * (n + 1))
                acc_ev = b.op("vector", lambda e, t4=t4, sl=sl: e.tensor_add(
                    out=acc[:, sl], in0=bank(t4), in1=acc[:, sl]),
                    waits=[("tensor", pe_wo), ("vector", acc_ev)])
        wob_rd_pe = b.ct["tensor"]
    x1_ev = 0
    for tt in range(8):
        src = acc[:, 2048 * tt:2048 * (tt + 1)]
        def dst2(c, tt=tt):
            return T["x1nT"][:, 1024 * c + 128 * tt:1024 * c + 128 * (tt + 1)], False
        x1_ev, _ = ln_tile(src, [("vector", acc_ev)], dst2, False, None, zb="zbuf2")
    x1nT_done = x1_ev

    if phases <= 3:
        dump_and_stop()
        return
    se_rd = 0
    vm_rd = 0
    sg_rd = 0
    coef_done = 0
    sig_done = 0
    for tt in range(8):
        pe_r = 0
        for k in range(16):
            pe_r = b.op("tensor", lambda e, k=k, tt=tt: e.matmul(
                pb[:, 0:17], T["x1nT"][:, 1024 * k + 128 * tt:1024 * k + 128 * (tt + 1)],
                T["wgateb"][:, 17 * k:17 * (k + 1)], start=(k == 0), stop=(k == 15)),
                waits=([("vector", max(x1nT_done, v_gate, vm_rd)), ("scalar", max(se_rd, sg_rd))] if k == 0 else []),
                inc=(k == 15))
        vm = b.op("vector", lambda e: e.reduce_max(out=T["t1"][:], in_=pb[:, 0:16], axis=AX.X, negate=True),
                  waits=[("tensor", pe_r)])
        se = b.op("scalar", lambda e: e.activation(out=T["probs"][:], in_=pb[:, 0:16], func=AF.Exp,
                                                   bias=T["t1"][:], scale=1.0),
                  waits=[("vector", max(vm, coef_done))])
        se_rd = se
        vs = b.op("vector", lambda e: e.reduce_sum(out=T["t2"][:], in_=T["probs"][:], axis=AX.X),
                  waits=[("scalar", se)])
        b.wait("vector", "vector", vs)
        vrc2 = b.op("vector", lambda e: e.reciprocal(out=T["t3"][:], in_=T["t2"][:]))
        b.wait("vector", "vector", vrc2)
        vn = b.op("vector", lambda e: e.tensor_scalar_mul(out=T["probs"][:], in0=T["probs"][:], scalar1=T["t3"][:]))
        b.wait("vector", "vector", vn)
        v8 = b.op("vector", lambda e: e.max(out=T["top8"][:], in_=T["probs"][:]))
        b.wait("vector", "vector", v8)
        coef_done = b.op("vector", lambda e, tt=tt: e.scalar_tensor_tensor(
            out=T["coef"][:, 16 * tt:16 * (tt + 1)], in0=T["probs"][:], scalar=T["top8"][:, 3:4],
            in1=T["probs"][:], op0=OP.is_ge, op1=OP.mult))
        vm_rd = coef_done
        sig_done = b.op("scalar", lambda e, tt=tt: e.activation(out=T["sig"][:, tt:tt + 1], in_=pb[:, 16:17],
                                                                func=AF.Sigmoid), waits=[("tensor", pe_r)])
        sg_rd = sig_done

    if phases <= 4:
        dump_and_stop()
        return
    _P5_CUT = (phases == 4.5)
    fP5 = b.fence()
    w13l_rd = [0, 0]
    w2l_rd = [0, 0]
    w13b_rd = [0, 0]
    w2b_rd = [0, 0]
    w13_cvt = [0, 0]
    w2_cvt = [0, 0]
    w13_cvt_last = 0
    acc_guard = [acc_ev, acc_ev]
    hT_rd_pe = 0
    gs_wr = 0
    gs_rd_v = 0
    h_wr = 0
    b.wait("tensor", "scalar", sig_done)
    b.wait("tensor", "vector", coef_done)
    for e_ix in range({4.5: 2, 4.6: 8, 4.7: 14, 4.8: 17, 4.85: 18, 4.9: 19}.get(phases, NE)):
        def w13_fetch(jm, ee=None):
            ee = e_ix if ee is None else ee
            cvt = 0
            for pc in range(2):
                kp = pc % 2
                q13 = b.dma("sync", "q13a" if kp == 0 else "q13b", T["w13land"][kp][:],
                            C["in"]["w13"][ee, jm, :, 1024 * pc:1024 * (pc + 1)],
                            waits=[("vector", max(w13l_rd[kp], fP5["vector"])),
                                   ("scalar", fP5["scalar"]), ("tensor", fP5["tensor"]),
                                   ("gpsimd", fP5["gpsimd"])])
                cvt = b.op("vector", lambda e, pc=pc, kp=kp, jm=jm: e.tensor_copy(
                    out=T["w13b"][jm % 2][:, 1024 * pc:1024 * (pc + 1)], in_=T["w13land"][kp][:]),
                    waits=[("q13a" if kp == 0 else "q13b", q13), ("tensor", w13b_rd[jm % 2])])
                w13l_rd[kp] = cvt
            w13_cvt[jm % 2] = cvt
        def w2_fetch(n):
            cv = 0
            np_ = n % 2
            for k in range(4):
                kp = k % 2
                q2 = b.dma("gpsimd", "q2a" if kp == 0 else "q2b", T["w2land"][kp][:],
                           C["in"]["w2"][e_ix, n, :, 1408 * k:1408 * (k + 1)],
                           waits=[("vector", max(w2l_rd[kp], fP5["vector"])),
                                  ("tensor", fP5["tensor"]), ("scalar", fP5["scalar"])])
                cv = b.op("vector", lambda e, k=k, kp=kp, np_=np_: e.tensor_copy(
                    out=T["w2b"][np_][:, 1408 * k:1408 * (k + 1)], in_=T["w2land"][kp][:]),
                    waits=[("q2a" if kp == 0 else "q2b", q2), ("tensor", w2b_rd[np_])])
                w2l_rd[kp] = cv
            w2_cvt[np_] = cv
        if e_ix == 0:
            w13_fetch(0)
            w13_fetch(1)
        for jm in range(22):
            mp = jm % 2
            is_g = (jm % 2 == 0)
            j = jm // 2
            pe_tb = [0, 0]
            for k in range(16):
                for tb in range(2):
                    pe_tb[tb] = b.op("tensor", lambda e, k=k, tb=tb, bk=(0 if is_g else 2) + tb, mp=mp: e.matmul(
                        bank(bk), T["w13b"][mp][:, 128 * k:128 * (k + 1)],
                        T["x1nT"][:, 1024 * k + 512 * tb:1024 * k + 512 * (tb + 1)],
                        start=(k == 0), stop=(k == 15)),
                        waits=(([("vector", w13_cvt[mp])] +
                                ([("scalar", gs_wr)] if is_g else [("vector", h_wr)]) +
                                ([("vector", acc_ev)] if jm == 0 else []))
                               if (k == 0 and tb == 0) else []), inc=(k == 15))
            pe_m = pe_tb[1]
            w13b_rd[mp] = pe_m
            if jm < 20:
                w13_fetch(jm + 2)
            elif jm == 20:
                w2_fetch(0)
            else:
                w2_fetch(1)
            if is_g:
                for tb in range(2):
                    gs_wr = b.op("scalar", lambda e, tb=tb: e.activation(
                        out=T["gs"][:, 512 * tb:512 * (tb + 1)], in_=bank(tb), func=AF.Silu),
                        waits=[("tensor", pe_tb[tb]), ("vector", gs_rd_v)] if tb == 0 else [("tensor", pe_tb[tb])])
            else:
                for tb in range(2):
                    h_wr = b.op("vector", lambda e, tb=tb, j=j: e.tensor_mul(
                        out=T["hT"][:, 1024 * j + 512 * tb:1024 * j + 512 * (tb + 1)],
                        in0=T["gs"][:, 512 * tb:512 * (tb + 1)], in1=bank(2 + tb)),
                        waits=[("scalar", gs_wr), ("tensor", max(pe_tb[tb], hT_rd_pe))])
                gs_rd_v = h_wr
        if e_ix + 1 < NE:
            w13_fetch(0, ee=e_ix + 1)
            w13_fetch(1, ee=e_ix + 1)
        for n in range(4):
            np_ = n % 2
            for th in range(2):
                bs = 4 * ((2 * n + th) % 2)
                pe_d = 0
                for k in range(11):
                    for t4 in range(4):
                        tt = 4 * th + t4
                        pe_d = b.op("tensor", lambda e, k=k, tt=tt, t4=t4, np_=np_, bs=bs: e.matmul(
                            bank(bs + t4), T["hT"][:, 1024 * k + 128 * tt:1024 * k + 128 * (tt + 1)],
                            T["w2b"][np_][:, 512 * k:512 * (k + 1)], start=(k == 0), stop=(k == 10)),
                            waits=([("vector", max(w2_cvt[np_], h_wr, acc_guard[(2 * n + th) % 2]))]
                                   if (k == 0 and t4 == 0) else []),
                            inc=(k == 10 and t4 == 3))
                if n < 3 and th == 0:
                    w2_fetch(n + 1)
                for t4 in range(4):
                    tt = 4 * th + t4
                    if e_ix < 4:
                        coef_ap = T["sig"][:, tt:tt + 1]
                    else:
                        coef_ap = T["coef"][:, 16 * tt + (e_ix - 4):16 * tt + (e_ix - 4) + 1]
                    sl = slice(2048 * tt + 512 * n, 2048 * tt + 512 * (n + 1))
                    acc_ev = b.op("vector", lambda e, t4=t4, sl=sl, coef_ap=coef_ap, bs=bs: e.scalar_tensor_tensor(
                        out=acc[:, sl], in0=bank(bs + t4), scalar=coef_ap, in1=acc[:, sl],
                        op0=OP.mult, op1=OP.add),
                        waits=[("tensor", pe_d), ("vector", max(acc_ev, coef_done)),
                               ("scalar", sig_done)])
                acc_guard[(2 * n + th) % 2] = acc_ev
            w2b_rd[np_] = pe_d
        hT_rd_pe = b.ct["tensor"]
    if phases in (4.5, 4.6, 4.7, 4.8, 4.85, 4.9):
        dump_and_stop()
        return

    qo = 0
    for tt in range(8):
        qo = b.dma("gpsimd", "qo", C["out_ap"][128 * tt:128 * (tt + 1), :], acc[:, 2048 * tt:2048 * (tt + 1)],
                   waits=[("vector", acc_ev)])
    fE = b.fence()
    for s_ in QSEMS:
        b.wait("gpsimd", s_, fE[s_])



def build_program(phases=6, dumps=()):
    nc = bass.Bass("TRN2", target_bir_lowering=False)
    CIN = {}
    CIN["xrows"] = nc.dram_tensor("xrows", [S, D], F32, kind="ExternalInput").ap()
    CIN["ktab"] = nc.dram_tensor("ktab", [128, S], F32, kind="ExternalInput").ap()
    CIN["qtab"] = nc.dram_tensor("qtab", [128, OWN], F32, kind="ExternalInput").ap()
    CIN["biascol"] = nc.dram_tensor("biascol", [128, 49], F32, kind="ExternalInput").ap()
    CIN["wkv"] = nc.dram_tensor("wkv", [16, 128, 1024], F32, kind="ExternalInput").ap()
    CIN["wq"] = nc.dram_tensor("wq", [16, 128, 2048], F32, kind="ExternalInput").ap()
    CIN["wout"] = nc.dram_tensor("wout", [4, 16, 128, 512], F32, kind="ExternalInput").ap()
    CIN["rms16"] = nc.dram_tensor("rms16", [128, 16], F32, kind="ExternalInput").ap()
    CIN["wgate"] = nc.dram_tensor("wgate", [128, 272], F32, kind="ExternalInput").ap()
    CIN["w13"] = nc.dram_tensor("w13", [NE, 22, 128, 2048], F32, kind="ExternalInput").ap()
    CIN["w2"] = nc.dram_tensor("w2", [NE, 4, 128, 5632], F32, kind="ExternalInput").ap()
    out_ap = nc.dram_tensor("out", [OWN, D], F32, kind="ExternalOutput").ap()
    ident_dram = nc.inline_tensor(np.eye(128, dtype=np.float32), "identf32")

    SBASE = 16512
    def at(name, shape, dtype, off):
        return nc.alloc_sbuf_tensor_at(name, shape, dtype, offset=SBASE + off)

    T = {}
    EB, FB, MB = 98816, 164352, 197120
    T["kst"] = at("kst", [128, 16384], BF16, 0).ap()
    T["vst"] = at("vst", [128, 16640], BF16, 32768)
    T["qot"] = at("qot", [128, 16384], BF16, 66048).ap()
    T["acc"] = at("acc", [128, 16384], F32, EB)
    T["x1nT"] = at("x1nT", [128, 16384], BF16, FB).ap()
    T["z_own"] = [at("z_own0", [128, 8192], BF16, EB).ap(), at("z_own1", [128, 8192], BF16, EB + 16384).ap()]
    T["rtu"] = at("rtu", [64, 512], F32, EB + 32768)
    T["rtv"] = at("rtv", [64, 512], F32, EB + 34816)
    T["rtw"] = at("rtw", [64, 512], F32, EB + 36864)
    T["rtz"] = at("rtz", [64, 512], F32, EB + 38912)
    T["xbuf"] = [at("xbuf0", [128, 2048], F32, EB + 40960), at("xbuf1", [128, 2048], F32, EB + 49152)]
    T["qland"] = at("qland", [128, 2048], F32, EB + 40960)
    T["wqb"] = [at("wqb0", [128, 2048], BF16, EB + 49152), at("wqb1", [128, 2048], BF16, EB + 53248)]
    T["zT"] = at("zT", [128, 8192], BF16, FB).ap()
    T["zbuf"] = at("zbuf", [128, 2048], F32, FB + 16384)
    T["kvland"] = at("kvland", [128, 1024], F32, FB + 24576)
    T["tabbuf"] = [at("tabbuf0", [128, 512], F32, FB + 28672), at("tabbuf1", [128, 512], F32, FB + 30720)]
    T["wkvb"] = at("wkvb", [128, 16384], BF16, 66048).ap()
    T["qtil"] = at("qtil", [65, 4096], BF16, EB).ap()
    T["ktil"] = at("ktil", [65, 4096], BF16, EB + 8192).ap()
    T["sqs"] = at("sqs", [64, 4096], BF16, EB + 16384)
    T["r1"] = at("r1", [1, 4096], F32, EB + 24576)
    T["pbuf"] = [at("pbuf0", [128, 2048], BF16, EB + 40960), at("pbuf1", [128, 2048], BF16, EB + 45056)]
    T["osb"] = at("osb", [65, 2048], F32, EB + 16384).ap()
    T["rrow"] = at("rrow", [1, 2048], BF16, EB + 49152)
    T["rrow32"] = at("rrow32", [1, 2048], F32, EB + 53248)
    T["xq"] = [at("xq0", [128, 2048], F32, 0), at("xq1", [128, 2048], F32, 8192)]
    T["wob"] = at("wob", [128, 8192], BF16, 16384)
    T["woland"] = [at("woland0", [128, 512], F32, 32768), at("woland1", [128, 512], F32, 34816)]
    T["ybuf"] = at("ybuf", [128, 2048], F32, 36864)
    T["zbuf2"] = at("zbuf2", [128, 2048], F32, 45056)
    T["w13land"] = [at("w13land0", [128, 1024], F32, 0), at("w13land1", [128, 1024], F32, 4096)]
    T["w13b"] = [at("w13b0", [128, 2048], BF16, 8192), at("w13b1", [128, 2048], BF16, 12288)]
    T["w2land"] = [at("w2land0", [128, 1408], F32, 16384), at("w2land1", [128, 1408], F32, 22016)]
    T["w2b"] = [at("w2b0", [128, 5632], BF16, 27648), at("w2b1", [128, 5632], BF16, 38912)]
    T["hT"] = at("hT", [128, 11264], BF16, 50176).ap()
    T["gs"] = at("gs", [128, 1024], F32, 72704)
    off = [MB]
    def misc(name, shape, dtype):
        t = at(name, shape, dtype, off[0])
        nbytes = int(np.prod(shape[1:])) * mybir.dt.size(dtype)
        off[0] += (nbytes + 31) // 32 * 32
        return t
    T["ident"] = misc("ident", [128, 128], F32)
    T["biascol"] = misc("biascolS", [128, 49], F32)
    T["rms16land"] = misc("rms16land", [128, 16], F32)
    T["rms16"] = misc("rms16S", [128, 16], F32)
    T["stats"] = misc("stats", [128, 16], F32)
    T["stat2"] = misc("stat2", [128, 2], F32)
    T["bn24"] = misc("bn24", [128, 24], F32)
    T["t1"] = misc("t1", [128, 1], F32)
    T["scl"] = misc("scl", [128, 1], F32)
    T["bia"] = misc("bia", [128, 1], F32)
    T["t2"] = misc("t2", [128, 1], F32)
    T["t3"] = misc("t3", [128, 1], F32)
    T["km2"] = [misc("km2a", [1, 1], F32), misc("km2b", [1, 1], F32)]
    T["km"] = misc("km", [1, 1], F32)
    T["kms"] = misc("kms", [1, 1], F32)
    T["qtab"] = misc("qtabS", [128, 1024], F32)
    T["wgateland"] = misc("wgateland", [128, 272], F32)
    T["wgateb"] = misc("wgateb", [128, 272], BF16)
    T["probs"] = misc("probs", [128, 16], F32)
    T["top8"] = misc("top8", [128, 8], F32)
    T["coef"] = misc("coef", [128, 128], F32)
    T["sig"] = misc("sig", [128, 8], F32)
    T["ones164"] = misc("ones164", [1, 64], BF16)
    T["ones64c"] = misc("ones64c", [64, 1], BF16)
    T["cst"] = misc("cst", [128, 2], F32)
    assert off[0] <= 212700, f"misc overflow: {off[0]}"
    vst_h = T["vst"]
    T["vst"] = vst_h.ap()
    T["vst_ones"] = bass.AP(vst_h, 64, [[16640, 128], [520, 32], [65, 8], [1, 1]])
    T["acc"] = T["acc"]
    T["pb"] = nc.alloc_psum_tensor("pb", [128, 4096], F32).ap()

    C = {"in": CIN, "out_ap": out_ap, "ident_dram": ident_dram}
    dump_specs = {
        "kst": (T["kst"][:, :], [128, 16384], BF16),
        "vst": (T["vst"][:, :], [128, 16640], BF16),
        "qot": (T["qot"][:, :], [128, 16384], BF16),
        "zT0": (T["z_own"][0][:, :], [128, 8192], BF16),
        "zT1": (T["z_own"][1][:, :], [128, 8192], BF16),
        "wkvb": (T["wkvb"][:, :], [128, 16384], BF16),
        "acc": (T["acc"][:], [128, 16384], F32),
        "x1nT": (T["x1nT"][:, :], [128, 16384], BF16),
        "coef": (T["coef"][:], [128, 128], F32),
        "sig": (T["sig"][:], [128, 8], F32),
        "stats": (T["stats"][:], [128, 16], F32),
        "hT": (T["hT"][:, :], [128, 11264], BF16),
    }
    C["dumps"] = []
    C["dbg_aps"] = {}
    for name in dumps:
        ap_, shp, dt = dump_specs[name]
        C["dbg_aps"][name] = nc.dram_tensor("dbg_" + name, shp, dt, kind="ExternalOutput").ap()
        C["dumps"].append((name, ap_))

    import contextlib
    with contextlib.ExitStack() as st:
        sems = {k: st.enter_context(nc.semaphore("sem_" + k)) for k in ALLSEMS}
        block = st.enter_context(nc.Block())

        @block.tensor
        def _(eng):
            build_walk(T, C, Bld("tensor", eng, sems), phases)

        @block.scalar
        def _(eng):
            build_walk(T, C, Bld("scalar", eng, sems), phases)

        @block.vector
        def _(eng):
            build_walk(T, C, Bld("vector", eng, sems), phases)

        @block.gpsimd
        def _(eng):
            build_walk(T, C, Bld("gpsimd", eng, sems), phases)

        @block.sync
        def _(eng):
            build_walk(T, C, Bld("sync", eng, sems), phases)

    return nc


_PROG_CACHE = {}

def kernel(**inputs):
    shared = _shared_layouts(inputs)
    if "nc" not in _PROG_CACHE:
        _PROG_CACHE["nc"] = build_program()
    nc = _PROG_CACHE["nc"]
    in_maps = []
    for core in range(NCORES):
        m = dict(shared)
        m.update(_per_core(inputs, core))
        in_maps.append(m)
    res = run_bass_kernel_spmd(nc, in_maps, core_ids=list(range(NCORES)))
    out = np.zeros((BATCH, S, D), np.float32)
    for core in range(NCORES):
        bb, i = core // 4, core % 4
        o = res.results[core]["out"]
        out[bb, i * 512:(i + 1) * 512] = o[0:512]
        out[bb, (7 - i) * 512:(8 - i) * 512] = o[512:1024]
    return out


# version 09
# speedup vs baseline: 2.1014x; 1.0090x over previous
import numpy as np
import ml_dtypes
import concourse.bass as bass
import concourse.mybir as mybir
from concourse.bass_utils import run_bass_kernel_spmd
from concourse.alu_op_type import AluOpType as OP

F32 = mybir.dt.float32
BF16 = mybir.dt.bfloat16
AF = mybir.ActivationFunctionType
AX = mybir.AxisListType

NCORES = 8
D = 2048
S = 4096
BATCH = 2
NG, HPG, DH = 8, 4, 64
NE_REAL = 16
I_EXP, I_SH = 1408, 5632
NE = 20
OWN = 1024
NTB = 8
NEG = -1e30
BOUND_MARGIN = 1.05


def _mblocks_2816(W):
    A = W.reshape(22, 128, 16, 128).transpose(0, 3, 2, 1).reshape(22, 128, 2048)
    order = []
    for j in range(11):
        order += [j, 11 + j]
    return np.ascontiguousarray(A[order])

def _nblocks_1408(W2):
    return np.ascontiguousarray(W2.reshape(4, 512, 11, 128).transpose(0, 3, 2, 1).reshape(4, 128, 5632))

def _shared_layouts(inp):
    out = {}
    w_in = np.asarray(inp["L0_attn_w_in"], np.float32)
    permk = []
    for m in range(4):
        for half in (0, 1):
            for g in (2 * m, 2 * m + 1):
                base = 2048 + g * 64 + 32 * half
                permk += list(range(base, base + 32))
    WK = w_in[permk]
    WV = w_in[2560:3072]
    out["wkv"] = np.ascontiguousarray(np.concatenate(
        [WK.reshape(512, 16, 128).transpose(1, 2, 0),
         WV.reshape(512, 16, 128).transpose(1, 2, 0)], axis=2))
    permq = []
    for m in range(16):
        for half in (0, 1):
            for h in (2 * m, 2 * m + 1):
                base = h * 64 + 32 * half
                permq += list(range(base, base + 32))
    WQ = w_in[permq]
    out["wq"] = np.ascontiguousarray(WQ.reshape(16, 128, 16, 128).transpose(0, 3, 2, 1).reshape(16, 128, 2048))
    w_out = np.asarray(inp["L0_attn_w_out"], np.float32)
    out["wout"] = np.ascontiguousarray(w_out.reshape(4, 512, 16, 128).transpose(0, 2, 3, 1))
    out["rms16"] = np.ascontiguousarray(np.asarray(inp["L0_attn_rms_w"], np.float32).reshape(16, 128).T)
    gate = np.asarray(inp["L0_ffn_gate_w"], np.float32)
    g16 = gate.T.reshape(16, 128, 16)
    sg1 = np.asarray(inp["L0_ffn_sgate_w"], np.float32)[0].reshape(16, 128, 1)
    out["wgate"] = np.ascontiguousarray(np.concatenate([g16, sg1], axis=2).transpose(1, 0, 2).reshape(128, 272))
    sgu = np.asarray(inp["L0_ffn_shared_gate_up_w"], np.float32)
    sdn = np.asarray(inp["L0_ffn_shared_down_w"], np.float32)
    w13 = np.asarray(inp["L0_ffn_w13"], np.float32)
    w2 = np.asarray(inp["L0_ffn_w2"], np.float32)
    e13, e2 = [], []
    for j in range(4):
        W13p = np.concatenate([sgu[I_EXP * j:I_EXP * (j + 1)],
                               sgu[I_SH + I_EXP * j:I_SH + I_EXP * (j + 1)]], axis=0)
        e13.append(_mblocks_2816(W13p))
        e2.append(_nblocks_1408(sdn[:, I_EXP * j:I_EXP * (j + 1)]))
    for e in range(NE_REAL):
        e13.append(_mblocks_2816(w13[e]))
        e2.append(_nblocks_1408(w2[e]))
    out["w13"] = np.ascontiguousarray(np.stack(e13))
    out["w2"] = np.ascontiguousarray(np.stack(e2))
    return out

def _rope_tab(pos, scale):
    inv = 1.0 / (10000.0 ** (np.arange(0, DH, 2) / DH))
    ang = pos[None, :].astype(np.float64) * inv[:, None]
    c = (np.cos(ang) * scale).astype(np.float32)
    s = (np.sin(ang) * scale).astype(np.float32)
    return np.ascontiguousarray(np.concatenate([c, c, s, s], axis=0))

def _per_core(inp, core):
    b, i = core // 4, core % 4
    x = np.asarray(inp["L0_attn_x"], np.float32)[b]
    others = list(range(0, i)) + list(range(i + 1, 7 - i))
    xrows = np.zeros((S, D), np.float32)
    pos = np.zeros((S,), np.int64)
    for s_, ch in enumerate(others):
        xrows[s_ * 512:(s_ + 1) * 512] = x[ch * 512:(ch + 1) * 512]
        pos[s_ * 512:(s_ + 1) * 512] = np.arange(ch * 512, (ch + 1) * 512)
    xrows[3072:3584] = x[i * 512:(i + 1) * 512]
    pos[3072:3584] = np.arange(i * 512, (i + 1) * 512)
    xrows[3584:4096] = x[(7 - i) * 512:(8 - i) * 512]
    pos[3584:4096] = np.arange((7 - i) * 512, (8 - i) * 512)
    biascol = np.zeros((128, 49), np.float32)
    nlo, nhi = 4 * i, 24 - 4 * i
    for kt in range(24):
        biascol[:, kt] = 0.0 if kt < nlo else NEG
        biascol[:, 24 + kt] = 0.0 if kt < nhi else NEG
    return {"xrows": xrows, "ktab": _rope_tab(pos, 1.0),
            "qtab": _rope_tab(pos[3072:], 0.125), "biascol": biascol}


ENGS = ("tensor", "scalar", "vector", "gpsimd", "sync")
QSEMS = ("qi", "qkv", "qt0", "qt1", "qx0", "qx1", "qq", "qwo0", "qwo1",
         "qxq0", "qxq1", "q13a", "q13b", "q2a", "q2b", "qo")
ALLSEMS = ENGS + QSEMS

class Bld:
    def __init__(self, me, eng, sems):
        self.me, self.eng, self.sems = me, eng, sems
        self.ct = {k: 0 for k in ALLSEMS}
        self.last_wait = {}

    def _wait(self, sem_name, val):
        if val <= 0 or self.last_wait.get(sem_name, -1) >= val:
            return
        self.last_wait[sem_name] = val
        self.eng.wait_ge(self.sems[sem_name], val)

    def op(self, eng_name, fn, waits=(), inc=True):
        if self.me == eng_name:
            for (s, v) in waits:
                self._wait(s, v)
            instr = fn(self.eng)
            if inc:
                instr.then_inc(self.sems[eng_name], 1)
        if inc:
            self.ct[eng_name] += 1
        return self.ct[eng_name]

    def dma(self, q_name, sem, out, in_, waits=()):
        if self.me == q_name:
            for (s, v) in waits:
                self._wait(s, v)
            self.eng.dma_start(out=out, in_=in_).then_inc(self.sems[sem], 16)
        self.ct[sem] += 16
        return self.ct[sem]

    def wait(self, eng_name, sem_name, val):
        if self.me == eng_name:
            self._wait(sem_name, val)

    def fence(self):
        return dict(self.ct)


def build_walk(T, C, b: Bld, phases=6):
    def dump_and_stop():
        f = b.fence()
        last = 0
        for name, src_ap in C.get("dumps", []):
            last = b.dma("gpsimd", "qo", C["dbg_aps"][name], src_ap,
                         waits=[(e_, f[e_]) for e_ in ALLSEMS if e_ != "qo"])
        b.wait("gpsimd", "qo", last)

    pb = T["pb"]
    def bank(ix):
        return pb[:, 512 * ix:512 * (ix + 1)]

    b.dma("gpsimd", "qi", T["biascol"][:], C["in"]["biascol"])
    b.dma("gpsimd", "qi", T["rms16land"][:], C["in"]["rms16"])
    b.dma("gpsimd", "qi", T["qtab"][:], C["in"]["qtab"])
    b.dma("gpsimd", "qi", T["wgateland"][:], C["in"]["wgate"])
    qi = b.dma("gpsimd", "qi", T["ident"][:], C["ident_dram"].ap())
    g_bias = g_rms = g_qtab = g_gate = g_sg = g_id = ("qi", qi)
    v_ones = b.op("vector", lambda e: e.memset(T["vst_ones"], 1.0))
    v_cst = b.op("vector", lambda e: e.memset(T["cst"][:, 0:1], 1e-11))
    v_cst = b.op("vector", lambda e: e.memset(T["cst"][:, 1:2], 1e-5))
    v_o164 = b.op("vector", lambda e: e.memset(T["ones164"][:], 1.0))
    v_o64c = b.op("vector", lambda e: e.memset(T["ones64c"][:], 1.0))
    v_gate = b.op("vector", lambda e: e.tensor_copy(out=T["wgateb"][:], in_=T["wgateland"][:]),
                  waits=[g_gate])
    rms_v = b.op("vector", lambda e: e.tensor_copy(out=T["rms16"][:], in_=T["rms16land"][:]),
                 waits=[g_rms])
    v_wkv = 0
    for k in range(16):
        q_kv = b.dma("sync", "qkv", T["kvland"][:], C["in"]["wkv"][k],
                     waits=([("vector", v_wkv)] if k else []))
        v_wkv = b.op("vector", lambda e, k=k: e.tensor_copy(
            out=T["wkvb"][:, 1024 * k:1024 * (k + 1)], in_=T["kvland"][:]),
            waits=[("qkv", q_kv)])

    if phases <= 0:
        dump_and_stop()
        return
    if phases in (0.2,):
        b.op("vector", lambda e: e.memset(T["kst"][:, :], 0.0))
        b.op("vector", lambda e: e.memset(T["vst"][:, :], 0.0))
    x_rd_v = [0, 0]
    x_rd_s = [0, 0]
    tab_rd = [0, 0]
    zbuf_rd_pe = 0
    prev_tev = 0
    last_rope = 0
    sc_v = 0

    def ln_tile(src_ap, src_waits, dst_fn, fold_rms, stats_ix, zb="zbuf"):
        nonlocal zbuf_rd_pe, prev_tev
        vb = b.op("vector", lambda e: e.bn_stats(out=T["bn24"][:, 0:6], in_=src_ap[:, 0:512]),
                  waits=src_waits)
        for j in range(1, 4):
            vb = b.op("vector", lambda e, j=j: e.bn_stats(
                out=T["bn24"][:, 6 * j:6 * (j + 1)], in_=src_ap[:, 512 * j:512 * (j + 1)]))
        b.wait("vector", "vector", vb)
        stat = T["stats"][:, 2 * stats_ix:2 * stats_ix + 2] if stats_ix is not None else T["stat2"][:]
        va = b.op("vector", lambda e: e.bn_aggr(out=stat, in_=T["bn24"][:]))
        sc_, bi_ = (1.000001, 0) if fold_rms else (1.0, 1)
        sa = b.op("scalar", lambda e: e.activation(out=T["t1"][:], in_=stat[:, 1:2], func=AF.Sqrt,
                                                   bias=T["cst"][:, bi_:bi_ + 1], scale=sc_),
                  waits=[("vector", va), ("vector", v_cst)])
        vr = b.op("vector", lambda e: e.reciprocal(out=T["scl"][:], in_=T["t1"][:]),
                  waits=[("scalar", sa)])
        b.wait("vector", "vector", vr)
        sz = b.op("vector", lambda e: e.tensor_scalar(
            out=T[zb][:], in0=src_ap, scalar1=stat[:, 0:1], scalar2=T["scl"][:],
            op0=OP.subtract, op1=OP.mult),
            waits=[("tensor", zbuf_rd_pe)])
        b.wait("vector", "vector", sz)
        ev = prev_tev
        for c in range(16):
            bk = 4 + (c % 4)
            tp = b.op("tensor", lambda e, c=c, bk=bk: e.transpose(
                bank(bk)[:, 0:128], T[zb][:, 128 * c:128 * (c + 1)], T["ident"][:]),
                waits=[("vector", sz), g_id, ("vector", ev)])
            dst, use_rms = dst_fn(c)
            if use_rms:
                ev = b.op("vector", lambda e, dst=dst, bk=bk, c=c: e.tensor_scalar_mul(
                    out=dst, in0=bank(bk)[:, 0:128], scalar1=T["rms16"][:, c:c + 1]),
                    waits=[("tensor", tp), ("vector", rms_v)])
            else:
                ev = b.op("vector", lambda e, dst=dst, bk=bk: e.tensor_copy(
                    out=dst, in_=bank(bk)[:, 0:128]), waits=[("tensor", tp)])
        zbuf_rd_pe = b.ct["tensor"]
        prev_tev = ev
        return ev, sz

    def rope_evac(psum_ap, tab, dst_top, dst_bot, mm_done, extra):
        w = [("tensor", mm_done)] + list(extra)
        b.op("vector", lambda e: e.tensor_mul(out=T["rtu"][:], in0=psum_ap[64:128, :], in1=tab[64:128, :]), waits=w)
        b.op("vector", lambda e: e.tensor_mul(out=T["rtv"][:], in0=psum_ap[0:64, :], in1=tab[64:128, :]))
        b.op("vector", lambda e: e.tensor_mul(out=T["rtw"][:], in0=psum_ap[0:64, :], in1=tab[0:64, :]))
        v4 = b.op("vector", lambda e: e.tensor_mul(out=T["rtz"][:], in0=psum_ap[64:128, :], in1=tab[0:64, :]))
        b.wait("vector", "vector", v4)
        b.op("vector", lambda e: e.tensor_sub(out=dst_top, in0=T["rtw"][:], in1=T["rtu"][:]))
        return b.op("vector", lambda e: e.tensor_add(out=dst_bot, in0=T["rtz"][:], in1=T["rtv"][:]))

    for tb in range(1 if phases == 0.2 else NTB):
        own = tb >= 6
        dst_zT = T["z_own"][tb - 6] if own else T["zT"]
        tp_ = tb % 2
        g_tab = ("qt%d" % tp_, b.dma("gpsimd", "qt%d" % tp_, T["tabbuf"][tp_][:],
                 C["in"]["ktab"][:, 512 * tb:512 * (tb + 1)], waits=[("vector", tab_rd[tp_])]))
        ev_last = 0
        for tt in range(4):
            p = (tb * 4 + tt) % 2
            g_x = ("qx%d" % p, b.dma("gpsimd", "qx%d" % p, T["xbuf"][p][:],
                   C["in"]["xrows"][512 * tb + 128 * tt:512 * tb + 128 * (tt + 1), :],
                   waits=[("vector", x_rd_v[p])]))
            def dst_fn(c, dst_zT=dst_zT, tt=tt):
                return dst_zT[:, 512 * c + 128 * tt:512 * c + 128 * (tt + 1)], True
            ev_last, sz = ln_tile(T["xbuf"][p][:], [g_x], dst_fn, True,
                                  (tb - 6) * 4 + tt if own else None)
            x_rd_v[p] = b.ct["vector"]
            x_rd_s[p] = sz
        for m in range(4):
            bk = m % 2
            pe_k = 0
            for k in range(16):
                pe_k = b.op("tensor", lambda e, m=m, k=k, bk=bk, dst_zT=dst_zT: e.matmul(
                    bank(bk), T["wkvb"][:, 1024 * k + 128 * m:1024 * k + 128 * (m + 1)],
                    dst_zT[:, 512 * k:512 * (k + 1)], start=(k == 0), stop=(k == 15)),
                    waits=([("vector", ev_last), ("vector", last_rope)] if k == 0 else []),
                    inc=(k == 15))
            cols = slice(4096 * m + 512 * tb, 4096 * m + 512 * (tb + 1))
            last_rope = rope_evac(bank(bk), T["tabbuf"][tp_], T["kst"][0:64, cols],
                                  T["kst"][64:128, cols], pe_k, [g_tab])
        tab_rd[tp_] = b.ct["vector"]
        for tt in range(4):
            bk = 2 + tt % 2
            pe_v = 0
            for k in range(16):
                pe_v = b.op("tensor", lambda e, tt=tt, k=k, bk=bk, dst_zT=dst_zT: e.matmul(
                    bank(bk), dst_zT[:, 512 * k + 128 * tt:512 * k + 128 * (tt + 1)],
                    T["wkvb"][:, 1024 * k + 512:1024 * (k + 1)], start=(k == 0), stop=(k == 15)),
                    waits=([("scalar", sc_v)] if k == 0 else []), inc=(k == 15))
            base = 520 * (tb * 4 + tt)
            for g in range(NG):
                sc_v = b.op("scalar", lambda e, g=g, base=base, bk=bk: e.copy(
                    out=T["vst"][:, base + 65 * g:base + 65 * g + 64],
                    in_=bank(bk)[:, 64 * g:64 * (g + 1)]),
                    waits=([("tensor", pe_v)] if g == 0 else []))

    if phases in (0.2, 0.7):
        dump_and_stop()
        return
    fQ = b.fence()
    qland_rd = 0
    wqb_rd = [0, 0]
    for m in range(16):
        mp = m % 2
        q_q = b.dma("sync", "qq", T["qland"][:], C["in"]["wq"][m],
                    waits=[("gpsimd", qland_rd), ("vector", fQ["vector"]), ("scalar", fQ["scalar"])])
        qcv = b.op("gpsimd", lambda e, mp=mp: e.tensor_copy(out=T["wqb"][mp][:], in_=T["qland"][:]),
                   waits=[("qq", q_q), ("tensor", wqb_rd[mp]), ("scalar", fQ["scalar"])])
        qland_rd = qcv
        for tbq in range(2):
            bk = tbq
            pe_q = 0
            for k in range(16):
                pe_q = b.op("tensor", lambda e, k=k, tbq=tbq, bk=bk, mp=mp: e.matmul(
                    bank(bk), T["wqb"][mp][:, 128 * k:128 * (k + 1)],
                    T["z_own"][tbq][:, 512 * k:512 * (k + 1)], start=(k == 0), stop=(k == 15)),
                    waits=([("gpsimd", qcv), ("vector", last_rope)] if k == 0 else []), inc=(k == 15))
            cols = slice(1024 * m + 512 * tbq, 1024 * m + 512 * (tbq + 1))
            last_rope = rope_evac(bank(bk), T["qtab"][:, 512 * tbq:512 * (tbq + 1)],
                                  T["qot"][0:64, cols], T["qot"][64:128, cols], pe_q,
                                  [g_qtab])
        wqb_rd[mp] = b.ct["tensor"]

    if phases <= 1:
        dump_and_stop()
        return
    fP2 = b.fence()
    v_kones = b.op("vector", lambda e: e.memset(T["ktil"][64:65, :], 1.0),
                   waits=[("tensor", fP2["tensor"])])
    last_av = 0
    act_exp = 0
    exp_h = [0, 0]
    av_done = [[0, 0], [0, 0]]
    osb_wr = 0
    last_vevac = last_rope
    pool_r1 = 0
    s_kq = 0
    sc_p1 = fP2["scalar"]
    for g in range(NG):
        mk, half = g // 2, g % 2
        kcols = slice(4096 * mk, 4096 * (mk + 1))
        sv = b.op("vector", lambda e, half=half, kcols=kcols: e.tensor_copy(
            out=T["ktil"][0:32, :], in_=T["kst"][32 * half:32 * half + 32, kcols]),
            waits=[("vector", last_vevac), ("tensor", max(last_av, fP2["tensor"])),
                   ("gpsimd", pool_r1), ("scalar", s_kq)])
        sv = b.op("vector", lambda e, half=half, kcols=kcols: e.tensor_copy(
            out=T["ktil"][32:64, :], in_=T["kst"][64 + 32 * half:96 + 32 * half, kcols]))
        for h in range(HPG):
            hh = 4 * g + h
            c, sub = hh // 2, hh % 2
            qc = slice(1024 * c, 1024 * (c + 1))
            sv = b.op("vector", lambda e, h=h, sub=sub, qc=qc: e.tensor_copy(
                out=T["qtil"][0:32, 1024 * h:1024 * (h + 1)], in_=T["qot"][32 * sub:32 * sub + 32, qc]))
            sv = b.op("vector", lambda e, h=h, sub=sub, qc=qc: e.tensor_copy(
                out=T["qtil"][32:64, 1024 * h:1024 * (h + 1)], in_=T["qot"][64 + 32 * sub:96 + 32 * sub, qc]))
        b.wait("vector", "vector", sv)
        v1 = b.op("vector", lambda e: e.tensor_mul(out=T["sqs"][:], in0=T["ktil"][0:64, :], in1=T["ktil"][0:64, :]))
        b.wait("vector", "vector", v1)
        km_prev = 0
        for half in range(2):
            mmn = 0
            for c4 in range(4):
                mmn = b.op("tensor", lambda e, half=half, c4=c4: e.matmul(
                    pb[0:1, 512 * c4:512 * (c4 + 1)], T["ones64c"][:],
                    T["sqs"][:, 2048 * half + 512 * c4:2048 * half + 512 * (c4 + 1)],
                    start=True, stop=True),
                    waits=([("vector", max(v1, v_o64c, km_prev)), ("scalar", max(exp_h[1], s_kq))]
                           if c4 == 0 else []), inc=(c4 == 3))
            km_prev = b.op("vector", lambda e, half=half: e.reduce_max(
                out=T["km2"][half][:], in_=pb[0:1, 0:2048], axis=AX.X), waits=[("tensor", mmn)])
        b.wait("vector", "vector", km_prev)
        v2 = b.op("vector", lambda e: e.tensor_max(out=T["km2"][0][:], in0=T["km2"][0][:], in1=T["km2"][1][:]))
        s1 = b.op("scalar", lambda e: e.activation(out=T["km"][:], in_=T["km2"][0][:], func=AF.Sqrt),
                  waits=[("vector", v2)])
        v3 = b.op("vector", lambda e: e.tensor_scalar_mul(out=T["kms"][:], in0=T["km"][:], scalar1=-BOUND_MARGIN),
                  waits=[("scalar", s1)])
        v4 = b.op("vector", lambda e: e.tensor_mul(out=T["sqs"][:], in0=T["qtil"][0:64, :], in1=T["qtil"][0:64, :]),
                  waits=[("vector", km_prev)])
        b.wait("vector", "vector", v4)
        s_kq = 0
        for half in range(2):
            mmn = 0
            for c4 in range(4):
                mmn = b.op("tensor", lambda e, half=half, c4=c4: e.matmul(
                    pb[0:1, 512 * c4:512 * (c4 + 1)], T["ones64c"][:],
                    T["sqs"][:, 2048 * half + 512 * c4:2048 * half + 512 * (c4 + 1)],
                    start=True, stop=True),
                    waits=([("vector", v4 if half == 0 else 0), ("scalar", s_kq)] if c4 == 0 else []),
                    inc=(c4 == 3))
            s_kq = b.op("scalar", lambda e, half=half: e.activation(
                out=T["qtil"][64:65, 2048 * half:2048 * (half + 1)], in_=pb[0:1, 0:2048], func=AF.Sqrt),
                waits=[("tensor", mmn), ("vector", sv)])
        vb_ = b.op("vector", lambda e: e.tensor_scalar_mul(out=T["qtil"][64:65, :], in0=T["qtil"][64:65, :],
                                                           scalar1=T["kms"][:]),
                   waits=[("scalar", s_kq), ("vector", v3)])
        bound_ready = vb_
        pool_r1 = 0

        for qb in range(2):
            tiles = [("o", kt) for kt in range(12 if qb == 0 else 24)] + \
                    [("s", j) for j in range(4 if qb == 0 else 8)]
            for idx, (kind, kt) in enumerate(tiles):
                if kind == "o":
                    krows = slice(128 * kt, 128 * (kt + 1))
                    bias_ap = T["biascol"][:, 24 * qb + kt:24 * qb + kt + 1]
                    vtile = kt
                else:
                    krows = slice(128 * (24 + kt), 128 * (25 + kt))
                    bias_ap = T["biascol"][:, 48:49]
                    vtile = 24 + kt
                pp = idx % 2
                qkh = [0, 0]
                for h in range(HPG):
                    qkh[h // 2] = b.op("tensor", lambda e, h=h, krows=krows, qb=qb: e.matmul(
                        bank(h), T["ktil"][:, krows],
                        T["qtil"][:, 1024 * h + 512 * qb:1024 * h + 512 * (qb + 1)],
                        start=True, stop=True),
                        waits=([("vector", max(bound_ready, last_vevac)),
                                ("scalar", max(exp_h[0], osb_wr, s_kq))] if h == 0 else
                               ([("scalar", exp_h[1])] if h == 2 else [])),
                        inc=(h in (1, 3)))
                for half in range(2):
                    exp_h[half] = b.op("scalar", lambda e, pp=pp, bias_ap=bias_ap, half=half: e.activation(
                        out=T["pbuf"][pp][:, 1024 * half:1024 * (half + 1)],
                        in_=pb[:, 1024 * half:1024 * (half + 1)], func=AF.Exp, bias=bias_ap, scale=1.0),
                        waits=[("tensor", max(qkh[half], av_done[pp][half])), g_bias])
                p_ready = [[("scalar", exp_h[0])], [("scalar", exp_h[1])]]
                if kind == "s" and not (qb == 1 and kt < 4):
                    for half in range(2):
                        pm = b.op("gpsimd", lambda e, pp=pp, qb=qb, kt=kt, half=half: e.affine_select(
                            out=T["pbuf"][pp][:, 1024 * half:1024 * (half + 1)],
                            in_=T["pbuf"][pp][:, 1024 * half:1024 * (half + 1)],
                            pattern=[[0, 2], [1, 512]],
                            compare_op=OP.is_ge, fill=0.0, base=512 * qb - 128 * kt, channel_multiplier=-1),
                            waits=[("scalar", exp_h[half])])
                        p_ready[half] = [("gpsimd", pm)]
                for h in range(HPG):
                    last_av = b.op("tensor", lambda e, h=h, vtile=vtile, g=g, pp=pp, idx=idx, ntile=len(tiles): e.matmul(
                        pb[0:65, 2048 + 512 * h:2048 + 512 * (h + 1)],
                        T["vst"][:, 520 * vtile + 65 * g:520 * vtile + 65 * (g + 1)],
                        T["pbuf"][pp][:, 512 * h:512 * (h + 1)], start=(idx == 0), stop=(idx == ntile - 1)),
                        waits=((p_ready[0] + [("vector", max(v_ones, last_vevac)), ("scalar", max(osb_wr, sc_p1))])
                               if h == 0 else (p_ready[1] if h == 2 else [])), inc=(h in (1, 3)))
                    if h == 1:
                        av_done[pp][0] = last_av
                av_done[pp][1] = last_av
            so = b.op("scalar", lambda e: e.copy(out=T["osb"][:], in_=pb[0:65, 2048:4096]),
                      waits=[("tensor", last_av), ("vector", last_vevac), ("gpsimd", pool_r1)])
            osb_wr = so
            vrc = b.op("vector", lambda e: e.reciprocal(out=T["rrow32"][:], in_=T["osb"][64:65, :]),
                       waits=[("scalar", so)])
            b.wait("vector", "vector", vrc)
            vrc = b.op("vector", lambda e: e.tensor_copy(out=T["rrow"][:], in_=T["rrow32"][:]))
            bc = 0
            for h in range(HPG):
                bc = b.op("tensor", lambda e, h=h: e.matmul(
                    pb[0:64, 2048 + 512 * h:2048 + 512 * (h + 1)], T["ones164"][:],
                    T["rrow"][:, 512 * h:512 * (h + 1)], start=True, stop=True),
                    waits=([("vector", max(vrc, v_o164)), ("scalar", so)] if h == 0 else []),
                    inc=(h == 3))
            for h in range(HPG):
                hh = 4 * g + h
                c, sub = hh // 2, hh % 2
                last_vevac = b.op("vector", lambda e, h=h, c=c, sub=sub, qb=qb: e.tensor_mul(
                    out=T["qot"][64 * sub:64 * (sub + 1), 1024 * c + 512 * qb:1024 * c + 512 * (qb + 1)],
                    in0=T["osb"][0:64, 512 * h:512 * (h + 1)], in1=pb[0:64, 2048 + 512 * h:2048 + 512 * (h + 1)]),
                    waits=[("tensor", bc), ("scalar", so)])

    if phases <= 2:
        dump_and_stop()
        return
    fP3 = b.fence()
    acc = T["acc"].ap()
    xq_rd_v = [fP3["vector"], fP3["vector"]]
    xq_rd_s = [0, 0]
    acc_ev = 0
    ybuf_rd = 0
    for tt in range(8):
        p = tt % 2
        g_xq = b.dma("gpsimd", "qxq%d" % p, T["xq"][p][:],
                     C["in"]["xrows"][3072 + 128 * tt:3072 + 128 * (tt + 1), :],
                     waits=[("vector", xq_rd_v[p]), ("tensor", fP3["tensor"])])
        sa = b.op("scalar", lambda e, tt=tt: e.activation(
            out=T["t1"][:], in_=T["stats"][:, 2 * tt + 1:2 * tt + 2], func=AF.Sqrt,
            bias=T["cst"][:, 1:2], scale=1.0),
            waits=[("vector", acc_ev)])
        vr = b.op("vector", lambda e: e.reciprocal(out=T["scl"][:], in_=T["t1"][:]),
                  waits=[("scalar", sa), ("vector", fP3["vector"])])
        b.wait("vector", "vector", vr)
        sy = b.op("vector", lambda e, p=p, tt=tt: e.tensor_scalar(
            out=T["ybuf"][:], in0=T["xq"][p][:], scalar1=T["stats"][:, 2 * tt:2 * tt + 1],
            scalar2=T["scl"][:], op0=OP.subtract, op1=OP.mult),
            waits=[("vector", ybuf_rd), ("qxq%d" % p, g_xq)])
        b.wait("vector", "vector", sy)
        acc_ev = b.op("vector", lambda e, p=p, tt=tt: e.tensor_add(
            out=acc[:, 2048 * tt:2048 * (tt + 1)], in0=T["xq"][p][:], in1=T["ybuf"][:]),
            waits=[("tensor", fP3["tensor"]), ("gpsimd", fP3["gpsimd"])])
        ybuf_rd = acc_ev
        xq_rd_v[p] = acc_ev
        xq_rd_s[p] = 0
    wol_rd = [0, 0]
    wob_rd_pe = [0, 0]
    wo_cvt = [0, 0]
    def wo_fetch(n):
        cvt = 0
        wp = n % 2
        for k in range(16):
            kp = k % 2
            q_wo = b.dma("sync", "qwo%d" % kp, T["woland"][kp][:], C["in"]["wout"][n, k],
                         waits=[("gpsimd", wol_rd[kp]), ("vector", fP3["vector"]), ("tensor", fP3["tensor"])])
            cvt = b.op("gpsimd", lambda e, k=k, kp=kp, wp=wp: e.tensor_copy(
                out=T["wob"][wp][:, 512 * k:512 * (k + 1)], in_=T["woland"][kp][:]),
                waits=[("qwo%d" % kp, q_wo), ("tensor", wob_rd_pe[wp]), ("vector", acc_ev)])
            wol_rd[kp] = cvt
        wo_cvt[wp] = cvt
    wo_fetch(0)
    p3_guard = [acc_ev, max(acc_ev, b.ct["vector"])]
    for n in range(4):
        wp = n % 2
        if n < 3:
            wo_fetch(n + 1)
        for th in range(2):
            bs3 = 4 * (th % 2)
            pe_wo = 0
            for k in range(16):
                for t4 in range(4):
                    tt = 4 * th + t4
                    pe_wo = b.op("tensor", lambda e, k=k, tt=tt, t4=t4, wp=wp, bs3=bs3: e.matmul(
                        bank(bs3 + t4), T["qot"][:, 1024 * k + 128 * tt:1024 * k + 128 * (tt + 1)],
                        T["wob"][wp][:, 512 * k:512 * (k + 1)], start=(k == 0), stop=(k == 15)),
                        waits=([("gpsimd", wo_cvt[wp]), ("vector", max(p3_guard[th], last_vevac))]
                               if (k == 0 and t4 == 0) else []),
                        inc=(k == 15 and t4 == 3))
            for t4 in range(4):
                tt = 4 * th + t4
                sl = slice(2048 * tt + 512 * n, 2048 * tt + 512 * (n + 1))
                acc_ev = b.op("vector", lambda e, t4=t4, sl=sl, bs3=bs3: e.tensor_add(
                    out=acc[:, sl], in0=bank(bs3 + t4), in1=acc[:, sl]),
                    waits=[("tensor", pe_wo), ("vector", acc_ev)])
            p3_guard[th] = acc_ev
        wob_rd_pe[wp] = b.ct["tensor"]
    x1_ev = 0
    for tt in range(8):
        src = acc[:, 2048 * tt:2048 * (tt + 1)]
        def dst2(c, tt=tt):
            return T["x1nT"][:, 1024 * c + 128 * tt:1024 * c + 128 * (tt + 1)], False
        x1_ev, _ = ln_tile(src, [("vector", acc_ev)], dst2, False, None, zb="zbuf2")
    x1nT_done = x1_ev

    if phases <= 3:
        dump_and_stop()
        return
    se_rd = 0
    vm_rd = 0
    sg_rd = 0
    coef_done = 0
    sig_done = 0
    for tt in range(8):
        pe_r = 0
        for k in range(16):
            pe_r = b.op("tensor", lambda e, k=k, tt=tt: e.matmul(
                pb[:, 0:17], T["x1nT"][:, 1024 * k + 128 * tt:1024 * k + 128 * (tt + 1)],
                T["wgateb"][:, 17 * k:17 * (k + 1)], start=(k == 0), stop=(k == 15)),
                waits=([("vector", max(x1nT_done, v_gate, vm_rd)), ("scalar", max(se_rd, sg_rd))] if k == 0 else []),
                inc=(k == 15))
        vm = b.op("vector", lambda e: e.reduce_max(out=T["t1"][:], in_=pb[:, 0:16], axis=AX.X, negate=True),
                  waits=[("tensor", pe_r)])
        se = b.op("scalar", lambda e: e.activation(out=T["probs"][:], in_=pb[:, 0:16], func=AF.Exp,
                                                   bias=T["t1"][:], scale=1.0),
                  waits=[("vector", max(vm, coef_done))])
        se_rd = se
        vs = b.op("vector", lambda e: e.reduce_sum(out=T["t2"][:], in_=T["probs"][:], axis=AX.X),
                  waits=[("scalar", se)])
        b.wait("vector", "vector", vs)
        vrc2 = b.op("vector", lambda e: e.reciprocal(out=T["t3"][:], in_=T["t2"][:]))
        b.wait("vector", "vector", vrc2)
        vn = b.op("vector", lambda e: e.tensor_scalar_mul(out=T["probs"][:], in0=T["probs"][:], scalar1=T["t3"][:]))
        b.wait("vector", "vector", vn)
        v8 = b.op("vector", lambda e: e.max(out=T["top8"][:], in_=T["probs"][:]))
        b.wait("vector", "vector", v8)
        coef_done = b.op("vector", lambda e, tt=tt: e.scalar_tensor_tensor(
            out=T["coef"][:, 16 * tt:16 * (tt + 1)], in0=T["probs"][:], scalar=T["top8"][:, 3:4],
            in1=T["probs"][:], op0=OP.is_ge, op1=OP.mult))
        vm_rd = coef_done
        sig_done = b.op("scalar", lambda e, tt=tt: e.activation(out=T["sig"][:, tt:tt + 1], in_=pb[:, 16:17],
                                                                func=AF.Sigmoid), waits=[("tensor", pe_r)])
        sg_rd = sig_done

    if phases <= 4:
        dump_and_stop()
        return
    _P5_CUT = (phases == 4.5)
    fP5 = b.fence()
    w13l_rd = [0, 0]
    w2l_rd = [0, 0]
    w13b_rd = [0, 0]
    w2b_rd = [0, 0]
    w13_cvt = [0, 0]
    w2_cvt = [0, 0]
    w13_cvt_last = 0
    acc_guard = [acc_ev, acc_ev]
    hT_rd_pe = 0
    gs_wr = 0
    gs_rd_v = 0
    h_wr = 0
    b.wait("tensor", "scalar", sig_done)
    b.wait("tensor", "vector", coef_done)
    for e_ix in range({4.5: 2, 4.6: 8, 4.7: 14, 4.8: 17, 4.85: 18, 4.9: 19}.get(phases, NE)):
        def w13_fetch(jm, ee=None):
            ee = e_ix if ee is None else ee
            cvt = 0
            for pc in range(2):
                kp = pc % 2
                q13 = b.dma("sync", "q13a" if kp == 0 else "q13b", T["w13land"][kp][:],
                            C["in"]["w13"][ee, jm, :, 1024 * pc:1024 * (pc + 1)],
                            waits=[("vector", max(w13l_rd[kp], fP5["vector"])),
                                   ("scalar", fP5["scalar"]), ("tensor", fP5["tensor"]),
                                   ("gpsimd", fP5["gpsimd"])])
                cvt = b.op("vector", lambda e, pc=pc, kp=kp, jm=jm: e.tensor_copy(
                    out=T["w13b"][jm % 2][:, 1024 * pc:1024 * (pc + 1)], in_=T["w13land"][kp][:]),
                    waits=[("q13a" if kp == 0 else "q13b", q13), ("tensor", w13b_rd[jm % 2])])
                w13l_rd[kp] = cvt
            w13_cvt[jm % 2] = cvt
        def w2_fetch(n):
            cv = 0
            np_ = n % 2
            for k in range(4):
                kp = k % 2
                q2 = b.dma("gpsimd", "q2a" if kp == 0 else "q2b", T["w2land"][kp][:],
                           C["in"]["w2"][e_ix, n, :, 1408 * k:1408 * (k + 1)],
                           waits=[("vector", max(w2l_rd[kp], fP5["vector"])),
                                  ("tensor", fP5["tensor"]), ("scalar", fP5["scalar"])])
                cv = b.op("vector", lambda e, k=k, kp=kp, np_=np_: e.tensor_copy(
                    out=T["w2b"][np_][:, 1408 * k:1408 * (k + 1)], in_=T["w2land"][kp][:]),
                    waits=[("q2a" if kp == 0 else "q2b", q2), ("tensor", w2b_rd[np_])])
                w2l_rd[kp] = cv
            w2_cvt[np_] = cv
        if e_ix == 0:
            w13_fetch(0)
            w13_fetch(1)
        for jm in range(22):
            mp = jm % 2
            is_g = (jm % 2 == 0)
            j = jm // 2
            pe_tb = [0, 0]
            for k in range(16):
                for tb in range(2):
                    pe_tb[tb] = b.op("tensor", lambda e, k=k, tb=tb, bk=(0 if is_g else 2) + tb, mp=mp: e.matmul(
                        bank(bk), T["w13b"][mp][:, 128 * k:128 * (k + 1)],
                        T["x1nT"][:, 1024 * k + 512 * tb:1024 * k + 512 * (tb + 1)],
                        start=(k == 0), stop=(k == 15)),
                        waits=(([("vector", w13_cvt[mp])] +
                                ([("scalar", gs_wr)] if is_g else [("vector", h_wr)]) +
                                ([("vector", acc_ev)] if jm == 0 else []))
                               if (k == 0 and tb == 0) else []), inc=(k == 15))
            pe_m = pe_tb[1]
            w13b_rd[mp] = pe_m
            if jm < 20:
                w13_fetch(jm + 2)
            elif jm == 20:
                w2_fetch(0)
            else:
                w2_fetch(1)
            if is_g:
                for tb in range(2):
                    gs_wr = b.op("scalar", lambda e, tb=tb: e.activation(
                        out=T["gs"][:, 512 * tb:512 * (tb + 1)], in_=bank(tb), func=AF.Silu),
                        waits=[("tensor", pe_tb[tb]), ("vector", gs_rd_v)] if tb == 0 else [("tensor", pe_tb[tb])])
            else:
                for tb in range(2):
                    h_wr = b.op("vector", lambda e, tb=tb, j=j: e.tensor_mul(
                        out=T["hT"][:, 1024 * j + 512 * tb:1024 * j + 512 * (tb + 1)],
                        in0=T["gs"][:, 512 * tb:512 * (tb + 1)], in1=bank(2 + tb)),
                        waits=[("scalar", gs_wr), ("tensor", max(pe_tb[tb], hT_rd_pe))])
                gs_rd_v = h_wr
        if e_ix + 1 < NE:
            w13_fetch(0, ee=e_ix + 1)
            w13_fetch(1, ee=e_ix + 1)
        for n in range(4):
            np_ = n % 2
            for th in range(2):
                bs = 4 * ((2 * n + th) % 2)
                pe_d = 0
                for k in range(11):
                    for t4 in range(4):
                        tt = 4 * th + t4
                        pe_d = b.op("tensor", lambda e, k=k, tt=tt, t4=t4, np_=np_, bs=bs: e.matmul(
                            bank(bs + t4), T["hT"][:, 1024 * k + 128 * tt:1024 * k + 128 * (tt + 1)],
                            T["w2b"][np_][:, 512 * k:512 * (k + 1)], start=(k == 0), stop=(k == 10)),
                            waits=([("vector", max(w2_cvt[np_], h_wr, acc_guard[(2 * n + th) % 2]))]
                                   if (k == 0 and t4 == 0) else []),
                            inc=(k == 10 and t4 == 3))
                if n < 3 and th == 0:
                    w2_fetch(n + 1)
                for t4 in range(4):
                    tt = 4 * th + t4
                    if e_ix < 4:
                        coef_ap = T["sig"][:, tt:tt + 1]
                    else:
                        coef_ap = T["coef"][:, 16 * tt + (e_ix - 4):16 * tt + (e_ix - 4) + 1]
                    sl = slice(2048 * tt + 512 * n, 2048 * tt + 512 * (n + 1))
                    acc_ev = b.op("vector", lambda e, t4=t4, sl=sl, coef_ap=coef_ap, bs=bs: e.scalar_tensor_tensor(
                        out=acc[:, sl], in0=bank(bs + t4), scalar=coef_ap, in1=acc[:, sl],
                        op0=OP.mult, op1=OP.add),
                        waits=[("tensor", pe_d), ("vector", max(acc_ev, coef_done)),
                               ("scalar", sig_done)])
                acc_guard[(2 * n + th) % 2] = acc_ev
            w2b_rd[np_] = pe_d
        hT_rd_pe = b.ct["tensor"]
    if phases in (4.5, 4.6, 4.7, 4.8, 4.85, 4.9):
        dump_and_stop()
        return

    qo = 0
    for tt in range(8):
        qo = b.dma("gpsimd", "qo", C["out_ap"][128 * tt:128 * (tt + 1), :], acc[:, 2048 * tt:2048 * (tt + 1)],
                   waits=[("vector", acc_ev)])
    fE = b.fence()
    for s_ in QSEMS:
        b.wait("gpsimd", s_, fE[s_])



def build_program(phases=6, dumps=()):
    nc = bass.Bass("TRN2", target_bir_lowering=False)
    CIN = {}
    CIN["xrows"] = nc.dram_tensor("xrows", [S, D], F32, kind="ExternalInput").ap()
    CIN["ktab"] = nc.dram_tensor("ktab", [128, S], F32, kind="ExternalInput").ap()
    CIN["qtab"] = nc.dram_tensor("qtab", [128, OWN], F32, kind="ExternalInput").ap()
    CIN["biascol"] = nc.dram_tensor("biascol", [128, 49], F32, kind="ExternalInput").ap()
    CIN["wkv"] = nc.dram_tensor("wkv", [16, 128, 1024], F32, kind="ExternalInput").ap()
    CIN["wq"] = nc.dram_tensor("wq", [16, 128, 2048], F32, kind="ExternalInput").ap()
    CIN["wout"] = nc.dram_tensor("wout", [4, 16, 128, 512], F32, kind="ExternalInput").ap()
    CIN["rms16"] = nc.dram_tensor("rms16", [128, 16], F32, kind="ExternalInput").ap()
    CIN["wgate"] = nc.dram_tensor("wgate", [128, 272], F32, kind="ExternalInput").ap()
    CIN["w13"] = nc.dram_tensor("w13", [NE, 22, 128, 2048], F32, kind="ExternalInput").ap()
    CIN["w2"] = nc.dram_tensor("w2", [NE, 4, 128, 5632], F32, kind="ExternalInput").ap()
    out_ap = nc.dram_tensor("out", [OWN, D], F32, kind="ExternalOutput").ap()
    ident_dram = nc.inline_tensor(np.eye(128, dtype=np.float32), "identf32")

    SBASE = 16512
    def at(name, shape, dtype, off):
        return nc.alloc_sbuf_tensor_at(name, shape, dtype, offset=SBASE + off)

    T = {}
    EB, FB, MB = 98816, 164352, 197120
    T["kst"] = at("kst", [128, 16384], BF16, 0).ap()
    T["vst"] = at("vst", [128, 16640], BF16, 32768)
    T["qot"] = at("qot", [128, 16384], BF16, 66048).ap()
    T["acc"] = at("acc", [128, 16384], F32, EB)
    T["x1nT"] = at("x1nT", [128, 16384], BF16, FB).ap()
    T["z_own"] = [at("z_own0", [128, 8192], BF16, EB).ap(), at("z_own1", [128, 8192], BF16, EB + 16384).ap()]
    T["rtu"] = at("rtu", [64, 512], F32, EB + 32768)
    T["rtv"] = at("rtv", [64, 512], F32, EB + 34816)
    T["rtw"] = at("rtw", [64, 512], F32, EB + 36864)
    T["rtz"] = at("rtz", [64, 512], F32, EB + 38912)
    T["xbuf"] = [at("xbuf0", [128, 2048], F32, EB + 40960), at("xbuf1", [128, 2048], F32, EB + 49152)]
    T["qland"] = at("qland", [128, 2048], F32, EB + 40960)
    T["wqb"] = [at("wqb0", [128, 2048], BF16, EB + 49152), at("wqb1", [128, 2048], BF16, EB + 53248)]
    T["zT"] = at("zT", [128, 8192], BF16, FB).ap()
    T["zbuf"] = at("zbuf", [128, 2048], F32, FB + 16384)
    T["kvland"] = at("kvland", [128, 1024], F32, FB + 24576)
    T["tabbuf"] = [at("tabbuf0", [128, 512], F32, FB + 28672), at("tabbuf1", [128, 512], F32, FB + 30720)]
    T["wkvb"] = at("wkvb", [128, 16384], BF16, 66048).ap()
    T["qtil"] = at("qtil", [65, 4096], BF16, EB).ap()
    T["ktil"] = at("ktil", [65, 4096], BF16, EB + 8192).ap()
    T["sqs"] = at("sqs", [64, 4096], BF16, EB + 16384)
    T["r1"] = at("r1", [1, 4096], F32, EB + 24576)
    T["pbuf"] = [at("pbuf0", [128, 2048], BF16, EB + 40960), at("pbuf1", [128, 2048], BF16, EB + 45056)]
    T["osb"] = at("osb", [65, 2048], F32, EB + 16384).ap()
    T["rrow"] = at("rrow", [1, 2048], BF16, EB + 49152)
    T["rrow32"] = at("rrow32", [1, 2048], F32, EB + 53248)
    T["xq"] = [at("xq0", [128, 2048], F32, 0), at("xq1", [128, 2048], F32, 8192)]
    T["wob"] = [at("wob0", [128, 8192], BF16, 16384), at("wob1", [128, 8192], BF16, 0)]
    T["woland"] = [at("woland0", [128, 512], F32, 32768), at("woland1", [128, 512], F32, 34816)]
    T["ybuf"] = at("ybuf", [128, 2048], F32, 36864)
    T["zbuf2"] = at("zbuf2", [128, 2048], F32, 45056)
    T["w13land"] = [at("w13land0", [128, 1024], F32, 0), at("w13land1", [128, 1024], F32, 4096)]
    T["w13b"] = [at("w13b0", [128, 2048], BF16, 8192), at("w13b1", [128, 2048], BF16, 12288)]
    T["w2land"] = [at("w2land0", [128, 1408], F32, 16384), at("w2land1", [128, 1408], F32, 22016)]
    T["w2b"] = [at("w2b0", [128, 5632], BF16, 27648), at("w2b1", [128, 5632], BF16, 38912)]
    T["hT"] = at("hT", [128, 11264], BF16, 50176).ap()
    T["gs"] = at("gs", [128, 1024], F32, 72704)
    off = [MB]
    def misc(name, shape, dtype):
        t = at(name, shape, dtype, off[0])
        nbytes = int(np.prod(shape[1:])) * mybir.dt.size(dtype)
        off[0] += (nbytes + 31) // 32 * 32
        return t
    T["ident"] = misc("ident", [128, 128], F32)
    T["biascol"] = misc("biascolS", [128, 49], F32)
    T["rms16land"] = misc("rms16land", [128, 16], F32)
    T["rms16"] = misc("rms16S", [128, 16], F32)
    T["stats"] = misc("stats", [128, 16], F32)
    T["stat2"] = misc("stat2", [128, 2], F32)
    T["bn24"] = misc("bn24", [128, 24], F32)
    T["t1"] = misc("t1", [128, 1], F32)
    T["scl"] = misc("scl", [128, 1], F32)
    T["bia"] = misc("bia", [128, 1], F32)
    T["t2"] = misc("t2", [128, 1], F32)
    T["t3"] = misc("t3", [128, 1], F32)
    T["km2"] = [misc("km2a", [1, 1], F32), misc("km2b", [1, 1], F32)]
    T["km"] = misc("km", [1, 1], F32)
    T["kms"] = misc("kms", [1, 1], F32)
    T["qtab"] = misc("qtabS", [128, 1024], F32)
    T["wgateland"] = misc("wgateland", [128, 272], F32)
    T["wgateb"] = misc("wgateb", [128, 272], BF16)
    T["probs"] = misc("probs", [128, 16], F32)
    T["top8"] = misc("top8", [128, 8], F32)
    T["coef"] = misc("coef", [128, 128], F32)
    T["sig"] = misc("sig", [128, 8], F32)
    T["ones164"] = misc("ones164", [1, 64], BF16)
    T["ones64c"] = misc("ones64c", [64, 1], BF16)
    T["cst"] = misc("cst", [128, 2], F32)
    assert off[0] <= 212700, f"misc overflow: {off[0]}"
    vst_h = T["vst"]
    T["vst"] = vst_h.ap()
    T["vst_ones"] = bass.AP(vst_h, 64, [[16640, 128], [520, 32], [65, 8], [1, 1]])
    T["acc"] = T["acc"]
    T["pb"] = nc.alloc_psum_tensor("pb", [128, 4096], F32).ap()

    C = {"in": CIN, "out_ap": out_ap, "ident_dram": ident_dram}
    dump_specs = {
        "kst": (T["kst"][:, :], [128, 16384], BF16),
        "vst": (T["vst"][:, :], [128, 16640], BF16),
        "qot": (T["qot"][:, :], [128, 16384], BF16),
        "zT0": (T["z_own"][0][:, :], [128, 8192], BF16),
        "zT1": (T["z_own"][1][:, :], [128, 8192], BF16),
        "wkvb": (T["wkvb"][:, :], [128, 16384], BF16),
        "acc": (T["acc"][:], [128, 16384], F32),
        "x1nT": (T["x1nT"][:, :], [128, 16384], BF16),
        "coef": (T["coef"][:], [128, 128], F32),
        "sig": (T["sig"][:], [128, 8], F32),
        "stats": (T["stats"][:], [128, 16], F32),
        "hT": (T["hT"][:, :], [128, 11264], BF16),
    }
    C["dumps"] = []
    C["dbg_aps"] = {}
    for name in dumps:
        ap_, shp, dt = dump_specs[name]
        C["dbg_aps"][name] = nc.dram_tensor("dbg_" + name, shp, dt, kind="ExternalOutput").ap()
        C["dumps"].append((name, ap_))

    import contextlib
    with contextlib.ExitStack() as st:
        sems = {k: st.enter_context(nc.semaphore("sem_" + k)) for k in ALLSEMS}
        block = st.enter_context(nc.Block())

        @block.tensor
        def _(eng):
            build_walk(T, C, Bld("tensor", eng, sems), phases)

        @block.scalar
        def _(eng):
            build_walk(T, C, Bld("scalar", eng, sems), phases)

        @block.vector
        def _(eng):
            build_walk(T, C, Bld("vector", eng, sems), phases)

        @block.gpsimd
        def _(eng):
            build_walk(T, C, Bld("gpsimd", eng, sems), phases)

        @block.sync
        def _(eng):
            build_walk(T, C, Bld("sync", eng, sems), phases)

    return nc


_PROG_CACHE = {}

def kernel(**inputs):
    shared = _shared_layouts(inputs)
    if "nc" not in _PROG_CACHE:
        _PROG_CACHE["nc"] = build_program()
    nc = _PROG_CACHE["nc"]
    in_maps = []
    for core in range(NCORES):
        m = dict(shared)
        m.update(_per_core(inputs, core))
        in_maps.append(m)
    res = run_bass_kernel_spmd(nc, in_maps, core_ids=list(range(NCORES)))
    out = np.zeros((BATCH, S, D), np.float32)
    for core in range(NCORES):
        bb, i = core // 4, core % 4
        o = res.results[core]["out"]
        out[bb, i * 512:(i + 1) * 512] = o[0:512]
        out[bb, (7 - i) * 512:(8 - i) * 512] = o[512:1024]
    return out


# version 11 (final)
# speedup vs baseline: 2.1920x; 1.0431x over previous
import numpy as np
import ml_dtypes
import concourse.bass as bass
import concourse.mybir as mybir
from concourse.bass_utils import run_bass_kernel_spmd
from concourse.alu_op_type import AluOpType as OP

F32 = mybir.dt.float32
BF16 = mybir.dt.bfloat16
AF = mybir.ActivationFunctionType
AX = mybir.AxisListType

NCORES = 8
D = 2048
S = 4096
BATCH = 2
NG, HPG, DH = 8, 4, 64
NE_REAL = 16
I_EXP, I_SH = 1408, 5632
NE = 20
OWN = 1024
NTB = 8
NEG = -1e30
BOUND_MARGIN = 1.05


def _mblocks_2816(W):
    A = W.reshape(22, 128, 16, 128).transpose(0, 3, 2, 1).reshape(22, 128, 2048)
    order = []
    for j in range(11):
        order += [j, 11 + j]
    return np.ascontiguousarray(A[order])

def _nblocks_1408(W2):
    return np.ascontiguousarray(W2.reshape(4, 512, 11, 128).transpose(0, 3, 2, 1).reshape(4, 128, 5632))

def _shared_layouts(inp):
    out = {}
    w_in = np.asarray(inp["L0_attn_w_in"], np.float32)
    permk = []
    for m in range(4):
        for half in (0, 1):
            for g in (2 * m, 2 * m + 1):
                base = 2048 + g * 64 + 32 * half
                permk += list(range(base, base + 32))
    WK = w_in[permk]
    WV = w_in[2560:3072]
    out["wkv"] = np.ascontiguousarray(np.concatenate(
        [WK.reshape(512, 16, 128).transpose(1, 2, 0),
         WV.reshape(512, 16, 128).transpose(1, 2, 0)], axis=2))
    permq = []
    for m in range(16):
        for half in (0, 1):
            for h in (2 * m, 2 * m + 1):
                base = h * 64 + 32 * half
                permq += list(range(base, base + 32))
    WQ = w_in[permq]
    out["wq"] = np.ascontiguousarray(WQ.reshape(16, 128, 16, 128).transpose(0, 3, 2, 1).reshape(16, 128, 2048))
    w_out = np.asarray(inp["L0_attn_w_out"], np.float32)
    out["wout"] = np.ascontiguousarray(w_out.reshape(4, 512, 16, 128).transpose(0, 2, 3, 1))
    out["rms16"] = np.ascontiguousarray(np.asarray(inp["L0_attn_rms_w"], np.float32).reshape(16, 128).T)
    gate = np.asarray(inp["L0_ffn_gate_w"], np.float32)
    g16 = gate.T.reshape(16, 128, 16)
    sg1 = np.asarray(inp["L0_ffn_sgate_w"], np.float32)[0].reshape(16, 128, 1)
    out["wgate"] = np.ascontiguousarray(np.concatenate([g16, sg1], axis=2).transpose(1, 0, 2).reshape(128, 272))
    sgu = np.asarray(inp["L0_ffn_shared_gate_up_w"], np.float32)
    sdn = np.asarray(inp["L0_ffn_shared_down_w"], np.float32)
    w13 = np.asarray(inp["L0_ffn_w13"], np.float32)
    w2 = np.asarray(inp["L0_ffn_w2"], np.float32)
    e13, e2 = [], []
    for j in range(4):
        W13p = np.concatenate([sgu[I_EXP * j:I_EXP * (j + 1)],
                               sgu[I_SH + I_EXP * j:I_SH + I_EXP * (j + 1)]], axis=0)
        e13.append(_mblocks_2816(W13p))
        e2.append(_nblocks_1408(sdn[:, I_EXP * j:I_EXP * (j + 1)]))
    for e in range(NE_REAL):
        e13.append(_mblocks_2816(w13[e]))
        e2.append(_nblocks_1408(w2[e]))
    out["w13"] = np.ascontiguousarray(np.stack(e13))
    out["w2"] = np.ascontiguousarray(np.stack(e2))
    return out

def _rope_tab(pos, scale):
    inv = 1.0 / (10000.0 ** (np.arange(0, DH, 2) / DH))
    ang = pos[None, :].astype(np.float64) * inv[:, None]
    c = (np.cos(ang) * scale).astype(np.float32)
    s = (np.sin(ang) * scale).astype(np.float32)
    return np.ascontiguousarray(np.concatenate([c, c, s, s], axis=0))

def _per_core(inp, core):
    b, i = core // 4, core % 4
    x = np.asarray(inp["L0_attn_x"], np.float32)[b]
    others = list(range(0, i)) + list(range(i + 1, 7 - i))
    xrows = np.zeros((S, D), np.float32)
    pos = np.zeros((S,), np.int64)
    for s_, ch in enumerate(others):
        xrows[s_ * 512:(s_ + 1) * 512] = x[ch * 512:(ch + 1) * 512]
        pos[s_ * 512:(s_ + 1) * 512] = np.arange(ch * 512, (ch + 1) * 512)
    xrows[3072:3584] = x[i * 512:(i + 1) * 512]
    pos[3072:3584] = np.arange(i * 512, (i + 1) * 512)
    xrows[3584:4096] = x[(7 - i) * 512:(8 - i) * 512]
    pos[3584:4096] = np.arange((7 - i) * 512, (8 - i) * 512)
    biascol = np.zeros((128, 49), np.float32)
    nlo, nhi = 4 * i, 24 - 4 * i
    for kt in range(24):
        biascol[:, kt] = 0.0 if kt < nlo else NEG
        biascol[:, 24 + kt] = 0.0 if kt < nhi else NEG
    return {"xrows": xrows, "ktab": _rope_tab(pos, 1.0),
            "qtab": _rope_tab(pos[3072:], 0.125), "biascol": biascol}


ENGS = ("tensor", "scalar", "vector", "gpsimd", "sync")
QSEMS = ("qi", "qkv", "qt0", "qt1", "qx0", "qx1", "qq", "qwo0", "qwo1",
         "qxq0", "qxq1", "q13a", "q13b", "q2a", "q2b", "qo")
ALLSEMS = ENGS + QSEMS

class Bld:
    def __init__(self, me, eng, sems):
        self.me, self.eng, self.sems = me, eng, sems
        self.ct = {k: 0 for k in ALLSEMS}
        self.last_wait = {}

    def _wait(self, sem_name, val):
        if val <= 0 or self.last_wait.get(sem_name, -1) >= val:
            return
        self.last_wait[sem_name] = val
        self.eng.wait_ge(self.sems[sem_name], val)

    def op(self, eng_name, fn, waits=(), inc=True):
        if self.me == eng_name:
            for (s, v) in waits:
                self._wait(s, v)
            instr = fn(self.eng)
            if inc:
                instr.then_inc(self.sems[eng_name], 1)
        if inc:
            self.ct[eng_name] += 1
        return self.ct[eng_name]

    def dma(self, q_name, sem, out, in_, waits=()):
        if self.me == q_name:
            for (s, v) in waits:
                self._wait(s, v)
            self.eng.dma_start(out=out, in_=in_).then_inc(self.sems[sem], 16)
        self.ct[sem] += 16
        return self.ct[sem]

    def wait(self, eng_name, sem_name, val):
        if self.me == eng_name:
            self._wait(sem_name, val)

    def fence(self):
        return dict(self.ct)


def build_walk(T, C, b: Bld, phases=6):
    def dump_and_stop():
        f = b.fence()
        last = 0
        for name, src_ap in C.get("dumps", []):
            last = b.dma("gpsimd", "qo", C["dbg_aps"][name], src_ap,
                         waits=[(e_, f[e_]) for e_ in ALLSEMS if e_ != "qo"])
        b.wait("gpsimd", "qo", last)

    pb = T["pb"]
    def bank(ix):
        return pb[:, 512 * ix:512 * (ix + 1)]

    b.dma("gpsimd", "qi", T["biascol"][:], C["in"]["biascol"])
    b.dma("gpsimd", "qi", T["rms16land"][:], C["in"]["rms16"])
    b.dma("gpsimd", "qi", T["qtab"][:], C["in"]["qtab"])
    b.dma("gpsimd", "qi", T["kvland"][:, 0:272], C["in"]["wgate"])
    qi = b.dma("gpsimd", "qi", T["ident"][:], C["ident_dram"].ap())
    g_bias = g_rms = g_qtab = g_gate = g_sg = g_id = ("qi", qi)
    v_ones = b.op("vector", lambda e: e.memset(T["vst_ones"], 1.0))
    v_cst = b.op("vector", lambda e: e.memset(T["cst"][:, 0:1], 1e-11))
    v_cst = b.op("vector", lambda e: e.memset(T["cst"][:, 1:2], 1e-5))
    v_o164 = b.op("vector", lambda e: e.memset(T["ones164"][:], 1.0))
    v_o64c = b.op("vector", lambda e: e.memset(T["ones64c"][:], 1.0))
    v_gate = b.op("vector", lambda e: e.tensor_copy(out=T["wgateb"][:], in_=T["kvland"][:, 0:272]),
                  waits=[g_gate])
    rms_v = b.op("vector", lambda e: e.tensor_copy(out=T["rms16"][:], in_=T["rms16land"][:]),
                 waits=[g_rms])
    v_wkv = 0
    for k in range(16):
        q_kv = b.dma("sync", "qkv", T["kvland"][:], C["in"]["wkv"][k],
                     waits=([("vector", v_wkv)] if k else [("vector", v_gate)]))
        v_wkv = b.op("vector", lambda e, k=k: e.tensor_scalar_mul(
            out=T["wkvb"][:, 1024 * k:1024 * (k + 1)], in0=T["kvland"][:],
            scalar1=T["rms16"][:, k:k + 1]),
            waits=[("qkv", q_kv), ("vector", rms_v)])

    if phases <= 0:
        dump_and_stop()
        return
    if phases in (0.2,):
        b.op("vector", lambda e: e.memset(T["kst"][:, :], 0.0))
        b.op("vector", lambda e: e.memset(T["vst"][:, :], 0.0))
    x_rd_v = [0, 0]
    tab_rd = [0, 0]
    zbuf_rd_pe = 0
    prev_tev = 0
    last_rope = 0
    sc_v = 0

    def ln_tile(src_ap, src_waits, dst_fn4, fold_rms, stats_ix, zb="zbuf"):
        nonlocal zbuf_rd_pe, prev_tev
        vb = b.op("vector", lambda e: e.bn_stats(out=T["bn24"][:, 0:6], in_=src_ap[:, 0:512]),
                  waits=src_waits)
        for j in range(1, 4):
            vb = b.op("vector", lambda e, j=j: e.bn_stats(
                out=T["bn24"][:, 6 * j:6 * (j + 1)], in_=src_ap[:, 512 * j:512 * (j + 1)]))
        b.wait("vector", "vector", vb)
        stat = T["stats"][:, 2 * stats_ix:2 * stats_ix + 2] if stats_ix is not None else T["stat2"][:]
        va = b.op("vector", lambda e: e.bn_aggr(out=stat, in_=T["bn24"][:]))
        sc_, bi_ = (1.000001, 0) if fold_rms else (1.0, 1)
        sa = b.op("scalar", lambda e: e.activation(out=T["t1"][:], in_=stat[:, 1:2], func=AF.Sqrt,
                                                   bias=T["cst"][:, bi_:bi_ + 1], scale=sc_),
                  waits=[("vector", va), ("vector", v_cst)])
        vr = b.op("vector", lambda e: e.reciprocal(out=T["scl"][:], in_=T["t1"][:]),
                  waits=[("scalar", sa)])
        b.wait("vector", "vector", vr)
        sz = b.op("vector", lambda e: e.tensor_scalar(
            out=T[zb][:], in0=src_ap, scalar1=stat[:, 0:1], scalar2=T["scl"][:],
            op0=OP.subtract, op1=OP.mult),
            waits=[("tensor", zbuf_rd_pe)])
        b.wait("vector", "vector", sz)
        ev = prev_tev
        for c4 in range(4):
            bk = c4 % 2
            tp = 0
            for ci in range(4):
                c = 4 * c4 + ci
                tp = b.op("tensor", lambda e, c=c, ci=ci, bk=bk: e.transpose(
                    bank(bk)[:, 128 * ci:128 * (ci + 1)], T[zb][:, 128 * c:128 * (c + 1)], T["ident"][:]),
                    waits=([("vector", sz), ("vector", last_rope), g_id, ("scalar", ev)] if ci == 0 else []),
                    inc=(ci == 3))
            ev = b.op("scalar", lambda e, c4=c4, bk=bk: e.copy(
                out=dst_fn4(c4), in_=bank(bk)), waits=[("tensor", tp)])
        zbuf_rd_pe = b.ct["tensor"]
        prev_tev = ev
        return ev, sz

    def rope_evac(psum_ap, tab, dst_top, dst_bot, mm_done, extra):
        w = [("tensor", mm_done), ("vector", last_rope)] + list(extra)
        b.op("vector", lambda e: e.tensor_mul(out=T["rtu"][:], in0=psum_ap[64:128, :], in1=tab[64:128, :]), waits=w)
        b.op("vector", lambda e: e.tensor_mul(out=T["rtv"][:], in0=psum_ap[0:64, :], in1=tab[64:128, :]))
        b.op("vector", lambda e: e.tensor_mul(out=T["rtw"][:], in0=psum_ap[0:64, :], in1=tab[0:64, :]))
        v4 = b.op("vector", lambda e: e.tensor_mul(out=T["rtz"][:], in0=psum_ap[64:128, :], in1=tab[0:64, :]))
        b.wait("vector", "vector", v4)
        b.op("vector", lambda e: e.tensor_sub(out=dst_top, in0=T["rtw"][:], in1=T["rtu"][:]))
        return b.op("vector", lambda e: e.tensor_add(out=dst_bot, in0=T["rtz"][:], in1=T["rtv"][:]))

    qland_rd = 0
    wqb_rd = [0, 0]
    q_cvt = [0, 0]
    def q_fetch(m):
        nonlocal qland_rd
        mp = m % 2
        q_q = b.dma("sync", "qq", T["qland"][:], C["in"]["wq"][m],
                    waits=[("gpsimd", qland_rd)])
        cv = 0
        for k in range(16):
            cv = b.op("gpsimd", lambda e, mp=mp, k=k: e.tensor_scalar_mul(
                out=T["wqb"][mp][:, 128 * k:128 * (k + 1)], in0=T["qland"][:, 128 * k:128 * (k + 1)],
                scalar1=T["rms16"][:, k:k + 1]),
                waits=([("qq", q_q), ("tensor", wqb_rd[mp]), ("vector", rms_v)] if k == 0 else []))
        q_cvt[mp] = cv
        qland_rd = cv
    def q_mblock(m):
        nonlocal last_rope
        mp = m % 2
        if m + 1 < 16:
            q_fetch(m + 1)
        for tbq in range(2):
            bk = 6 + tbq
            pe_q = 0
            for k in range(16):
                pe_q = b.op("tensor", lambda e, k=k, tbq=tbq, bk=bk, mp=mp: e.matmul(
                    bank(bk), T["wqb"][mp][:, 128 * k:128 * (k + 1)],
                    T["z_own"][tbq][:, 512 * k:512 * (k + 1)], start=(k == 0), stop=(k == 15)),
                    waits=([("gpsimd", q_cvt[mp]), ("vector", last_rope), ("scalar", sc_v)] if k == 0 else []),
                    inc=(k == 15))
            cols = slice(1024 * m + 512 * tbq, 1024 * m + 512 * (tbq + 1))
            last_rope = rope_evac(bank(bk), T["qtab"][:, 512 * tbq:512 * (tbq + 1)],
                                  T["qot"][0:64, cols], T["qot"][64:128, cols], pe_q,
                                  [g_qtab])
        wqb_rd[mp] = b.ct["tensor"]

    def ln_full(tb):
        own = tb >= 6
        zt_h = T["z_own_h"][tb - 6] if own else T["zT_h"]
        for tt in range(4):
            p = (tb * 4 + tt) % 2
            g_x = ("qx%d" % p, b.dma("gpsimd", "qx%d" % p, T["xbuf"][p][:],
                   C["in"]["xrows"][512 * tb + 128 * tt:512 * tb + 128 * (tt + 1), :],
                   waits=[("vector", x_rd_v[p])]))
            def dst_fn4(c4, zt_h=zt_h, tt=tt):
                return bass.AP(zt_h, 2048 * c4 + 128 * tt, [[8192, 128], [512, 4], [1, 128]])
            ln_tile(T["xbuf"][p][:], [g_x], dst_fn4, True,
                    (tb - 6) * 4 + tt if own else None)
            x_rd_v[p] = b.ct["vector"]

    def kv_k_mms(tb):
        nonlocal sc_v
        dst_zT = T["z_own"][tb - 6] if tb >= 6 else T["zT"]
        tp_ = tb % 2
        g_tab = ("qt%d" % tp_, b.dma("gpsimd", "qt%d" % tp_, T["tabbuf"][tp_][:],
                 C["in"]["ktab"][:, 512 * tb:512 * (tb + 1)], waits=[("vector", tab_rd[tp_])]))
        ks = []
        for m in range(4):
            pe_k = 0
            for k in range(16):
                pe_k = b.op("tensor", lambda e, m=m, k=k, dst_zT=dst_zT: e.matmul(
                    bank(m), T["wkvb"][:, 1024 * k + 128 * m:1024 * k + 128 * (m + 1)],
                    dst_zT[:, 512 * k:512 * (k + 1)], start=(k == 0), stop=(k == 15)),
                    waits=([("scalar", max(prev_tev, sc_v)), ("vector", last_rope)] if k == 0 else []),
                    inc=(k == 15))
            ks.append(pe_k)
        return ks, g_tab, tp_

    def kv_ropes(tb, ks, g_tab, tp_):
        nonlocal last_rope
        for m in range(4):
            cols = slice(4096 * m + 512 * tb, 4096 * m + 512 * (tb + 1))
            last_rope = rope_evac(bank(m), T["tabbuf"][tp_], T["kst"][0:64, cols],
                                  T["kst"][64:128, cols], ks[m], [g_tab])
        tab_rd[tp_] = b.ct["vector"]

    def kv_v(tb):
        nonlocal sc_v
        dst_zT = T["z_own"][tb - 6] if tb >= 6 else T["zT"]
        for tt in range(4):
            bk = 4 + tt
            pe_v = 0
            for k in range(16):
                pe_v = b.op("tensor", lambda e, tt=tt, k=k, bk=bk, dst_zT=dst_zT: e.matmul(
                    bank(bk), dst_zT[:, 512 * k + 128 * tt:512 * k + 128 * (tt + 1)],
                    T["wkvb"][:, 1024 * k + 512:1024 * (k + 1)], start=(k == 0), stop=(k == 15)),
                    waits=([("scalar", sc_v)] if k == 0 else []), inc=(k == 15))
            base = 520 * (tb * 4 + tt)
            for g in range(NG):
                sc_v = b.op("scalar", lambda e, g=g, base=base, bk=bk: e.copy(
                    out=T["vst"][:, base + 65 * g:base + 65 * g + 64],
                    in_=bank(bk)[:, 64 * g:64 * (g + 1)]),
                    waits=([("tensor", pe_v)] if g == 0 else []))

    seq = [6, 7, 0, 1, 2, 3, 4, 5]
    if phases == 0.2:
        seq = [6]
    q_fetch(0)
    ln_full(seq[0])
    for i, tb in enumerate(seq):
        ki = kv_k_mms(tb)
        kv_v(tb)
        kv_ropes(tb, *ki)
        if i + 1 < len(seq):
            ln_full(seq[i + 1])
    for m in range(16):
        q_mblock(m)

    if phases in (0.2, 0.7):
        dump_and_stop()
        return
    if phases <= 1:
        dump_and_stop()
        return
    fP2 = b.fence()
    v_kones = b.op("vector", lambda e: e.memset(T["ktil"][64:65, :], 1.0),
                   waits=[("tensor", fP2["tensor"])])
    last_av = 0
    act_exp = 0
    exp_h = [0, 0]
    av_done = [[0, 0], [0, 0]]
    osb_wr = 0
    last_vevac = last_rope
    pool_r1 = 0
    s_kq = 0
    sc_p1 = fP2["scalar"]
    for g in range(NG):
        mk, half = g // 2, g % 2
        kcols = slice(4096 * mk, 4096 * (mk + 1))
        sv = b.op("vector", lambda e, half=half, kcols=kcols: e.tensor_copy(
            out=T["ktil"][0:32, :], in_=T["kst"][32 * half:32 * half + 32, kcols]),
            waits=[("vector", last_vevac), ("tensor", max(last_av, fP2["tensor"])),
                   ("gpsimd", pool_r1), ("scalar", s_kq)])
        sv = b.op("vector", lambda e, half=half, kcols=kcols: e.tensor_copy(
            out=T["ktil"][32:64, :], in_=T["kst"][64 + 32 * half:96 + 32 * half, kcols]))
        for h in range(HPG):
            hh = 4 * g + h
            c, sub = hh // 2, hh % 2
            qc = slice(1024 * c, 1024 * (c + 1))
            sv = b.op("vector", lambda e, h=h, sub=sub, qc=qc: e.tensor_copy(
                out=T["qtil"][0:32, 1024 * h:1024 * (h + 1)], in_=T["qot"][32 * sub:32 * sub + 32, qc]))
            sv = b.op("vector", lambda e, h=h, sub=sub, qc=qc: e.tensor_copy(
                out=T["qtil"][32:64, 1024 * h:1024 * (h + 1)], in_=T["qot"][64 + 32 * sub:96 + 32 * sub, qc]))
        b.wait("vector", "vector", sv)
        v1 = b.op("vector", lambda e: e.tensor_mul(out=T["sqs"][:], in0=T["ktil"][0:64, :], in1=T["ktil"][0:64, :]))
        b.wait("vector", "vector", v1)
        km_prev = 0
        for half in range(2):
            mmn = 0
            for c4 in range(4):
                mmn = b.op("tensor", lambda e, half=half, c4=c4: e.matmul(
                    pb[0:1, 512 * c4:512 * (c4 + 1)], T["ones64c"][:],
                    T["sqs"][:, 2048 * half + 512 * c4:2048 * half + 512 * (c4 + 1)],
                    start=True, stop=True),
                    waits=([("vector", max(v1, v_o64c, km_prev)), ("scalar", max(exp_h[1], s_kq, sc_p1))]
                           if c4 == 0 else []), inc=(c4 == 3))
            km_prev = b.op("vector", lambda e, half=half: e.reduce_max(
                out=T["km2"][half][:], in_=pb[0:1, 0:2048], axis=AX.X), waits=[("tensor", mmn)])
        b.wait("vector", "vector", km_prev)
        v2 = b.op("vector", lambda e: e.tensor_max(out=T["km2"][0][:], in0=T["km2"][0][:], in1=T["km2"][1][:]))
        s1 = b.op("scalar", lambda e: e.activation(out=T["km"][:], in_=T["km2"][0][:], func=AF.Sqrt),
                  waits=[("vector", v2)])
        v3 = b.op("vector", lambda e: e.tensor_scalar_mul(out=T["kms"][:], in0=T["km"][:], scalar1=-BOUND_MARGIN),
                  waits=[("scalar", s1)])
        v4 = b.op("vector", lambda e: e.tensor_mul(out=T["sqs"][:], in0=T["qtil"][0:64, :], in1=T["qtil"][0:64, :]),
                  waits=[("vector", km_prev)])
        b.wait("vector", "vector", v4)
        s_kq = 0
        for half in range(2):
            mmn = 0
            for c4 in range(4):
                mmn = b.op("tensor", lambda e, half=half, c4=c4: e.matmul(
                    pb[0:1, 512 * c4:512 * (c4 + 1)], T["ones64c"][:],
                    T["sqs"][:, 2048 * half + 512 * c4:2048 * half + 512 * (c4 + 1)],
                    start=True, stop=True),
                    waits=([("vector", v4 if half == 0 else 0), ("scalar", s_kq)] if c4 == 0 else []),
                    inc=(c4 == 3))
            s_kq = b.op("scalar", lambda e, half=half: e.activation(
                out=T["qtil"][64:65, 2048 * half:2048 * (half + 1)], in_=pb[0:1, 0:2048], func=AF.Sqrt),
                waits=[("tensor", mmn), ("vector", sv)])
        vb_ = b.op("vector", lambda e: e.tensor_scalar_mul(out=T["qtil"][64:65, :], in0=T["qtil"][64:65, :],
                                                           scalar1=T["kms"][:]),
                   waits=[("scalar", s_kq), ("vector", v3)])
        bound_ready = vb_
        pool_r1 = 0

        for qb in range(2):
            tiles = [("o", kt) for kt in range(12 if qb == 0 else 24)] + \
                    [("s", j) for j in range(4 if qb == 0 else 8)]
            for idx, (kind, kt) in enumerate(tiles):
                if kind == "o":
                    krows = slice(128 * kt, 128 * (kt + 1))
                    bias_ap = T["biascol"][:, 24 * qb + kt:24 * qb + kt + 1]
                    vtile = kt
                else:
                    krows = slice(128 * (24 + kt), 128 * (25 + kt))
                    bias_ap = T["biascol"][:, 48:49]
                    vtile = 24 + kt
                pp = idx % 2
                qkh = [0, 0]
                for h in range(HPG):
                    qkh[h // 2] = b.op("tensor", lambda e, h=h, krows=krows, qb=qb: e.matmul(
                        bank(h), T["ktil"][:, krows],
                        T["qtil"][:, 1024 * h + 512 * qb:1024 * h + 512 * (qb + 1)],
                        start=True, stop=True),
                        waits=([("vector", max(bound_ready, last_vevac)),
                                ("scalar", max(exp_h[0], osb_wr, s_kq))] if h == 0 else
                               ([("scalar", exp_h[1])] if h == 2 else [])),
                        inc=(h in (1, 3)))
                for half in range(2):
                    exp_h[half] = b.op("scalar", lambda e, pp=pp, bias_ap=bias_ap, half=half: e.activation(
                        out=T["pbuf"][pp][:, 1024 * half:1024 * (half + 1)],
                        in_=pb[:, 1024 * half:1024 * (half + 1)], func=AF.Exp, bias=bias_ap, scale=1.0),
                        waits=[("tensor", max(qkh[half], av_done[pp][half])), g_bias])
                p_ready = [[("scalar", exp_h[0])], [("scalar", exp_h[1])]]
                if kind == "s" and not (qb == 1 and kt < 4):
                    for half in range(2):
                        pm = b.op("gpsimd", lambda e, pp=pp, qb=qb, kt=kt, half=half: e.affine_select(
                            out=T["pbuf"][pp][:, 1024 * half:1024 * (half + 1)],
                            in_=T["pbuf"][pp][:, 1024 * half:1024 * (half + 1)],
                            pattern=[[0, 2], [1, 512]],
                            compare_op=OP.is_ge, fill=0.0, base=512 * qb - 128 * kt, channel_multiplier=-1),
                            waits=[("scalar", exp_h[half])])
                        p_ready[half] = [("gpsimd", pm)]
                for h in range(HPG):
                    last_av = b.op("tensor", lambda e, h=h, vtile=vtile, g=g, pp=pp, idx=idx, ntile=len(tiles): e.matmul(
                        pb[0:65, 2048 + 512 * h:2048 + 512 * (h + 1)],
                        T["vst"][:, 520 * vtile + 65 * g:520 * vtile + 65 * (g + 1)],
                        T["pbuf"][pp][:, 512 * h:512 * (h + 1)], start=(idx == 0), stop=(idx == ntile - 1)),
                        waits=((p_ready[0] + [("vector", max(v_ones, last_vevac)), ("scalar", max(osb_wr, sc_p1))])
                               if h == 0 else (p_ready[1] if h == 2 else [])), inc=(h in (1, 3)))
                    if h == 1:
                        av_done[pp][0] = last_av
                av_done[pp][1] = last_av
            so = b.op("scalar", lambda e: e.copy(out=T["osb"][:], in_=pb[0:65, 2048:4096]),
                      waits=[("tensor", last_av), ("vector", last_vevac), ("gpsimd", pool_r1)])
            osb_wr = so
            vrc = b.op("vector", lambda e: e.reciprocal(out=T["rrow32"][:], in_=T["osb"][64:65, :]),
                       waits=[("scalar", so)])
            b.wait("vector", "vector", vrc)
            vrc = b.op("vector", lambda e: e.tensor_copy(out=T["rrow"][:], in_=T["rrow32"][:]))
            bc = 0
            for h in range(HPG):
                bc = b.op("tensor", lambda e, h=h: e.matmul(
                    pb[0:64, 2048 + 512 * h:2048 + 512 * (h + 1)], T["ones164"][:],
                    T["rrow"][:, 512 * h:512 * (h + 1)], start=True, stop=True),
                    waits=([("vector", max(vrc, v_o164)), ("scalar", so)] if h == 0 else []),
                    inc=(h == 3))
            for h in range(HPG):
                hh = 4 * g + h
                c, sub = hh // 2, hh % 2
                last_vevac = b.op("vector", lambda e, h=h, c=c, sub=sub, qb=qb: e.tensor_mul(
                    out=T["qot"][64 * sub:64 * (sub + 1), 1024 * c + 512 * qb:1024 * c + 512 * (qb + 1)],
                    in0=T["osb"][0:64, 512 * h:512 * (h + 1)], in1=pb[0:64, 2048 + 512 * h:2048 + 512 * (h + 1)]),
                    waits=[("tensor", bc), ("scalar", so)])

    if phases <= 2:
        dump_and_stop()
        return
    fP3 = b.fence()
    acc = T["acc"].ap()
    xq_rd_v = [fP3["vector"], fP3["vector"]]
    xq_rd_s = [0, 0]
    acc_ev = 0
    ybuf_rd = 0
    for tt in range(8):
        p = tt % 2
        g_xq = b.dma("gpsimd", "qxq%d" % p, T["xq"][p][:],
                     C["in"]["xrows"][3072 + 128 * tt:3072 + 128 * (tt + 1), :],
                     waits=[("vector", xq_rd_v[p]), ("tensor", fP3["tensor"])])
        sa = b.op("scalar", lambda e, tt=tt: e.activation(
            out=T["t1"][:], in_=T["stats"][:, 2 * tt + 1:2 * tt + 2], func=AF.Sqrt,
            bias=T["cst"][:, 1:2], scale=1.0),
            waits=[("vector", acc_ev)])
        vr = b.op("vector", lambda e: e.reciprocal(out=T["scl"][:], in_=T["t1"][:]),
                  waits=[("scalar", sa), ("vector", fP3["vector"])])
        b.wait("vector", "vector", vr)
        sy = b.op("vector", lambda e, p=p, tt=tt: e.tensor_scalar(
            out=T["ybuf"][:], in0=T["xq"][p][:], scalar1=T["stats"][:, 2 * tt:2 * tt + 1],
            scalar2=T["scl"][:], op0=OP.subtract, op1=OP.mult),
            waits=[("vector", ybuf_rd), ("qxq%d" % p, g_xq)])
        b.wait("vector", "vector", sy)
        acc_ev = b.op("vector", lambda e, p=p, tt=tt: e.tensor_add(
            out=acc[:, 2048 * tt:2048 * (tt + 1)], in0=T["xq"][p][:], in1=T["ybuf"][:]),
            waits=[("tensor", fP3["tensor"]), ("gpsimd", fP3["gpsimd"])])
        ybuf_rd = acc_ev
        xq_rd_v[p] = acc_ev
        xq_rd_s[p] = 0
    wol_rd = [0, 0]
    wob_rd_pe = [0, 0]
    wo_cvt = [0, 0]
    def wo_fetch(n):
        cvt = 0
        wp = n % 2
        for k in range(16):
            kp = k % 2
            q_wo = b.dma("sync", "qwo%d" % kp, T["woland"][kp][:], C["in"]["wout"][n, k],
                         waits=[("gpsimd", wol_rd[kp]), ("vector", fP3["vector"]), ("tensor", fP3["tensor"])])
            cvt = b.op("gpsimd", lambda e, k=k, kp=kp, wp=wp: e.tensor_copy(
                out=T["wob"][wp][:, 512 * k:512 * (k + 1)], in_=T["woland"][kp][:]),
                waits=[("qwo%d" % kp, q_wo), ("tensor", wob_rd_pe[wp]), ("vector", acc_ev)])
            wol_rd[kp] = cvt
        wo_cvt[wp] = cvt
    wo_fetch(0)
    p3_guard = [acc_ev, max(acc_ev, b.ct["vector"])]
    for n in range(4):
        wp = n % 2
        if n < 3:
            wo_fetch(n + 1)
        for th in range(2):
            bs3 = 4 * (th % 2)
            pe_wo = 0
            for k in range(16):
                for t4 in range(4):
                    tt = 4 * th + t4
                    pe_wo = b.op("tensor", lambda e, k=k, tt=tt, t4=t4, wp=wp, bs3=bs3: e.matmul(
                        bank(bs3 + t4), T["qot"][:, 1024 * k + 128 * tt:1024 * k + 128 * (tt + 1)],
                        T["wob"][wp][:, 512 * k:512 * (k + 1)], start=(k == 0), stop=(k == 15)),
                        waits=([("gpsimd", wo_cvt[wp]), ("vector", max(p3_guard[th], last_vevac))]
                               if (k == 0 and t4 == 0) else []),
                        inc=(k == 15 and t4 == 3))
            for t4 in range(4):
                tt = 4 * th + t4
                sl = slice(2048 * tt + 512 * n, 2048 * tt + 512 * (n + 1))
                acc_ev = b.op("vector", lambda e, t4=t4, sl=sl, bs3=bs3: e.tensor_add(
                    out=acc[:, sl], in0=bank(bs3 + t4), in1=acc[:, sl]),
                    waits=[("tensor", pe_wo), ("vector", acc_ev)])
            p3_guard[th] = acc_ev
        wob_rd_pe[wp] = b.ct["tensor"]
    x1_ev = 0
    for tt in range(8):
        src = acc[:, 2048 * tt:2048 * (tt + 1)]
        def dst2(c4, tt=tt):
            return bass.AP(T["x1nT_h"], 4096 * c4 + 128 * tt, [[16384, 128], [1024, 4], [1, 128]])
        x1_ev, _ = ln_tile(src, [("vector", acc_ev)], dst2, False, None, zb="zbuf2")
    x1nT_done = x1_ev

    if phases <= 3:
        dump_and_stop()
        return
    se_rd = 0
    vm_rd = 0
    sg_rd = 0
    coef_done = 0
    sig_done = 0
    for tt in range(8):
        pe_r = 0
        for k in range(16):
            pe_r = b.op("tensor", lambda e, k=k, tt=tt: e.matmul(
                pb[:, 0:17], T["x1nT"][:, 1024 * k + 128 * tt:1024 * k + 128 * (tt + 1)],
                T["wgateb"][:, 17 * k:17 * (k + 1)], start=(k == 0), stop=(k == 15)),
                waits=([("vector", max(v_gate, vm_rd)), ("scalar", max(x1nT_done, se_rd, sg_rd))] if k == 0 else []),
                inc=(k == 15))
        vm = b.op("vector", lambda e: e.reduce_max(out=T["t1"][:], in_=pb[:, 0:16], axis=AX.X, negate=True),
                  waits=[("tensor", pe_r)])
        se = b.op("scalar", lambda e: e.activation(out=T["probs"][:], in_=pb[:, 0:16], func=AF.Exp,
                                                   bias=T["t1"][:], scale=1.0),
                  waits=[("vector", max(vm, coef_done))])
        se_rd = se
        vs = b.op("vector", lambda e: e.reduce_sum(out=T["t2"][:], in_=T["probs"][:], axis=AX.X),
                  waits=[("scalar", se)])
        b.wait("vector", "vector", vs)
        vrc2 = b.op("vector", lambda e: e.reciprocal(out=T["t3"][:], in_=T["t2"][:]))
        b.wait("vector", "vector", vrc2)
        vn = b.op("vector", lambda e: e.tensor_scalar_mul(out=T["probs"][:], in0=T["probs"][:], scalar1=T["t3"][:]))
        b.wait("vector", "vector", vn)
        v8 = b.op("vector", lambda e: e.max(out=T["top8"][:], in_=T["probs"][:]))
        b.wait("vector", "vector", v8)
        coef_done = b.op("vector", lambda e, tt=tt: e.scalar_tensor_tensor(
            out=T["coef"][:, 16 * tt:16 * (tt + 1)], in0=T["probs"][:], scalar=T["top8"][:, 3:4],
            in1=T["probs"][:], op0=OP.is_ge, op1=OP.mult))
        vm_rd = coef_done
        sig_done = b.op("scalar", lambda e, tt=tt: e.activation(out=T["sig"][:, tt:tt + 1], in_=pb[:, 16:17],
                                                                func=AF.Sigmoid), waits=[("tensor", pe_r)])
        sg_rd = sig_done

    if phases <= 4:
        dump_and_stop()
        return
    _P5_CUT = (phases == 4.5)
    fP5 = b.fence()
    w13l_rd = [0, 0]
    w2l_rd = [0, 0]
    w13b_rd = [0, 0]
    w2b_rd = [0, 0]
    w13_cvt = [0, 0]
    w2_cvt = [0, 0]
    w13_cvt_last = 0
    acc_guard = [acc_ev, acc_ev]
    hT_rd_pe = 0
    gs_wr = 0
    gs_rd_v = 0
    h_wr = 0
    b.wait("tensor", "scalar", sig_done)
    b.wait("tensor", "vector", coef_done)
    for e_ix in range({4.5: 2, 4.6: 8, 4.7: 14, 4.8: 17, 4.85: 18, 4.9: 19}.get(phases, NE)):
        def w13_fetch(jm, ee=None):
            ee = e_ix if ee is None else ee
            cvt = 0
            for pc in range(2):
                kp = pc % 2
                q13 = b.dma("sync", "q13a" if kp == 0 else "q13b", T["w13land"][kp][:],
                            C["in"]["w13"][ee, jm, :, 1024 * pc:1024 * (pc + 1)],
                            waits=[("vector", max(w13l_rd[kp], fP5["vector"])),
                                   ("scalar", fP5["scalar"]), ("tensor", fP5["tensor"]),
                                   ("gpsimd", fP5["gpsimd"])])
                cvt = b.op("vector", lambda e, pc=pc, kp=kp, jm=jm: e.tensor_copy(
                    out=T["w13b"][jm % 2][:, 1024 * pc:1024 * (pc + 1)], in_=T["w13land"][kp][:]),
                    waits=[("q13a" if kp == 0 else "q13b", q13), ("tensor", w13b_rd[jm % 2])])
                w13l_rd[kp] = cvt
            w13_cvt[jm % 2] = cvt
        def w2_fetch(n):
            cv = 0
            np_ = n % 2
            for k in range(4):
                kp = k % 2
                q2 = b.dma("gpsimd", "q2a" if kp == 0 else "q2b", T["w2land"][kp][:],
                           C["in"]["w2"][e_ix, n, :, 1408 * k:1408 * (k + 1)],
                           waits=[("vector", max(w2l_rd[kp], fP5["vector"])),
                                  ("tensor", fP5["tensor"]), ("scalar", fP5["scalar"])])
                cv = b.op("vector", lambda e, k=k, kp=kp, np_=np_: e.tensor_copy(
                    out=T["w2b"][np_][:, 1408 * k:1408 * (k + 1)], in_=T["w2land"][kp][:]),
                    waits=[("q2a" if kp == 0 else "q2b", q2), ("tensor", w2b_rd[np_])])
                w2l_rd[kp] = cv
            w2_cvt[np_] = cv
        if e_ix == 0:
            w13_fetch(0)
            w13_fetch(1)
        for jm in range(22):
            mp = jm % 2
            is_g = (jm % 2 == 0)
            j = jm // 2
            pe_tb = [0, 0]
            for k in range(16):
                for tb in range(2):
                    pe_tb[tb] = b.op("tensor", lambda e, k=k, tb=tb, bk=(0 if is_g else 2) + tb, mp=mp: e.matmul(
                        bank(bk), T["w13b"][mp][:, 128 * k:128 * (k + 1)],
                        T["x1nT"][:, 1024 * k + 512 * tb:1024 * k + 512 * (tb + 1)],
                        start=(k == 0), stop=(k == 15)),
                        waits=(([("vector", w13_cvt[mp])] +
                                ([("scalar", gs_wr)] if is_g else [("vector", h_wr)]) +
                                ([("vector", acc_ev)] if jm == 0 else []))
                               if (k == 0 and tb == 0) else []), inc=(k == 15))
            pe_m = pe_tb[1]
            w13b_rd[mp] = pe_m
            if jm < 20:
                w13_fetch(jm + 2)
            elif jm == 20:
                w2_fetch(0)
            else:
                w2_fetch(1)
            if is_g:
                for tb in range(2):
                    gs_wr = b.op("scalar", lambda e, tb=tb: e.activation(
                        out=T["gs"][:, 512 * tb:512 * (tb + 1)], in_=bank(tb), func=AF.Silu),
                        waits=[("tensor", pe_tb[tb]), ("vector", gs_rd_v)] if tb == 0 else [("tensor", pe_tb[tb])])
            else:
                for tb in range(2):
                    h_wr = b.op("vector", lambda e, tb=tb, j=j: e.tensor_mul(
                        out=T["hT"][:, 1024 * j + 512 * tb:1024 * j + 512 * (tb + 1)],
                        in0=T["gs"][:, 512 * tb:512 * (tb + 1)], in1=bank(2 + tb)),
                        waits=[("scalar", gs_wr), ("tensor", max(pe_tb[tb], hT_rd_pe))])
                gs_rd_v = h_wr
        if e_ix + 1 < NE:
            w13_fetch(0, ee=e_ix + 1)
            w13_fetch(1, ee=e_ix + 1)
        for n in range(4):
            np_ = n % 2
            for th in range(2):
                bs = 4 * ((2 * n + th) % 2)
                pe_d = 0
                for k in range(11):
                    for t4 in range(4):
                        tt = 4 * th + t4
                        pe_d = b.op("tensor", lambda e, k=k, tt=tt, t4=t4, np_=np_, bs=bs: e.matmul(
                            bank(bs + t4), T["hT"][:, 1024 * k + 128 * tt:1024 * k + 128 * (tt + 1)],
                            T["w2b"][np_][:, 512 * k:512 * (k + 1)], start=(k == 0), stop=(k == 10)),
                            waits=([("vector", max(w2_cvt[np_], h_wr, acc_guard[(2 * n + th) % 2]))]
                                   if (k == 0 and t4 == 0) else []),
                            inc=(k == 10 and t4 == 3))
                if n < 3 and th == 0:
                    w2_fetch(n + 1)
                for t4 in range(4):
                    tt = 4 * th + t4
                    if e_ix < 4:
                        coef_ap = T["sig"][:, tt:tt + 1]
                    else:
                        coef_ap = T["coef"][:, 16 * tt + (e_ix - 4):16 * tt + (e_ix - 4) + 1]
                    sl = slice(2048 * tt + 512 * n, 2048 * tt + 512 * (n + 1))
                    acc_ev = b.op("vector", lambda e, t4=t4, sl=sl, coef_ap=coef_ap, bs=bs: e.scalar_tensor_tensor(
                        out=acc[:, sl], in0=bank(bs + t4), scalar=coef_ap, in1=acc[:, sl],
                        op0=OP.mult, op1=OP.add),
                        waits=[("tensor", pe_d), ("vector", max(acc_ev, coef_done)),
                               ("scalar", sig_done)])
                acc_guard[(2 * n + th) % 2] = acc_ev
            w2b_rd[np_] = pe_d
        hT_rd_pe = b.ct["tensor"]
    if phases in (4.5, 4.6, 4.7, 4.8, 4.85, 4.9):
        dump_and_stop()
        return

    qo = 0
    for tt in range(8):
        qo = b.dma("gpsimd", "qo", C["out_ap"][128 * tt:128 * (tt + 1), :], acc[:, 2048 * tt:2048 * (tt + 1)],
                   waits=[("vector", acc_ev)])
    fE = b.fence()
    for s_ in QSEMS:
        b.wait("gpsimd", s_, fE[s_])



def build_program(phases=6, dumps=()):
    nc = bass.Bass("TRN2", target_bir_lowering=False)
    CIN = {}
    CIN["xrows"] = nc.dram_tensor("xrows", [S, D], F32, kind="ExternalInput").ap()
    CIN["ktab"] = nc.dram_tensor("ktab", [128, S], F32, kind="ExternalInput").ap()
    CIN["qtab"] = nc.dram_tensor("qtab", [128, OWN], F32, kind="ExternalInput").ap()
    CIN["biascol"] = nc.dram_tensor("biascol", [128, 49], F32, kind="ExternalInput").ap()
    CIN["wkv"] = nc.dram_tensor("wkv", [16, 128, 1024], F32, kind="ExternalInput").ap()
    CIN["wq"] = nc.dram_tensor("wq", [16, 128, 2048], F32, kind="ExternalInput").ap()
    CIN["wout"] = nc.dram_tensor("wout", [4, 16, 128, 512], F32, kind="ExternalInput").ap()
    CIN["rms16"] = nc.dram_tensor("rms16", [128, 16], F32, kind="ExternalInput").ap()
    CIN["wgate"] = nc.dram_tensor("wgate", [128, 272], F32, kind="ExternalInput").ap()
    CIN["w13"] = nc.dram_tensor("w13", [NE, 22, 128, 2048], F32, kind="ExternalInput").ap()
    CIN["w2"] = nc.dram_tensor("w2", [NE, 4, 128, 5632], F32, kind="ExternalInput").ap()
    out_ap = nc.dram_tensor("out", [OWN, D], F32, kind="ExternalOutput").ap()
    ident_dram = nc.inline_tensor(np.eye(128, dtype=np.float32), "identf32")

    SBASE = 16512
    def at(name, shape, dtype, off):
        return nc.alloc_sbuf_tensor_at(name, shape, dtype, offset=SBASE + off)

    T = {}
    EB, FB, MB = 98816, 164352, 197120
    T["kst"] = at("kst", [128, 16384], BF16, 0).ap()
    T["vst"] = at("vst", [128, 16640], BF16, 32768)
    T["qot"] = at("qot", [128, 16384], BF16, 66048).ap()
    T["acc"] = at("acc", [128, 16384], F32, EB)
    x1nT_h = at("x1nT", [128, 16384], BF16, FB)
    T["x1nT_h"] = x1nT_h
    T["x1nT"] = x1nT_h.ap()
    z_own_h = [at("z_own0", [128, 8192], BF16, EB), at("z_own1", [128, 8192], BF16, EB + 16384)]
    T["z_own_h"] = z_own_h
    T["z_own"] = [z_own_h[0].ap(), z_own_h[1].ap()]
    T["rtu"] = at("rtu", [64, 512], F32, EB + 32768)
    T["rtv"] = at("rtv", [64, 512], F32, EB + 34816)
    T["rtw"] = at("rtw", [64, 512], F32, EB + 36864)
    T["rtz"] = at("rtz", [64, 512], F32, EB + 38912)
    T["xbuf"] = [at("xbuf0", [128, 2048], F32, EB + 40960), at("xbuf1", [128, 2048], F32, EB + 49152)]
    T["wqb"] = [at("wqb0", [128, 2048], BF16, EB + 57344), at("wqb1", [128, 2048], BF16, EB + 61440)]
    zT_h = at("zT", [128, 8192], BF16, FB)
    T["zT_h"] = zT_h
    T["zT"] = zT_h.ap()
    T["zbuf"] = at("zbuf", [128, 2048], F32, FB + 16384)
    T["kvland"] = at("kvland", [128, 1024], F32, FB + 24576)
    T["tabbuf"] = [at("tabbuf0", [128, 512], F32, FB + 28672), at("tabbuf1", [128, 512], F32, FB + 30720)]
    T["wkvb"] = at("wkvb", [128, 16384], BF16, 66048).ap()
    T["qtil"] = at("qtil", [65, 4096], BF16, EB).ap()
    T["ktil"] = at("ktil", [65, 4096], BF16, EB + 8192).ap()
    T["sqs"] = at("sqs", [64, 4096], BF16, EB + 16384)
    T["r1"] = at("r1", [1, 4096], F32, EB + 24576)
    T["pbuf"] = [at("pbuf0", [128, 2048], BF16, EB + 40960), at("pbuf1", [128, 2048], BF16, EB + 45056)]
    T["osb"] = at("osb", [65, 2048], F32, EB + 16384).ap()
    T["rrow"] = at("rrow", [1, 2048], BF16, EB + 49152)
    T["rrow32"] = at("rrow32", [1, 2048], F32, EB + 53248)
    T["xq"] = [at("xq0", [128, 2048], F32, 0), at("xq1", [128, 2048], F32, 8192)]
    T["wob"] = [at("wob0", [128, 8192], BF16, 16384), at("wob1", [128, 8192], BF16, 0)]
    T["woland"] = [at("woland0", [128, 512], F32, 32768), at("woland1", [128, 512], F32, 34816)]
    T["ybuf"] = at("ybuf", [128, 2048], F32, 36864)
    T["zbuf2"] = at("zbuf2", [128, 2048], F32, 45056)
    T["w13land"] = [at("w13land0", [128, 1024], F32, 0), at("w13land1", [128, 1024], F32, 4096)]
    T["w13b"] = [at("w13b0", [128, 2048], BF16, 8192), at("w13b1", [128, 2048], BF16, 12288)]
    T["w2land"] = [at("w2land0", [128, 1408], F32, 16384), at("w2land1", [128, 1408], F32, 22016)]
    T["w2b"] = [at("w2b0", [128, 5632], BF16, 27648), at("w2b1", [128, 5632], BF16, 38912)]
    T["hT"] = at("hT", [128, 11264], BF16, 50176).ap()
    T["gs"] = at("gs", [128, 1024], F32, 72704)
    off = [MB]
    def misc(name, shape, dtype):
        t = at(name, shape, dtype, off[0])
        nbytes = int(np.prod(shape[1:])) * mybir.dt.size(dtype)
        off[0] += (nbytes + 31) // 32 * 32
        return t
    T["ident"] = misc("ident", [128, 128], F32)
    T["biascol"] = misc("biascolS", [128, 49], F32)
    T["rms16land"] = misc("rms16land", [128, 16], F32)
    T["rms16"] = misc("rms16S", [128, 16], F32)
    T["stats"] = misc("stats", [128, 16], F32)
    T["stat2"] = misc("stat2", [128, 2], F32)
    T["bn24"] = misc("bn24", [128, 24], F32)
    T["t1"] = misc("t1", [128, 1], F32)
    T["scl"] = misc("scl", [128, 1], F32)
    T["bia"] = misc("bia", [128, 1], F32)
    T["t2"] = misc("t2", [128, 1], F32)
    T["t3"] = misc("t3", [128, 1], F32)
    T["km2"] = [misc("km2a", [1, 1], F32), misc("km2b", [1, 1], F32)]
    T["km"] = misc("km", [1, 1], F32)
    T["kms"] = misc("kms", [1, 1], F32)
    T["qtab"] = misc("qtabS", [128, 1024], F32)
    T["qland"] = misc("qland", [128, 2048], F32)
    T["wgateb"] = misc("wgateb", [128, 272], BF16)
    T["probs"] = misc("probs", [128, 16], F32)
    T["top8"] = misc("top8", [128, 8], F32)
    T["coef"] = misc("coef", [128, 128], F32)
    T["sig"] = misc("sig", [128, 8], F32)
    T["ones164"] = misc("ones164", [1, 64], BF16)
    T["ones64c"] = misc("ones64c", [64, 1], BF16)
    T["cst"] = misc("cst", [128, 2], F32)
    assert off[0] <= 212700, f"misc overflow: {off[0]}"
    vst_h = T["vst"]
    T["vst"] = vst_h.ap()
    T["vst_ones"] = bass.AP(vst_h, 64, [[16640, 128], [520, 32], [65, 8], [1, 1]])
    T["acc"] = T["acc"]
    T["pb"] = nc.alloc_psum_tensor("pb", [128, 4096], F32).ap()

    C = {"in": CIN, "out_ap": out_ap, "ident_dram": ident_dram}
    dump_specs = {
        "kst": (T["kst"][:, :], [128, 16384], BF16),
        "vst": (T["vst"][:, :], [128, 16640], BF16),
        "qot": (T["qot"][:, :], [128, 16384], BF16),
        "zT0": (T["z_own"][0][:, :], [128, 8192], BF16),
        "zT1": (T["z_own"][1][:, :], [128, 8192], BF16),
        "wkvb": (T["wkvb"][:, :], [128, 16384], BF16),
        "acc": (T["acc"][:], [128, 16384], F32),
        "x1nT": (T["x1nT"][:, :], [128, 16384], BF16),
        "coef": (T["coef"][:], [128, 128], F32),
        "sig": (T["sig"][:], [128, 8], F32),
        "stats": (T["stats"][:], [128, 16], F32),
        "hT": (T["hT"][:, :], [128, 11264], BF16),
    }
    C["dumps"] = []
    C["dbg_aps"] = {}
    for name in dumps:
        ap_, shp, dt = dump_specs[name]
        C["dbg_aps"][name] = nc.dram_tensor("dbg_" + name, shp, dt, kind="ExternalOutput").ap()
        C["dumps"].append((name, ap_))

    import contextlib
    with contextlib.ExitStack() as st:
        sems = {k: st.enter_context(nc.semaphore("sem_" + k)) for k in ALLSEMS}
        block = st.enter_context(nc.Block())

        @block.tensor
        def _(eng):
            build_walk(T, C, Bld("tensor", eng, sems), phases)

        @block.scalar
        def _(eng):
            build_walk(T, C, Bld("scalar", eng, sems), phases)

        @block.vector
        def _(eng):
            build_walk(T, C, Bld("vector", eng, sems), phases)

        @block.gpsimd
        def _(eng):
            build_walk(T, C, Bld("gpsimd", eng, sems), phases)

        @block.sync
        def _(eng):
            build_walk(T, C, Bld("sync", eng, sems), phases)

    return nc


_PROG_CACHE = {}

def kernel(**inputs):
    shared = _shared_layouts(inputs)
    if "nc" not in _PROG_CACHE:
        _PROG_CACHE["nc"] = build_program()
    nc = _PROG_CACHE["nc"]
    in_maps = []
    for core in range(NCORES):
        m = dict(shared)
        m.update(_per_core(inputs, core))
        in_maps.append(m)
    res = run_bass_kernel_spmd(nc, in_maps, core_ids=list(range(NCORES)))
    out = np.zeros((BATCH, S, D), np.float32)
    for core in range(NCORES):
        bb, i = core // 4, core % 4
        o = res.results[core]["out"]
        out[bb, i * 512:(i + 1) * 512] = o[0:512]
        out[bb, (7 - i) * 512:(8 - i) * 512] = o[512:1024]
    return out
